# version 08
# baseline (speedup 1.0000x reference)
import contextlib
import numpy as np
import concourse.bass as bass
import concourse.mybir as mybir
from concourse.bass_utils import run_bass_kernel_spmd
from concourse.aot_env import allow_ici_all_to_all

F32 = mybir.dt.float32
BF16 = mybir.dt.bfloat16
AF = mybir.ActivationFunctionType
ALU = mybir.AluOpType
AX = mybir.AxisListType

NCORES = 8
B, T, C = 16, 2048, 2048
H, DH = 16, 128
FC = C // 128
SCALE = float(DH) ** -0.5
NEG = -1.0e30
FH = 5632
HC = FH // 128
NB = 256

FUSED = True

ENGS = ("tensor", "vector", "scalar", "gpsimd", "sync")
DMA_POOL = 8


class _Op:
    __slots__ = ("eng", "fn", "reads", "writes", "dma", "idx", "need_inc", "token",
                 "deps", "sem", "selfwait", "barrier", "cc")


class Plan:

    def __init__(self, nc):
        self.nc = nc
        self.ops = []
        self.last_w = {}
        self.readers = {}
        self.dma_idx = {e: 0 for e in ENGS}
        self.cc_idx = 0

    def op(self, eng, fn, r=(), w=(), dma=False, cc=False):
        o = _Op()
        o.eng, o.fn, o.reads, o.writes, o.dma = eng, fn, list(r), list(w), dma
        o.cc = cc
        o.need_inc = False
        o.token = None
        o.sem = None
        o.selfwait = None
        o.barrier = False
        o.idx = len(self.ops)
        if cc:
            o.sem = ("cc", self.cc_idx)
            self.cc_idx += 1
        elif dma:
            o.sem = (eng, self.dma_idx[eng] % DMA_POOL)
            self.dma_idx[eng] += 1
        deps = {}

        def add_dep(p):
            if p is None or p is o:
                return
            if p.eng == o.eng and not p.dma and not o.dma and p.eng == "tensor":
                return
            deps[p.idx] = p

        for k in o.reads:
            add_dep(self.last_w.get(k))
        for k in o.writes:
            add_dep(self.last_w.get(k))
            for rd in self.readers.get(k, ()):
                add_dep(rd)
        best = {}
        for p in deps.values():
            sk = p.sem if (p.dma or p.cc) else p.eng
            q = best.get(sk)
            if q is None or p.idx > q.idx:
                best[sk] = p
        o.deps = list(best.values())
        for p in o.deps:
            p.need_inc = True
        for k in o.writes:
            self.last_w[k] = o
            self.readers[k] = []
        for k in o.reads:
            self.readers.setdefault(k, []).append(o)
        self.ops.append(o)
        return o

    def barrier(self):
        last_tensor = None
        for o in reversed(self.ops):
            if o.eng == "tensor" and not o.barrier:
                last_tensor = o
                break
        if last_tensor is not None:
            last_tensor.need_inc = True
        for e in ENGS:
            o = _Op()
            o.eng, o.fn, o.reads, o.writes, o.dma = e, (lambda en: en.nop()), [], [], False
            if e == "tensor":
                o.fn = None
            o.cc = False
            o.need_inc = e != "tensor"
            o.token = None
            o.sem = None
            o.selfwait = None
            o.barrier = True
            o.idx = len(self.ops)
            o.deps = []
            self.ops.append(o)
        self.last_w = {}
        self.readers = {}

    def build(self, es):
        nc = self.nc
        sems = {e: es.enter_context(nc.semaphore(f"sem_{e}")) for e in ENGS}
        dsems = {}
        for e in ENGS:
            n = min(self.dma_idx[e], DMA_POOL)
            for i in range(n):
                dsems[(e, i)] = es.enter_context(nc.semaphore(f"dsem_{e}{i}"))
        for i in range(self.cc_idx):
            dsems[("cc", i)] = es.enter_context(nc.semaphore(f"ccsem{i}"))
        counts = {e: 0 for e in ENGS}
        dcounts = {k: 0 for k in dsems}
        barrier_waits = {}
        pending_barrier = []
        for o in self.ops:
            if o.barrier:
                pending_barrier.append(o)
                if o.need_inc:
                    counts[o.eng] += 1
                o.token = counts[o.eng]
                if (o.idx + 1 >= len(self.ops) or not getattr(self.ops[o.idx + 1], "barrier", False)):
                    import os
                    snap = {}
                    for e2 in ENGS:
                        if counts[e2]:
                            snap[e2] = counts[e2]
                    if os.environ.get("BARRIER_NODMA", "0") != "1":
                        for k2, v2 in dcounts.items():
                            if v2:
                                snap[k2] = v2
                    for bo in pending_barrier:
                        barrier_waits[bo.idx] = snap
                    pending_barrier = []
                continue
            if o.cc:
                o.need_inc = True
                dcounts[o.sem] = 1
                o.token = 1
            elif o.dma:
                o.need_inc = True
                o.selfwait = dcounts[o.sem]
                dcounts[o.sem] += 16
                o.token = dcounts[o.sem]
            elif o.need_inc:
                counts[o.eng] += 1
                o.token = counts[o.eng]
        totals = dict(counts)
        dtotals = dict(dcounts)

        def semref_key(k):
            return dsems[k] if isinstance(k, tuple) else sems[k]

        def emit_for(eng_name, e):
            fl = {}
            for o in self.ops:
                if o.eng != eng_name:
                    continue
                if o.barrier:
                    snap = barrier_waits[o.idx]
                    if o.fn is not None:
                        ins = o.fn(e)
                        ins.then_inc(sems[o.eng], 1)
                    for k, v in snap.items():
                        if k == eng_name:
                            continue
                        if v > fl.get(k, 0):
                            e.wait_ge(semref_key(k), v)
                            fl[k] = v
                    continue
                if o.dma and o.selfwait > fl.get(o.sem, 0):
                    e.wait_ge(dsems[o.sem], o.selfwait)
                    fl[o.sem] = o.selfwait
                for p in o.deps:
                    k = p.sem if (p.dma or p.cc) else p.eng
                    if p.token > fl.get(k, 0):
                        e.wait_ge(semref_key(k), p.token)
                        fl[k] = p.token
                ins = o.fn(e)
                if o.need_inc:
                    if o.cc:
                        ins.then_inc(dsems[o.sem], 1)
                    else:
                        ins.then_inc(dsems[o.sem] if o.dma else sems[o.eng], 16 if o.dma else 1)
            if eng_name == "sync":
                for e2 in ENGS:
                    if totals[e2] > fl.get(e2, 0):
                        e.wait_ge(sems[e2], totals[e2])
                for k, v in dtotals.items():
                    if v > fl.get(k, 0):
                        e.wait_ge(dsems[k], v)

        with nc.Block() as block:
            for e in ENGS:
                getattr(block, e)((lambda en: (lambda eng_obj: emit_for(en, eng_obj)))(e))


def keys2k(name, c0, c1):
    return [f"{name}#{i}" for i in range(c0 // 2048, (c1 + 2047) // 2048)]


class NS:
    pass


def alloc_all(nc, es):
    t = NS()
    t.ps = [es.enter_context(nc.psum_tensor(f"ps{i}", [128, 512], F32)) for i in range(7)]
    t.psl = es.enter_context(nc.psum_tensor("psl", [1, 512], F32))
    t.ps_i = 0

    def rot():
        t.ps_i = (t.ps_i + 1) % 7
        return t.ps[t.ps_i], f"PS.{t.ps_i}"
    t.rot = rot
    sb = lambda n, sh, dt: es.enter_context(nc.sbuf_tensor(n, sh, dt))
    t.G1t = sb("G1", [128, 32768], BF16)
    t.G2t = sb("G2", [128, 32768], BF16)
    t.G3t = sb("G3", [128, 16384], BF16)
    t.XT = [sb(f"XT{i}", [128, C], F32) for i in range(2)]
    t.XNT = sb("XNT", [128, C], BF16)
    t.SCR = sb("SCR", [128, 512], F32)
    t.ones_r = sb("ones_r", [1, 512], BF16)
    t.ones_c = sb("ones_c", [128, 1], BF16)
    t.ones_f = sb("ones_f", [128, 1], F32)
    t.ones_fr = sb("ones_fr", [1, 128], F32)
    t.negm = [sb(f"negm{i}", [128, 1], F32) for i in range(2)]
    t.accq = [sb(f"accq{i}", [128, 1], F32) for i in range(2)]
    t.tmp1 = [sb(f"tmp1_{i}", [128, 1], F32) for i in range(2)]
    t.rstd = [sb(f"rstd{i}", [128, 1], F32) for i in range(2)]
    t.BQ = sb("BQ", [1, 2048], BF16)
    t.BK = sb("BK", [1, 2048], BF16)
    t.BV = sb("BV", [1, 2048], BF16)
    t.BO = sb("BO", [1, 2048], BF16)
    t.BF_ = sb("BF_", [1, 2048], BF16)
    
    t.MASKT = sb("MASKT", [128, 128], BF16)
    t.GRW = sb("GRW", [128, 1], F32)
    t.INV = sb("INV", [128, 1], F32)
    t.T11 = sb("T11", [1, 1], F32)

    t.G1 = t.G1t.ap()
    t.G2 = t.G2t.ap()
    t.G3 = t.G3t.ap()
    g2, g3 = t.G2, t.G3
    g3f = t.G3t.ap().bitcast(F32)
    t.XNFM = t.G1
    t.PTOFF = []
    off = 0
    for c in range(16):
        t.PTOFF.append(off)
        off += T - c * 128
    t.PT = g2[:, 0:17408]
    t.LNL = g2[0:1, 30720:32768]
    t.LNL_K = keys2k("G2", 30720, 32768)
    t.WQH = [g2[:, 18432:20480], g2[:, 20480:22528]]
    t.WKH = [g2[:, 22528:24576], g2[:, 24576:26624]]
    t.QFM = g2[:, 26624:28672]
    t.KFM = g2[:, 28672:30720]
    t.WQH_K = [keys2k("G2", 18432, 20480), keys2k("G2", 20480, 22528)]
    t.WKH_K = [keys2k("G2", 22528, 24576), keys2k("G2", 24576, 26624)]
    t.QFM_K = keys2k("G2", 26624, 28672)
    t.KFM_K = keys2k("G2", 28672, 30720)
    t.VH = g3[:, 0:2048]
    t.YH = g3[:, 2048:4096]
    t.RBT = g3[:, 4096:6144]
    t.GT = [g3[:, 6144:8192], g3[:, 8192:10240]]
    t.WVC = g3[:, 10240:12288]
    t.VT = g3[:, 12288:12800]
    t.VH_K, t.YH_K, t.RBT_K = ["G3#0"], ["G3#1"], ["G3#2"]
    t.GT_K = [["G3#3"], ["G3#4"]]
    t.WVC_K, t.VT_K = ["G3#5"], ["G3#6"]
    t.SH = g3f[:, 5120:5376]
    t.PH = g3f[:, 5376:5632]
    t.PN = [g3f[:, 5632 + i * 256: 5888 + i * 256] for i in range(4)]
    t.OB = g3f[:, 6656:7168]
    t.OBB = g3[:, 14336:14848]
    t.S1T = g3[:, 14848:15360]
    t.SCRB = g3[:, 15360:15488]
    t.MX = g3f[:, 7744:7760]
    t.SM = g3f[:, 7760:7776]
    t.BS_K = keys2k("G3", 10240, 15552)
    t.SGM = g3f[:, 2816:3840]
    t.SGM_K = keys2k("G3", 5632, 7680)
    t.GFT = g3[:, 10240:11264]
    t.GFT_K = keys2k("G3", 10240, 11264)
    return t


def prologue_consts(P, t):
    P.op("gpsimd", lambda e: e.memset(t.ones_r[:, :], 1.0), w=["ones_r"])
    P.op("gpsimd", lambda e: e.memset(t.ones_c[:, :], 1.0), w=["ones_c"])
    P.op("gpsimd", lambda e: e.memset(t.ones_f[:, :], 1.0), w=["ones_f"])
    P.op("gpsimd", lambda e: e.memset(t.ones_fr[:, :], 1.0), w=["ones_fr"])


def ln_tile(P, t, xt, xt_key, xnt, xnt_key, pp):
    negm, accq, tmp1, rstd = t.negm[pp], t.accq[pp], t.tmp1[pp], t.rstd[pp]
    kn, ka, kt, kr = f"negm{pp}", f"accq{pp}", f"tmp1{pp}", f"rstd{pp}"
    grp, gk = t.rot()
    P.op("vector", lambda e: e.reduce_sum(negm[:, :], xt[:, :], axis=AX.X, negate=True),
         r=[xt_key], w=[kn])
    P.op("vector", lambda e: e.tensor_scalar(negm[:, :], negm[:, :], 1.0 / C, None, op0=ALU.mult),
         r=[kn], w=[kn])
    P.op("scalar", lambda e: e.activation(grp[:, :], xt[:, 0:512], AF.Square, accum_out=accq[:, :]),
         r=[xt_key], w=[gk, ka])
    for j in range(1, 4):
        P.op("scalar", lambda e, j=j: e.activation(grp[:, :], xt[:, j * 512:(j + 1) * 512],
                                                   AF.Square, accum_out=tmp1[:, :]),
             r=[xt_key], w=[gk, kt])
        P.op("vector", lambda e: e.tensor_tensor(accq[:, :], accq[:, :], tmp1[:, :], op=ALU.add),
             r=[ka, kt], w=[ka])
    P.op("vector", lambda e: e.tensor_tensor(tmp1[:, :], negm[:, :], negm[:, :], op=ALU.mult),
         r=[kn], w=[kt])
    P.op("vector", lambda e: e.tensor_scalar(tmp1[:, :], tmp1[:, :], -1.0, 1e-5,
                                             op0=ALU.mult, op1=ALU.add),
         r=[kt], w=[kt])
    P.op("scalar", lambda e: e.activation(rstd[:, :], accq[:, :], AF.Sqrt, bias=tmp1[:, :],
                                          scale=1.0 / C),
         r=[ka, kt], w=[kr])
    P.op("vector", lambda e: e.reciprocal(rstd[:, :], rstd[:, :]), r=[kr], w=[kr])
    P.op("vector", lambda e: e.tensor_scalar(xnt[:, :], xt[:, :], negm[:, :], rstd[:, :],
                                             op0=ALU.add, op1=ALU.mult),
         r=[xt_key, kn, kr], w=[xnt_key])


def build_stage_a(nc, P, t, io):
    x_h, x1dst = io["x"], io["x1dst"]
    wq_h, wk_h, wv_h = io["wq"], io["wk"], io["wv"]
    bq_h, bk_h, bv_h, bo_h, bf_h = io["bq"], io["bk"], io["bv"], io["bo"], io["bf"]
    wo_h, wf_h = io["wo"], io["wf"]
    xn_d, bw_d, g_d, v_d, y_d = io["xn_d"], io["bw_d"], io["g_d"], io["v_d"], io["y_d"]

    for (row, src, nm) in ((t.BQ, bq_h, "BQ"), (t.BK, bk_h, "BK"), (t.BV, bv_h, "BV")):
        P.op("gpsimd", lambda e, row=row, src=src: e.dma_start(
            out=row[0:1, :], in_=src.rearrange("h d -> (h d)").unsqueeze(0)),
            w=[nm], dma=True)
    P.op("gpsimd", lambda e: e.dma_start(out=t.BO[0:1, :], in_=bo_h.unsqueeze(0)), w=["BO"], dma=True)
    P.op("gpsimd", lambda e: e.dma_start(out=t.BF_[0:1, :], in_=bf_h.unsqueeze(0)), w=["BF_"], dma=True)
    P.op("gpsimd", lambda e: e.memset(t.MASKT[:, :], 0.0), w=["MASKT"])
    P.op("gpsimd", lambda e: e.affine_select(out=t.MASKT[:, :], in_=t.MASKT[:, :],
                                             compare_op=ALU.is_ge, fill=NEG, base=0,
                                             pattern=[[1, 128]], channel_multiplier=-1),
         r=["MASKT"], w=["MASKT"])

    for oc in range(16):
        pp = oc % 2
        xt, gt, gtk = t.XT[pp], t.GT[pp], t.GT_K[pp]
        P.op("sync", lambda e, oc=oc, xt=xt: e.dma_start(
            out=xt[:, :], in_=wf_h[oc * 128:(oc + 1) * 128, :]), w=[f"xt{pp}"], dma=True)
        P.op("vector", lambda e, xt=xt, pp=pp: e.reduce_sum(
            t.tmp1[pp][:, :], xt[:, :], axis=AX.X, apply_absolute_value=True),
            r=[f"xt{pp}"], w=[f"tmp1{pp}"])
        P.op("vector", lambda e, pp=pp: e.tensor_scalar(
            t.tmp1[pp][:, :], t.tmp1[pp][:, :], 1.0 / C, None, op0=ALU.mult),
            r=[f"tmp1{pp}"], w=[f"tmp1{pp}"])
        P.op("scalar", lambda e, xt=xt: e.sign(t.XNT[:, :], xt[:, :]), r=[f"xt{pp}"], w=["XNT"])
        P.op("vector", lambda e, gt=gt, pp=pp: e.tensor_scalar(
            gt[:, :], t.XNT[:, :], t.tmp1[pp][:, :], None, op0=ALU.mult),
            r=["XNT", f"tmp1{pp}"], w=gtk)
        P.op("gpsimd", lambda e, gt=gt, oc=oc: e.dma_start(
            out=bw_d[oc * 128:(oc + 1) * 128, :], in_=gt[:, :]),
            r=gtk, w=[f"bwd.{oc}"], dma=True)
    bwd_all = [f"bwd.{oc}" for oc in range(16)]

    for b in range(2):
        for i in range(16):
            pp = i % 2
            xt = t.XT[pp]
            P.op("sync", lambda e, b=b, i=i, xt=xt: e.dma_start(
                out=xt[:, :], in_=x_h[b, i * 128:(i + 1) * 128, :]), w=[f"xt{pp}"], dma=True)
            ln_tile(P, t, xt, f"xt{pp}", t.XNT, "XNT", pp)
            P.op("gpsimd", lambda e, i=i: e.dma_start(
                out=xn_d[i * 128:(i + 1) * 128, :], in_=t.XNT[:, :]),
                r=["XNT"], w=[f"xnd.{i}"], dma=True)
        xnd_all = [f"xnd.{i}" for i in range(16)]
        for fc in range(FC):
            P.op("scalar", lambda e, fc=fc: e.dma_start(
                out=t.XNFM[:, fc * 2048:(fc + 1) * 2048],
                in_=xn_d[:, fc * 128:(fc + 1) * 128], transpose=True),
                r=xnd_all, w=[f"G1#{fc}"], dma=True)

        for half in range(2):
            for ic in range(FC):
                c0 = ic * 1024
                P.op("scalar", lambda e, ic=ic, half=half, c0=c0: e.dma_start(
                    out=t.PT[:, c0:c0 + 1024],
                    in_=bw_d[half * 1024:(half + 1) * 1024, ic * 128:(ic + 1) * 128],
                    transpose=True),
                    r=bwd_all, w=keys2k("G2", c0, c0 + 1024), dma=True)
            for i in range(16):
                pp = i % 2
                gt, gtk = t.GT[pp], t.GT_K[pp]
                for cg in range(2):
                    grp, gk = t.rot()
                    ocol = half * 1024 + cg * 512
                    for ic in range(FC):
                        P.op("tensor", lambda e, grp=grp, ic=ic, i=i, cg=cg: e.matmul(
                            grp[:, :], t.XNFM[:, ic * 2048 + i * 128: ic * 2048 + (i + 1) * 128],
                            t.PT[:, ic * 1024 + cg * 512: ic * 1024 + (cg + 1) * 512],
                            start=(ic == 0), stop=False),
                            r=[f"G1#{ic}"] + keys2k("G2", ic * 1024, ic * 1024 + 1024), w=[gk])
                    P.op("tensor", lambda e, grp=grp, ocol=ocol: e.matmul(
                        grp[:, :], t.ones_r[0:1, 0:128], t.BF_[0:1, ocol:ocol + 512],
                        start=False, stop=True),
                        r=["ones_r", "BF_"], w=[gk])
                    P.op("scalar", lambda e, grp=grp, gt=gt, cg=cg: e.activation(
                        gt[:, cg * 512:(cg + 1) * 512], grp[:, :], AF.Silu),
                        r=[gk], w=gtk)
                P.op("gpsimd", lambda e, gt=gt, i=i, half=half: e.dma_start(
                    out=g_d[i * 128:(i + 1) * 128, half * 1024:(half + 1) * 1024],
                    in_=gt[:, 0:1024]),
                    r=gtk, w=[f"gd.{i}.{half}"], dma=True)

        for hgv in range(4):
            for ic in range(FC):
                c0 = ic * 512
                P.op("gpsimd", lambda e, hgv=hgv, ic=ic, c0=c0: e.dma_start(
                    out=t.PT[:, c0:c0 + 512],
                    in_=wv_h.rearrange("h i d -> i h d")[ic * 128:(ic + 1) * 128,
                                                         4 * hgv:4 * hgv + 4, :]),
                    w=keys2k("G2", c0, c0 + 512), dma=True)
            for i in range(16):
                grp, gk = t.rot()
                for ic in range(FC):
                    P.op("tensor", lambda e, grp=grp, ic=ic, i=i: e.matmul(
                        grp[:, :], t.XNFM[:, ic * 2048 + i * 128: ic * 2048 + (i + 1) * 128],
                        t.PT[:, ic * 512:(ic + 1) * 512], start=(ic == 0), stop=False),
                        r=[f"G1#{ic}"] + keys2k("G2", ic * 512, ic * 512 + 512), w=[gk])
                P.op("tensor", lambda e, grp=grp, hgv=hgv: e.matmul(
                    grp[:, :], t.ones_r[0:1, 0:128], t.BV[0:1, hgv * 512:(hgv + 1) * 512],
                    start=False, stop=True),
                    r=["ones_r", "BV"], w=[gk])
                P.op("scalar", lambda e, grp=grp: e.activation(t.VT[:, :], grp[:, :], AF.Copy),
                     r=[gk], w=t.VT_K)
                P.op("gpsimd", lambda e, i=i, hgv=hgv: e.dma_start(
                    out=v_d.rearrange("h t d -> t h d")[i * 128:(i + 1) * 128,
                                                        4 * hgv:4 * hgv + 4, :],
                    in_=t.VT[:, :]),
                    r=t.VT_K, w=[f"vd.{hgv}.{i}"], dma=True)

        for h in range(H):
            pp = h % 2
            wq, wk = t.WQH[pp], t.WKH[pp]
            wqk, wkk = t.WQH_K[pp], t.WKH_K[pp]
            P.op("gpsimd", lambda e, h=h, wq=wq: e.dma_start(
                out=wq[:, :], in_=wq_h[h].rearrange("(fc f) d -> f fc d", f=128)),
                w=wqk, dma=True)
            P.op("gpsimd", lambda e, h=h, wk=wk: e.dma_start(
                out=wk[:, :], in_=wk_h[h].rearrange("(fc f) d -> f fc d", f=128)),
                w=wkk, dma=True)
            P.op("sync", lambda e, h=h: e.dma_start(
                out=t.VH[:, :].rearrange("t (c d) -> t c d", d=128),
                in_=v_d[h].rearrange("(c t) d -> t c d", t=128)),
                r=[f"vd.{h // 4}.{i}" for i in range(16)], w=t.VH_K, dma=True)
            for dst, dkey, wgt, wkey, brow, bkey, scl in (
                    (t.QFM, t.QFM_K, wq, wqk, t.BQ, "BQ", SCALE),
                    (t.KFM, t.KFM_K, wk, wkk, t.BK, "BK", None)):
                for tg in range(4):
                    grp, gk = t.rot()
                    for fc in range(FC):
                        P.op("tensor", lambda e, grp=grp, wgt=wgt, fc=fc, tg=tg: e.matmul(
                            grp[:, :], wgt[:, fc * 128:(fc + 1) * 128],
                            t.XNFM[:, fc * 2048 + tg * 512: fc * 2048 + (tg + 1) * 512],
                            start=(fc == 0), stop=False),
                            r=wkey + [f"G1#{fc}"], w=[gk])
                    P.op("tensor", lambda e, grp=grp, brow=brow, h=h: e.matmul(
                        grp[:, :], brow[0:1, h * 128:(h + 1) * 128], t.ones_r[0:1, 0:512],
                        start=False, stop=True),
                        r=[bkey, "ones_r"], w=[gk])
                    if scl is not None:
                        P.op("scalar", lambda e, grp=grp, dst=dst, tg=tg, scl=scl: e.activation(
                            dst[:, tg * 512:(tg + 1) * 512], grp[:, :], AF.Copy, scale=scl),
                            r=[gk], w=dkey)
                    else:
                        P.op("scalar", lambda e, grp=grp, dst=dst, tg=tg: e.activation(
                            dst[:, tg * 512:(tg + 1) * 512], grp[:, :], AF.Copy),
                            r=[gk], w=dkey)
            for c in range(16):
                Lc = T - c * 128
                for g in range((Lc + 511) // 512):
                    N = min(512, Lc - g * 512)
                    grp, gk = t.rot()
                    P.op("tensor", lambda e, grp=grp, c=c, g=g, N=N: e.matmul(
                        grp[:, 0:N], t.KFM[:, c * 128:(c + 1) * 128],
                        t.QFM[:, c * 128 + g * 512: c * 128 + g * 512 + N],
                        start=True, stop=True),
                        r=t.KFM_K + t.QFM_K, w=[gk])
                    if g == 0:
                        P.op("vector", lambda e, grp=grp: e.tensor_tensor(
                            grp[:, 0:128], grp[:, 0:128], t.MASKT[:, :], op=ALU.add),
                            r=[gk, "MASKT"], w=[gk])
                    p0 = t.PTOFF[c] + g * 512
                    P.op("scalar", lambda e, grp=grp, p0=p0, N=N: e.activation(
                        t.PT[:, p0:p0 + N], grp[:, 0:N], AF.Exp),
                        r=[gk], w=keys2k("G2", p0, p0 + N))
            for tg in range(4):
                t0 = tg * 512
                cmax = min(15, (t0 + 511) // 128)
                for c in range(cmax + 1):
                    lo = max(t0, c * 128)
                    N = t0 + 512 - lo
                    src = t.PTOFF[c] + lo - c * 128
                    P.op("tensor", lambda e, lo=lo, t0=t0, N=N, src=src, c=c, cmax=cmax: e.matmul(
                        t.psl[0:1, lo - t0:lo - t0 + N], t.ones_c[:, :], t.PT[:, src:src + N],
                        start=(c == 0), stop=(c == cmax)),
                        r=["ones_c"] + keys2k("G2", src, src + N), w=["PSL"])
                P.op("scalar", lambda e, tg=tg: e.activation(
                    t.LNL[0:1, tg * 512:(tg + 1) * 512], t.psl[0:1, :], AF.Ln),
                    r=["PSL"], w=t.LNL_K)
            for c in range(16):
                for tg in range((c * 128) // 512, 4):
                    t0 = tg * 512
                    lo = max(t0, c * 128)
                    N = t0 + 512 - lo
                    src = t.PTOFF[c] + lo - c * 128
                    cmax = tg * 4 + 3
                    P.op("tensor", lambda e, c=c, tg=tg, lo=lo, t0=t0, N=N, src=src, cmax=cmax:
                         e.matmul(t.ps[tg][:, lo - t0:lo - t0 + N], t.VH[:, c * 128:(c + 1) * 128],
                                  t.PT[:, src:src + N], start=(c == 0), stop=(c == cmax)),
                         r=t.VH_K + keys2k("G2", src, src + N), w=[f"PS.{tg}"])
            for tg in range(4):
                gi = 4 + (tg % 3)
                grp = t.ps[gi]
                gk = f"PS.{gi}"
                P.op("tensor", lambda e, grp=grp, tg=tg: e.matmul(
                    grp[:, :], t.ones_r[0:1, 0:128], t.LNL[0:1, tg * 512:(tg + 1) * 512],
                    start=True, stop=True),
                    r=["ones_r"] + t.LNL_K, w=[gk])
                P.op("scalar", lambda e, grp=grp, tg=tg: e.activation(
                    t.RBT[:, tg * 512:(tg + 1) * 512], grp[:, :], AF.Exp, scale=-1.0),
                    r=[gk], w=t.RBT_K)
                P.op("vector", lambda e, tg=tg: e.tensor_tensor(
                    t.YH[:, tg * 512:(tg + 1) * 512], t.ps[tg][:, :],
                    t.RBT[:, tg * 512:(tg + 1) * 512], op=ALU.mult),
                    r=[f"PS.{tg}"] + t.RBT_K, w=t.YH_K)
            P.op("gpsimd", lambda e, h=h: e.dma_start(out=y_d[h], in_=t.YH[:, :]),
                 r=t.YH_K, w=[f"yd.{h}"], dma=True)

        for fc in range(FC):
            P.op("gpsimd", lambda e, fc=fc: e.dma_start(
                out=t.XNFM[:, fc * 2048:(fc + 1) * 2048], in_=wo_h[fc * 128:(fc + 1) * 128, :]),
                w=[f"G1#{fc}"], dma=True)
        yd_all = [f"yd.{h}" for h in range(H)]
        for i in range(16):
            pp = i % 2
            xt = t.XT[pp]
            gt, gtk = t.GT[pp], t.GT_K[pp]
            yt = t.XNT if pp == 0 else t.WVC
            ytk = ["XNT"] if pp == 0 else t.WVC_K
            P.op("sync", lambda e, b=b, i=i, xt=xt: e.dma_start(
                out=xt[:, :], in_=x_h[b, i * 128:(i + 1) * 128, :]), w=[f"xt{pp}"], dma=True)
            P.op("sync", lambda e, i=i, yt=yt: e.dma_start(
                out=yt[:, :].rearrange("d (h t) -> d h t", h=H),
                in_=y_d.rearrange("h d t -> d h t")[:, :, i * 128:(i + 1) * 128]),
                r=yd_all, w=ytk, dma=True)
            P.op("gpsimd", lambda e, i=i, gt=gt: e.dma_start(
                out=gt[:, :], in_=g_d[i * 128:(i + 1) * 128, :]),
                r=[f"gd.{i}.0", f"gd.{i}.1"], w=gtk, dma=True)
            for cg in range(4):
                grp, gk = t.rot()
                for fc in range(FC):
                    P.op("tensor", lambda e, grp=grp, yt=yt, fc=fc, cg=cg: e.matmul(
                        grp[:, :], yt[:, fc * 128:(fc + 1) * 128],
                        t.XNFM[:, fc * 2048 + cg * 512: fc * 2048 + (cg + 1) * 512],
                        start=(fc == 0), stop=False),
                        r=ytk + [f"G1#{fc}"], w=[gk])
                P.op("tensor", lambda e, grp=grp, cg=cg: e.matmul(
                    grp[:, :], t.ones_r[0:1, 0:128], t.BO[0:1, cg * 512:(cg + 1) * 512],
                    start=False, stop=True),
                    r=["ones_r", "BO"], w=[gk])
                P.op("vector", lambda e, grp=grp, gt=gt, cg=cg: e.scalar_tensor_tensor(
                    t.SCR[:, 0:512], grp[:, :], 1.0, gt[:, cg * 512:(cg + 1) * 512],
                    op0=ALU.mult, op1=ALU.mult),
                    r=[gk] + gtk, w=["SCR"])
                P.op("vector", lambda e, xt=xt, cg=cg: e.tensor_tensor(
                    xt[:, cg * 512:(cg + 1) * 512], xt[:, cg * 512:(cg + 1) * 512],
                    t.SCR[:, 0:512], op=ALU.add),
                    r=[f"xt{pp}", "SCR"], w=[f"xt{pp}"])
            dst_ap, dst_key = x1dst(b, i)
            P.op("sync", lambda e, xt=xt, dst_ap=dst_ap: e.dma_start(out=dst_ap, in_=xt[:, :]),
                 r=[f"xt{pp}"], w=[dst_key], dma=True)


def build_stage_b(nc, P, t, io):
    x1src = io["x1src"]
    wqkv_h, wout_h = io["wqkv"], io["wout"]
    w1_h, w2_h, w3_h = io["w1"], io["w2"], io["w3"]
    out_h = io["out"]
    wqkv_bf, wout_bf = io["wqkv_bf"], io["wout_bf"]
    q1_d, q2_d, q3_d = io["q1_d"], io["q2_d"], io["q3_d"]
    q1t_d, q2t_d, q3t_d = io["q1t_d"], io["q2t_d"], io["q3t_d"]
    xn_d, y1_d, s1_d, gfm_d = io["xn1_d"], io["y1_d"], io["s1_d"], io["gfm_d"]
    G1, G2, G3 = t.G1, t.G2, t.G3

    for j in range(12):
        P.op("gpsimd", lambda e, j=j: e.dma_start(
            out=wqkv_bf[j * 512:(j + 1) * 512, :], in_=wqkv_h[j * 512:(j + 1) * 512, :]),
            w=[f"wqkvbf.{j}"], dma=True)
    for j in range(4):
        P.op("gpsimd", lambda e, j=j: e.dma_start(
            out=wout_bf[j * 512:(j + 1) * 512, :], in_=wout_h[j * 512:(j + 1) * 512, :]),
            w=[f"woutbf.{j}"], dma=True)

    b0_steps = []

    def quantize(w_hbm, q_d, rows, cols, tagw):
        nchunk = rows // 128
        nparts = (cols + 2047) // 2048
        b0_steps.append(lambda: P.op("gpsimd", lambda e: e.memset(t.GRW[:, :], 0.0), w=["GRW"]))

        def gamma_step(ch, part):
            pp = ch % 2
            xt = t.XT[pp]
            c0, c1 = part * 2048, min(cols, (part + 1) * 2048)
            P.op("sync", lambda e, ch=ch, xt=xt, c0=c0, c1=c1: e.dma_start(
                out=xt[:, 0:c1 - c0], in_=w_hbm[ch * 128:(ch + 1) * 128, c0:c1]),
                w=[f"xt{pp}"], dma=True)
            P.op("vector", lambda e, xt=xt, pp=pp, n=c1 - c0: e.reduce_sum(
                t.tmp1[pp][:, :], xt[:, 0:n], axis=AX.X, apply_absolute_value=True),
                r=[f"xt{pp}"], w=[f"tmp1{pp}"])
            P.op("gpsimd", lambda e, pp=pp: e.tensor_tensor(
                t.GRW[:, :], t.GRW[:, :], t.tmp1[pp][:, :], op=ALU.add),
                r=["GRW", f"tmp1{pp}"], w=["GRW"])
        for ch in range(nchunk):
            for part in range(nparts):
                b0_steps.append(lambda ch=ch, part=part: gamma_step(ch, part))

        def finalize():
            P.op("tensor", lambda e: e.matmul(t.psl[0:1, 0:1], t.GRW[:, :], t.ones_f[:, :],
                                              start=True, stop=True),
                 r=["GRW", "ones_f"], w=["PSL"])
            P.op("vector", lambda e: e.tensor_scalar(
                t.T11[:, :], t.psl[0:1, 0:1], 1.0 / (rows * cols), 1e-8,
                op0=ALU.mult, op1=ALU.add),
                r=["PSL"], w=["T11"])
            P.op("vector", lambda e: e.reciprocal(t.T11[:, :], t.T11[:, :]), r=["T11"], w=["T11"])
            grp, gk = t.rot()
            P.op("tensor", lambda e, grp=grp: e.matmul(grp[:, 0:1], t.ones_fr[:, :], t.T11[:, :],
                                                       start=True, stop=True),
                 r=["ones_fr", "T11"], w=[gk])
            P.op("vector", lambda e, grp=grp: e.tensor_copy(t.INV[:, :], grp[:, 0:1]),
                 r=[gk], w=["INV"])
        b0_steps.append(finalize)

        def quant_step(ch, part):
            pp = ch % 2
            xt = t.XT[pp]
            c0, c1 = part * 2048, min(cols, (part + 1) * 2048)
            n = c1 - c0
            P.op("sync", lambda e, ch=ch, xt=xt, c0=c0, c1=c1: e.dma_start(
                out=xt[:, 0:c1 - c0], in_=w_hbm[ch * 128:(ch + 1) * 128, c0:c1]),
                w=[f"xt{pp}"], dma=True)
            P.op("scalar", lambda e, xt=xt, n=n: e.activation(t.XNT[:, 0:n], xt[:, 0:n], AF.Sign),
                 r=[f"xt{pp}"], w=["XNT"])
            P.op("gpsimd", lambda e, xt=xt, n=n: e.tensor_scalar(
                xt[:, 0:n], xt[:, 0:n], t.INV[:, :], None, op0=ALU.mult),
                r=[f"xt{pp}", "INV"], w=[f"xt{pp}"])
            for sg in range(0, n, 512):
                grp, gk = t.rot()
                P.op("scalar", lambda e, xt=xt, sg=sg, grp=grp: e.activation(
                    grp[:, :], xt[:, sg:sg + 512], AF.Square),
                    r=[f"xt{pp}"], w=[gk])
                P.op("vector", lambda e, sg=sg, grp=grp: e.scalar_tensor_tensor(
                    t.XNT[:, sg:sg + 512], grp[:, :], 0.25, t.XNT[:, sg:sg + 512],
                    op0=ALU.is_gt, op1=ALU.mult),
                    r=[gk, "XNT"], w=["XNT"])
            P.op("gpsimd", lambda e, ch=ch, c0=c0, c1=c1: e.dma_start(
                out=q_d[ch * 128:(ch + 1) * 128, c0:c1], in_=t.XNT[:, 0:c1 - c0]),
                r=["XNT"], w=[f"{tagw}.{ch}.{c0}"], dma=True)
        for ch in range(nchunk):
            for part in range(nparts):
                b0_steps.append(lambda ch=ch, part=part: quant_step(ch, part))

    q1keys = [f"q1.{ch}.0" for ch in range(HC)]
    q2keys = [f"q2.{ch}.0" for ch in range(HC)]
    q3keys = [f"q3.{ch}.{c0}" for ch in range(FC) for c0 in range(0, FH, 2048)]

    def emit_b0():
        quantize(w1_h, q1_d, FH, C, "q1")
        quantize(w2_h, q2_d, FH, C, "q2")
        quantize(w3_h, q3_d, C, FH, "q3")

        def ptq_step(qd, qtd, rk, tag, ec):
            P.op("scalar", lambda e, qd=qd, ec=ec: e.dma_start(
                out=G3[:, 0:FH], in_=qd[:, ec * 128:(ec + 1) * 128], transpose=True),
                r=rk, w=keys2k("G3", 0, FH), dma=True)
            P.op("gpsimd", lambda e, qtd=qtd, ec=ec: e.dma_start(
                out=qtd[ec * 128:(ec + 1) * 128, :], in_=G3[:, 0:FH]),
                r=keys2k("G3", 0, FH), w=[f"{tag}.{ec}"], dma=True)
        for (qd, qtd, rk, tag) in ((q1_d, q1t_d, q1keys, "q1t"), (q2_d, q2t_d, q2keys, "q2t")):
            for ec in range(FC):
                b0_steps.append(lambda a=qd, b2=qtd, c=rk, d=tag, ec=ec: ptq_step(a, b2, c, d, ec))

        def pt3_step(hc):
            P.op("scalar", lambda e, hc=hc: e.dma_start(
                out=G3[:, 0:C], in_=q3_d[:, hc * 128:(hc + 1) * 128], transpose=True),
                r=q3keys, w=keys2k("G3", 0, C), dma=True)
            P.op("gpsimd", lambda e, hc=hc: e.dma_start(
                out=q3t_d[hc * 128:(hc + 1) * 128, :], in_=G3[:, 0:C]),
                r=keys2k("G3", 0, C), w=[f"q3t.{hc}"], dma=True)
        for hc in range(HC):
            b0_steps.append(lambda hc=hc: pt3_step(hc))

    def b0_tick(n=2):
        for _ in range(n):
            if b0_steps:
                b0_steps.pop(0)()

    def b0_flush():
        while b0_steps:
            b0_steps.pop(0)()

    q1t_all = [f"q1t.{ec}" for ec in range(FC)]
    q2t_all = [f"q2t.{ec}" for ec in range(FC)]
    q3t_all = [f"q3t.{hc}" for hc in range(HC)]

    for ncx in range(2):
        for l in range(16):
            pp = l % 2
            xt = t.XT[pp]
            src_ap, src_keys = x1src(l, ncx)
            P.op("sync", lambda e, xt=xt, src_ap=src_ap: e.dma_start(out=xt[:, :], in_=src_ap),
                 r=src_keys, w=[f"xt{pp}"], dma=True)
            ln_tile(P, t, xt, f"xt{pp}", t.XNT, "XNT", pp)
            P.op("gpsimd", lambda e, l=l, ncx=ncx: e.dma_start(
                out=xn_d[ncx, l * 128:(l + 1) * 128, :], in_=t.XNT[:, :]),
                r=["XNT"], w=[f"xnd.{ncx}.{l}"], dma=True)
    emit_b0()

    for ncx in range(2):
        xnd_nc = [f"xnd.{ncx}.{l}" for l in range(16)]
        for ec in range(FC):
            P.op("scalar", lambda e, ec=ec, ncx=ncx: e.dma_start(
                out=G1[:, ec * 2048:(ec + 1) * 2048],
                in_=xn_d[ncx][:, ec * 128:(ec + 1) * 128], transpose=True),
                r=xnd_nc, w=[f"G1#{ec}"], dma=True)
        for hg in range(4):
            bases = [hg * 512, 2048 + hg * 512, 4096 + hg * 512]
            for part, base in enumerate(bases):
                for ec in range(FC):
                    c0 = 24576 + ec * 512
                    P.op("scalar", lambda e, base=base, ec=ec, c0=c0: e.dma_start(
                        out=G2[:, c0:c0 + 512],
                        in_=wqkv_bf[base:base + 512, ec * 128:(ec + 1) * 128], transpose=True),
                        r=[f"wqkvbf.{base // 512}"], w=keys2k("G2", c0, c0 + 512), dma=True)
                for l in range(16):
                    grp, gk = t.rot()
                    for ec in range(FC):
                        P.op("tensor", lambda e, grp=grp, ec=ec, l=l: e.matmul(
                            grp[:, :], G1[:, ec * 2048 + l * 128: ec * 2048 + (l + 1) * 128],
                            G2[:, 24576 + ec * 512: 24576 + (ec + 1) * 512],
                            start=(ec == 0), stop=(ec == FC - 1)),
                            r=[f"G1#{ec}"] + keys2k("G2", 24576 + ec * 512, 25088 + ec * 512),
                            w=[gk])
                    d0 = part * 8192 + l * 512
                    P.op("scalar", lambda e, grp=grp, d0=d0: e.activation(
                        G2[:, d0:d0 + 512], grp[:, :], AF.Copy),
                        r=[gk], w=keys2k("G2", d0, d0 + 512))
                    b0_tick(1)
            qkkeys = keys2k("G2", 0, 16384)
            vkeys = keys2k("G2", 16384, 24576)
            for hl in range(4):
                P.op("gpsimd", lambda e: e.memset(t.SH[:, :], NEG), w=t.BS_K)
                for l in range(16):
                    qc = l * 512 + hl * 128
                    for m in range(l + 1):
                        kc = 8192 + m * 512 + hl * 128
                        P.op("vector", lambda e, qc=qc, kc=kc, l=l, m=m: e.scalar_tensor_tensor(
                            t.SCRB[:, :], G2[:, qc:qc + 128], SCALE, G2[:, kc:kc + 128],
                            op0=ALU.mult, op1=ALU.mult,
                            accum_out=t.SH[:, l * 16 + m: l * 16 + m + 1]),
                            r=qkkeys + t.BS_K, w=t.BS_K)
                P.op("vector", lambda e: e.reduce_max(
                    t.MX[:, :], t.SH[:, :].rearrange("p (l m) -> p l m", m=16), axis=AX.X),
                    r=t.BS_K, w=t.BS_K)
                P.op("vector", lambda e: e.tensor_tensor(
                    t.PH[:, :].rearrange("p (l m) -> p l m", m=16),
                    t.SH[:, :].rearrange("p (l m) -> p l m", m=16),
                    t.MX[:, :].unsqueeze(2).broadcast_to([128, 16, 16]), op=ALU.subtract),
                    r=t.BS_K, w=t.BS_K)
                P.op("scalar", lambda e: e.activation(t.PH[:, :], t.PH[:, :], AF.Exp),
                     r=t.BS_K, w=t.BS_K)
                P.op("vector", lambda e: e.reduce_sum(
                    t.SM[:, :], t.PH[:, :].rearrange("p (l m) -> p l m", m=16), axis=AX.X),
                    r=t.BS_K, w=t.BS_K)
                P.op("vector", lambda e: e.reciprocal(t.SM[:, :], t.SM[:, :]), r=t.BS_K, w=t.BS_K)
                pn = t.PN[hl]
                P.op("vector", lambda e, pn=pn: e.tensor_tensor(
                    pn[:, :].rearrange("p (l m) -> p l m", m=16),
                    t.PH[:, :].rearrange("p (l m) -> p l m", m=16),
                    t.SM[:, :].unsqueeze(2).broadcast_to([128, 16, 16]), op=ALU.mult),
                    r=t.BS_K, w=t.BS_K)
                b0_tick(2)
            for l in range(16):
                for hl in range(4):
                    pn = t.PN[hl]
                    for m in range(l + 1):
                        vc = 16384 + m * 512 + hl * 128
                        if m == 0:
                            P.op("vector", lambda e, vc=vc, hl=hl, pn=pn, l=l: e.tensor_scalar(
                                t.OB[:, hl * 128:(hl + 1) * 128], G2[:, vc:vc + 128],
                                pn[:, l * 16: l * 16 + 1], None, op0=ALU.mult),
                                r=vkeys + t.BS_K, w=t.BS_K)
                        else:
                            P.op("vector", lambda e, vc=vc, hl=hl, pn=pn, l=l, m=m:
                                 e.scalar_tensor_tensor(
                                     t.OB[:, hl * 128:(hl + 1) * 128], G2[:, vc:vc + 128],
                                     pn[:, l * 16 + m: l * 16 + m + 1],
                                     t.OB[:, hl * 128:(hl + 1) * 128],
                                     op0=ALU.mult, op1=ALU.add),
                                 r=vkeys + t.BS_K, w=t.BS_K)
                P.op("vector", lambda e: e.tensor_copy(t.OBB[:, :], t.OB[:, :]),
                     r=t.BS_K, w=t.BS_K)
                P.op("gpsimd", lambda e, ncx=ncx, l=l, hg=hg: e.dma_start(
                    out=y1_d[ncx, l * 128:(l + 1) * 128, hg * 512:(hg + 1) * 512],
                    in_=t.OBB[:, :]),
                    r=t.BS_K, w=[f"y1.{ncx}.{l}.{hg}"], dma=True)
                b0_tick(2)
        y1keys = [f"y1.{ncx}.{l}.{hg}" for l in range(16) for hg in range(4)]
        for ec in range(FC):
            P.op("scalar", lambda e, ec=ec, ncx=ncx: e.dma_start(
                out=G1[:, ec * 2048:(ec + 1) * 2048],
                in_=y1_d[ncx][:, ec * 128:(ec + 1) * 128], transpose=True),
                r=y1keys, w=[f"G1#{ec}"], dma=True)
        for cg in range(4):
            for ec in range(FC):
                c0 = 24576 + ec * 512
                P.op("scalar", lambda e, cg=cg, ec=ec, c0=c0: e.dma_start(
                    out=G2[:, c0:c0 + 512],
                    in_=wout_bf[cg * 512:(cg + 1) * 512, ec * 128:(ec + 1) * 128],
                    transpose=True),
                    r=[f"woutbf.{cg}"], w=keys2k("G2", c0, c0 + 512), dma=True)
            for l in range(16):
                grp, gk = t.rot()
                for ec in range(FC):
                    P.op("tensor", lambda e, grp=grp, ec=ec, l=l: e.matmul(
                        grp[:, :], G1[:, ec * 2048 + l * 128: ec * 2048 + (l + 1) * 128],
                        G2[:, 24576 + ec * 512: 24576 + (ec + 1) * 512],
                        start=(ec == 0), stop=(ec == FC - 1)),
                        r=[f"G1#{ec}"] + keys2k("G2", 24576 + ec * 512, 25088 + ec * 512),
                        w=[gk])
                P.op("scalar", lambda e, grp=grp: e.activation(t.S1T[:, :], grp[:, :], AF.Copy),
                     r=[gk], w=t.BS_K)
                P.op("gpsimd", lambda e, ncx=ncx, l=l, cg=cg: e.dma_start(
                    out=s1_d[ncx, l * 128:(l + 1) * 128, cg * 512:(cg + 1) * 512],
                    in_=t.S1T[:, :]),
                    r=t.BS_K, w=[f"s1.{ncx}.{l}.{cg}"], dma=True)

    b0_flush()
    for rg in range(4):
        l0 = rg * 4
        xnd_rg = [f"xnd.{ncx}.{l}" for ncx in range(2) for l in range(l0, l0 + 4)]
        for ec in range(FC):
            for ncx in range(2):
                c0 = 12288 + ec * 1024 + ncx * 512
                P.op("scalar", lambda e, ec=ec, ncx=ncx, l0=l0, c0=c0: e.dma_start(
                    out=G2[:, c0:c0 + 512],
                    in_=xn_d[ncx, l0 * 128:(l0 + 4) * 128, ec * 128:(ec + 1) * 128],
                    transpose=True),
                    r=xnd_rg, w=keys2k("G2", c0, c0 + 512), dma=True)
        for hc in range(HC):
            for (qtd, qk, c0) in ((q1t_d, q1t_all, 28672), (q2t_d, q2t_all, 30720)):
                P.op("sync", lambda e, qtd=qtd, hc=hc, c0=c0: e.dma_start(
                    out=G2[:, c0:c0 + 2048].rearrange("p (ec d) -> p ec d", d=128),
                    in_=qtd[:, hc * 128:(hc + 1) * 128].rearrange("(ec e) d -> e ec d", e=128)),
                    r=qk, w=keys2k("G2", c0, c0 + 2048), dma=True)
            pa = hc % 2
            gsets = ((t.ps[pa * 2], f"PS.{pa * 2}"), (t.ps[pa * 2 + 1], f"PS.{pa * 2 + 1}"))
            gsets2 = ((t.ps[4 + pa], f"PS.{4 + pa}"), (t.ps[6], "PS.6"))
            for (wc0, gs) in ((28672, gsets), (30720, gsets2)):
                for half, (gg, kk) in enumerate(gs):
                    for ec in range(FC):
                        P.op("tensor", lambda e, gg=gg, wc0=wc0, ec=ec, half=half: e.matmul(
                            gg[:, :], G2[:, wc0 + ec * 128: wc0 + (ec + 1) * 128],
                            G2[:, 12288 + ec * 1024 + half * 512: 12800 + ec * 1024 + half * 512],
                            start=(ec == 0), stop=(ec == FC - 1)),
                            r=keys2k("G2", wc0, wc0 + 2048) +
                              keys2k("G2", 12288 + ec * 1024, 12288 + (ec + 1) * 1024),
                            w=[kk])
            for half, (gg, kk) in enumerate(gsets):
                P.op("scalar", lambda e, gg=gg, half=half: e.activation(
                    t.SGM[:, half * 512:(half + 1) * 512], gg[:, :], AF.Sigmoid),
                    r=[kk], w=t.SGM_K)
            hcol = hc * 1024
            hidt = G1 if hcol + 1024 <= 32768 else G2
            hbase = hcol if hcol + 1024 <= 32768 else hcol - 32768
            hname = "G1" if hcol + 1024 <= 32768 else "G2"
            for half, (gg, kk) in enumerate(gsets2):
                P.op("vector", lambda e, hidt=hidt, hbase=hbase, half=half, gg=gg: e.tensor_tensor(
                    hidt[:, hbase + half * 512: hbase + (half + 1) * 512],
                    t.SGM[:, half * 512:(half + 1) * 512], gg[:, :], op=ALU.mult),
                    r=t.SGM_K + [kk], w=keys2k(hname, hbase + half * 512, hbase + (half + 1) * 512))
        for ec in range(FC):
            P.op("sync", lambda e, ec=ec: e.dma_start(
                out=G3[:, 0:FH].rearrange("p (hcc d) -> p hcc d", d=128),
                in_=q3t_d[:, ec * 128:(ec + 1) * 128].rearrange("(hcc h) d -> h hcc d", h=128)),
                r=q3t_all, w=keys2k("G3", 0, FH), dma=True)
            pa = ec % 2
            for half in range(2):
                gg = t.ps[pa * 2 + half]
                kk = f"PS.{pa * 2 + half}"
                for hc in range(HC):
                    hcol = hc * 1024
                    hidt = G1 if hcol + 1024 <= 32768 else G2
                    hbase = hcol if hcol + 1024 <= 32768 else hcol - 32768
                    hname = "G1" if hcol + 1024 <= 32768 else "G2"
                    P.op("tensor", lambda e, gg=gg, hc=hc, hidt=hidt, hbase=hbase, half=half:
                         e.matmul(gg[:, :], G3[:, hc * 128:(hc + 1) * 128],
                                  hidt[:, hbase + half * 512: hbase + half * 512 + 512],
                                  start=(hc == 0), stop=(hc == HC - 1)),
                         r=keys2k("G3", hc * 128, (hc + 1) * 128) +
                           keys2k(hname, hbase, hbase + 1024), w=[kk])
                P.op("scalar", lambda e, gg=gg, half=half: e.activation(
                    t.GFT[:, half * 512:(half + 1) * 512], gg[:, :], AF.Silu),
                    r=[kk], w=t.GFT_K)
            P.op("gpsimd", lambda e, ec=ec: e.dma_start(
                out=gfm_d[ec * 128:(ec + 1) * 128, :], in_=t.GFT[:, :]),
                r=t.GFT_K, w=[f"gfm.{ec}"], dma=True)
        gfm_all = [f"gfm.{ec}" for ec in range(FC)]
        for rt in range(8):
            ncx = rt // 4
            l = l0 + rt % 4
            pp = rt % 2
            xt = t.XT[pp]
            src_ap, src_keys = x1src(l, ncx)
            P.op("sync", lambda e, xt=xt, src_ap=src_ap: e.dma_start(out=xt[:, :], in_=src_ap),
                 r=src_keys, w=[f"xt{pp}"], dma=True)
            P.op("scalar", lambda e, rt=rt: e.dma_start(
                out=t.XNT[:, :], in_=gfm_d[:, rt * 128:(rt + 1) * 128], transpose=True),
                r=gfm_all, w=["XNT"], dma=True)
            P.op("sync", lambda e, ncx=ncx, l=l: e.dma_start(
                out=G3[:, 8192:8192 + 2048], in_=s1_d[ncx, l * 128:(l + 1) * 128, :]),
                r=[f"s1.{ncx}.{l}.{cg}" for cg in range(4)],
                w=keys2k("G3", 8192, 8192 + 2048), dma=True)
            for cg in range(4):
                P.op("vector", lambda e, cg=cg: e.tensor_tensor(
                    t.SCR[:, 0:512],
                    G3[:, 8192 + cg * 512:8192 + (cg + 1) * 512],
                    t.XNT[:, cg * 512:(cg + 1) * 512], op=ALU.mult),
                    r=keys2k("G3", 8192, 10240) + ["XNT"], w=["SCR"])
                P.op("vector", lambda e, xt=xt, cg=cg: e.tensor_tensor(
                    xt[:, cg * 512:(cg + 1) * 512], xt[:, cg * 512:(cg + 1) * 512],
                    t.SCR[:, 0:512], op=ALU.add),
                    r=[f"xt{pp}", "SCR"], w=[f"xt{pp}"])
            P.op("sync", lambda e, l=l, ncx=ncx, xt=xt: e.dma_start(
                out=out_h[l, ncx * 128:(ncx + 1) * 128, :], in_=xt[:, :]),
                r=[f"xt{pp}"], w=[f"out.{l}.{ncx}"], dma=True)


def a_scratch(nc):
    return dict(
        xn_d=nc.dram_tensor("xn_d", [T, C], BF16).ap(),
        bw_d=nc.dram_tensor("bw_d", [C, C], BF16).ap(),
        g_d=nc.dram_tensor("g_d", [T, C], BF16).ap(),
        v_d=nc.dram_tensor("v_d", [H, T, DH], BF16).ap(),
        y_d=nc.dram_tensor("y_d", [H, DH, T], BF16).ap(),
    )


def b_scratch(nc):
    return dict(
        wqkv_bf=nc.dram_tensor("wqkv_bf", [3 * C, C], BF16).ap(),
        wout_bf=nc.dram_tensor("wout_bf", [C, C], BF16).ap(),
        q1_d=nc.dram_tensor("q1_d", [FH, C], BF16).ap(),
        q2_d=nc.dram_tensor("q2_d", [FH, C], BF16).ap(),
        q3_d=nc.dram_tensor("q3_d", [C, FH], BF16).ap(),
        q1t_d=nc.dram_tensor("q1t_d", [C, FH], BF16).ap(),
        q2t_d=nc.dram_tensor("q2t_d", [C, FH], BF16).ap(),
        q3t_d=nc.dram_tensor("q3t_d", [FH, C], BF16).ap(),
        xn1_d=nc.dram_tensor("xn1_d", [2, 2048, C], BF16).ap(),
        y1_d=nc.dram_tensor("y1_d", [2, 2048, C], BF16).ap(),
        s1_d=nc.dram_tensor("s1_d", [2, 2048, C], BF16).ap(),
        gfm_d=nc.dram_tensor("gfm_d", [C, 1024], BF16).ap(),
    )


def a_inputs(nc):
    return dict(
        x=nc.dram_tensor("xa", [2, T, C], F32, kind="ExternalInput").ap(),
        wq=nc.dram_tensor("wq", [H, C, DH], F32, kind="ExternalInput").ap(),
        wk=nc.dram_tensor("wk", [H, C, DH], F32, kind="ExternalInput").ap(),
        wv=nc.dram_tensor("wv", [H, C, DH], F32, kind="ExternalInput").ap(),
        bq=nc.dram_tensor("bq", [H, DH], F32, kind="ExternalInput").ap(),
        bk=nc.dram_tensor("bk", [H, DH], F32, kind="ExternalInput").ap(),
        bv=nc.dram_tensor("bv", [H, DH], F32, kind="ExternalInput").ap(),
        wo=nc.dram_tensor("wo", [C, C], F32, kind="ExternalInput").ap(),
        bo=nc.dram_tensor("bo", [C], F32, kind="ExternalInput").ap(),
        wf=nc.dram_tensor("wf", [C, C], F32, kind="ExternalInput").ap(),
        bf=nc.dram_tensor("bf", [C], F32, kind="ExternalInput").ap(),
    )


def b_inputs(nc):
    return dict(
        wqkv=nc.dram_tensor("wqkv", [3 * C, C], F32, kind="ExternalInput").ap(),
        wout=nc.dram_tensor("wout", [C, C], F32, kind="ExternalInput").ap(),
        w1=nc.dram_tensor("w1", [FH, C], F32, kind="ExternalInput").ap(),
        w2=nc.dram_tensor("w2", [FH, C], F32, kind="ExternalInput").ap(),
        w3=nc.dram_tensor("w3", [C, FH], F32, kind="ExternalInput").ap(),
    )


def build_a_program():
    nc = bass.Bass("TRN2", target_bir_lowering=False)
    es = contextlib.ExitStack()
    t = alloc_all(nc, es)
    io = a_inputs(nc)
    io.update(a_scratch(nc))
    x1_h = nc.dram_tensor("x1out", [2, T, C], F32, kind="ExternalOutput").ap()
    io["x1dst"] = lambda b, i: (x1_h[b, i * 128:(i + 1) * 128, :], f"x1.{b}.{i}")
    P = Plan(nc)
    prologue_consts(P, t)
    build_stage_a(nc, P, t, io)
    with es:
        P.build(es)
    return nc


def build_b_program():
    nc = bass.Bass("TRN2", target_bir_lowering=False)
    es = contextlib.ExitStack()
    t = alloc_all(nc, es)
    io = b_inputs(nc)
    io.update(b_scratch(nc))
    x1_h = nc.dram_tensor("x1c", [B, NB, C], F32, kind="ExternalInput").ap()
    io["x1src"] = lambda l, ncx: (x1_h[l, ncx * 128:(ncx + 1) * 128, :], [])
    io["out"] = nc.dram_tensor("outc", [B, NB, C], F32, kind="ExternalOutput").ap()
    P = Plan(nc)
    prologue_consts(P, t)
    build_stage_b(nc, P, t, io)
    with es:
        P.build(es)
    return nc


def build_fused_program():
    nc = bass.Bass("TRN2", target_bir_lowering=False)
    es = contextlib.ExitStack()
    t = alloc_all(nc, es)
    io = a_inputs(nc)
    io.update(b_inputs(nc))
    io.update(a_scratch(nc))
    io.update(b_scratch(nc))
    x1s = [nc.dram_tensor(f"x1s{b}", [T, C], F32).ap() for b in range(2)]
    xall = [nc.dram_tensor(f"xall{b}", [NCORES * NB, C], F32).ap() for b in range(2)]
    io["x1dst"] = lambda b, i: (x1s[b][i * 128:(i + 1) * 128, :], f"x1.{b}.{i}")
    io["x1src"] = lambda l, ncx: (
        xall[l % 2][(l // 2) * NB + ncx * 128: (l // 2) * NB + (ncx + 1) * 128, :],
        [f"xall.{l % 2}"])
    io["out"] = nc.dram_tensor("outc", [B, NB, C], F32, kind="ExternalOutput").ap()

    P = Plan(nc)
    prologue_consts(P, t)
    build_stage_a(nc, P, t, io)
    for b in range(2):
        def cc(e, b=b):
            with allow_ici_all_to_all():
                return e.collective_compute(
                    "AllToAll", ALU.bypass,
                    replica_groups=[list(range(NCORES))],
                    ins=[x1s[b].opt()],
                    outs=[xall[b].opt()],
                )
        P.op("gpsimd", cc, r=[f"x1.{b}.{i}" for i in range(16)], w=[f"xall.{b}"], cc=True)
    P.barrier()
    build_stage_b(nc, P, t, io)
    with es:
        P.build(es)
    return nc


def kernel(L0_attn_x, L0_attn_Wq, L0_attn_bq, L0_attn_Wk, L0_attn_bk, L0_attn_Wv,
           L0_attn_bv, L0_attn_Wo, L0_attn_bo, L0_ffn_weight, L0_ffn_bias,
           L1_attn_weight_qkv, L1_attn_weight_out, L1_ffn_w1, L1_ffn_w2, L1_ffn_w3):
    f = lambda a: np.ascontiguousarray(np.asarray(a, np.float32))
    x = f(L0_attn_x)
    wA = {"wq": f(L0_attn_Wq), "wk": f(L0_attn_Wk), "wv": f(L0_attn_Wv),
          "bq": f(L0_attn_bq), "bk": f(L0_attn_bk), "bv": f(L0_attn_bv),
          "wo": f(L0_attn_Wo), "bo": f(L0_attn_bo),
          "wf": f(L0_ffn_weight), "bf": f(L0_ffn_bias)}
    wB = {"wqkv": f(L1_attn_weight_qkv), "wout": f(L1_attn_weight_out),
          "w1": f(L1_ffn_w1), "w2": f(L1_ffn_w2), "w3": f(L1_ffn_w3)}
    if FUSED:
        ncF = build_fused_program()
        in_maps = [dict(wA, **wB, xa=np.ascontiguousarray(x[2 * c:2 * c + 2]))
                   for c in range(NCORES)]
        res = run_bass_kernel_spmd(ncF, in_maps, core_ids=list(range(NCORES)))
        return np.concatenate([res.results[c]["outc"] for c in range(NCORES)], axis=1)

    ncA = build_a_program()
    in_maps = [dict(wA, xa=np.ascontiguousarray(x[2 * c:2 * c + 2])) for c in range(NCORES)]
    resA = run_bass_kernel_spmd(ncA, in_maps, core_ids=list(range(NCORES)))
    x1 = np.concatenate([resA.results[c]["x1out"] for c in range(NCORES)], axis=0)
    ncB = build_b_program()
    in_maps_b = [dict(wB, x1c=np.ascontiguousarray(x1[:, c * NB:(c + 1) * NB, :]))
                 for c in range(NCORES)]
    resB = run_bass_kernel_spmd(ncB, in_maps_b, core_ids=list(range(NCORES)))
    return np.concatenate([resB.results[c]["outc"] for c in range(NCORES)], axis=1)


# version 13
# speedup vs baseline: 1.0167x; 1.0167x over previous
import contextlib
import numpy as np
import concourse.bass as bass
import concourse.mybir as mybir
from concourse.bass_utils import run_bass_kernel_spmd
from concourse.aot_env import allow_ici_all_to_all

F32 = mybir.dt.float32
BF16 = mybir.dt.bfloat16
AF = mybir.ActivationFunctionType
ALU = mybir.AluOpType
AX = mybir.AxisListType

NCORES = 8
B, T, C = 16, 2048, 2048
H, DH = 16, 128
FC = C // 128
SCALE = float(DH) ** -0.5
NEG = -1.0e30
FH = 5632
HC = FH // 128
NB = 256

FUSED = True

ENGS = ("tensor", "vector", "scalar", "gpsimd", "sync")
DMA_POOL = 8


class _Op:
    __slots__ = ("eng", "fn", "reads", "writes", "dma", "idx", "need_inc", "token",
                 "deps", "sem", "selfwait", "barrier", "cc")


class Plan:

    def __init__(self, nc):
        self.nc = nc
        self.ops = []
        self.last_w = {}
        self.readers = {}
        self.dma_idx = {e: 0 for e in ENGS}
        self.cc_idx = 0

    def op(self, eng, fn, r=(), w=(), dma=False, cc=False):
        o = _Op()
        o.eng, o.fn, o.reads, o.writes, o.dma = eng, fn, list(r), list(w), dma
        o.cc = cc
        o.need_inc = False
        o.token = None
        o.sem = None
        o.selfwait = None
        o.barrier = False
        o.idx = len(self.ops)
        if cc:
            o.sem = ("cc", self.cc_idx)
            self.cc_idx += 1
        elif dma:
            o.sem = (eng, self.dma_idx[eng] % DMA_POOL)
            self.dma_idx[eng] += 1
        deps = {}

        def add_dep(p):
            if p is None or p is o:
                return
            if p.eng == o.eng and not p.dma and not o.dma and p.eng == "tensor":
                return
            deps[p.idx] = p

        for k in o.reads:
            add_dep(self.last_w.get(k))
        for k in o.writes:
            add_dep(self.last_w.get(k))
            for rd in self.readers.get(k, ()):
                add_dep(rd)
        best = {}
        for p in deps.values():
            sk = p.sem if (p.dma or p.cc) else p.eng
            q = best.get(sk)
            if q is None or p.idx > q.idx:
                best[sk] = p
        o.deps = list(best.values())
        for p in o.deps:
            p.need_inc = True
        for k in o.writes:
            self.last_w[k] = o
            self.readers[k] = []
        for k in o.reads:
            self.readers.setdefault(k, []).append(o)
        self.ops.append(o)
        return o

    def barrier(self):
        last_tensor = None
        for o in reversed(self.ops):
            if o.eng == "tensor" and not o.barrier:
                last_tensor = o
                break
        if last_tensor is not None:
            last_tensor.need_inc = True
        for e in ENGS:
            o = _Op()
            o.eng, o.fn, o.reads, o.writes, o.dma = e, (lambda en: en.nop()), [], [], False
            if e == "tensor":
                o.fn = None
            o.cc = False
            o.need_inc = e != "tensor"
            o.token = None
            o.sem = None
            o.selfwait = None
            o.barrier = True
            o.idx = len(self.ops)
            o.deps = []
            self.ops.append(o)
        self.last_w = {}
        self.readers = {}

    def build(self, es):
        nc = self.nc
        sems = {e: es.enter_context(nc.semaphore(f"sem_{e}")) for e in ENGS}
        dsems = {}
        for e in ENGS:
            n = min(self.dma_idx[e], DMA_POOL)
            for i in range(n):
                dsems[(e, i)] = es.enter_context(nc.semaphore(f"dsem_{e}{i}"))
        for i in range(self.cc_idx):
            dsems[("cc", i)] = es.enter_context(nc.semaphore(f"ccsem{i}"))
        counts = {e: 0 for e in ENGS}
        dcounts = {k: 0 for k in dsems}
        barrier_waits = {}
        pending_barrier = []
        for o in self.ops:
            if o.barrier:
                pending_barrier.append(o)
                if o.need_inc:
                    counts[o.eng] += 1
                o.token = counts[o.eng]
                if (o.idx + 1 >= len(self.ops) or not getattr(self.ops[o.idx + 1], "barrier", False)):
                    import os
                    snap = {}
                    for e2 in ENGS:
                        if counts[e2]:
                            snap[e2] = counts[e2]
                    if os.environ.get("BARRIER_NODMA", "0") != "1":
                        for k2, v2 in dcounts.items():
                            if v2:
                                snap[k2] = v2
                    for bo in pending_barrier:
                        barrier_waits[bo.idx] = snap
                    pending_barrier = []
                continue
            if o.cc:
                o.need_inc = True
                dcounts[o.sem] = 1
                o.token = 1
            elif o.dma:
                o.need_inc = True
                o.selfwait = dcounts[o.sem]
                dcounts[o.sem] += 16
                o.token = dcounts[o.sem]
            elif o.need_inc:
                counts[o.eng] += 1
                o.token = counts[o.eng]
        totals = dict(counts)
        dtotals = dict(dcounts)

        def semref_key(k):
            return dsems[k] if isinstance(k, tuple) else sems[k]

        def emit_for(eng_name, e):
            fl = {}
            for o in self.ops:
                if o.eng != eng_name:
                    continue
                if o.barrier:
                    snap = barrier_waits[o.idx]
                    if o.fn is not None:
                        ins = o.fn(e)
                        ins.then_inc(sems[o.eng], 1)
                    for k, v in snap.items():
                        if k == eng_name:
                            continue
                        if v > fl.get(k, 0):
                            e.wait_ge(semref_key(k), v)
                            fl[k] = v
                    continue
                if o.dma and o.selfwait > fl.get(o.sem, 0):
                    e.wait_ge(dsems[o.sem], o.selfwait)
                    fl[o.sem] = o.selfwait
                for p in o.deps:
                    k = p.sem if (p.dma or p.cc) else p.eng
                    if p.token > fl.get(k, 0):
                        e.wait_ge(semref_key(k), p.token)
                        fl[k] = p.token
                ins = o.fn(e)
                if o.need_inc:
                    if o.cc:
                        ins.then_inc(dsems[o.sem], 1)
                    else:
                        ins.then_inc(dsems[o.sem] if o.dma else sems[o.eng], 16 if o.dma else 1)
            if eng_name == "sync":
                for e2 in ENGS:
                    if totals[e2] > fl.get(e2, 0):
                        e.wait_ge(sems[e2], totals[e2])
                for k, v in dtotals.items():
                    if v > fl.get(k, 0):
                        e.wait_ge(dsems[k], v)

        with nc.Block() as block:
            for e in ENGS:
                getattr(block, e)((lambda en: (lambda eng_obj: emit_for(en, eng_obj)))(e))


def keys2k(name, c0, c1):
    return [f"{name}#{i}" for i in range(c0 // 2048, (c1 + 2047) // 2048)]


class NS:
    pass


def alloc_all(nc, es):
    t = NS()
    t.ps = [es.enter_context(nc.psum_tensor(f"ps{i}", [128, 512], F32)) for i in range(7)]
    t.psl = es.enter_context(nc.psum_tensor("psl", [1, 512], F32))
    t.ps_i = 0

    def rot():
        t.ps_i = (t.ps_i + 1) % 7
        return t.ps[t.ps_i], f"PS.{t.ps_i}"
    t.rot = rot
    sb = lambda n, sh, dt: es.enter_context(nc.sbuf_tensor(n, sh, dt))
    t.G1t = sb("G1", [128, 32768], BF16)
    t.G2t = sb("G2", [128, 32768], BF16)
    t.G3t = sb("G3", [128, 16384], BF16)
    t.XT = [sb(f"XT{i}", [128, C], F32) for i in range(2)]
    t.XNT = sb("XNT", [128, C], BF16)
    t.SCR = sb("SCR", [128, 512], F32)
    t.ones_r = sb("ones_r", [1, 512], BF16)
    t.ones_c = sb("ones_c", [128, 1], BF16)
    t.ones_f = sb("ones_f", [128, 1], F32)
    t.ones_fr = sb("ones_fr", [1, 128], F32)
    t.negm = [sb(f"negm{i}", [128, 1], F32) for i in range(2)]
    t.accq = [sb(f"accq{i}", [128, 1], F32) for i in range(2)]
    t.tmp1 = [sb(f"tmp1_{i}", [128, 1], F32) for i in range(2)]
    t.rstd = [sb(f"rstd{i}", [128, 1], F32) for i in range(2)]
    t.BQ = sb("BQ", [1, 2048], BF16)
    t.BK = sb("BK", [1, 2048], BF16)
    t.BV = sb("BV", [1, 2048], BF16)
    t.BO = sb("BO", [1, 2048], BF16)
    t.BF_ = sb("BF_", [1, 2048], BF16)
    
    t.MASKT = sb("MASKT", [128, 128], BF16)
    t.GRW = sb("GRW", [128, 1], F32)
    t.INV = sb("INV", [128, 1], F32)
    t.T11 = sb("T11", [1, 1], F32)

    t.G1 = t.G1t.ap()
    t.G2 = t.G2t.ap()
    t.G3 = t.G3t.ap()
    g2, g3 = t.G2, t.G3
    g3f = t.G3t.ap().bitcast(F32)
    t.XNFM = t.G1
    t.PTOFF = []
    off = 0
    for c in range(16):
        t.PTOFF.append(off)
        off += T - c * 128
    t.PT = g2[:, 0:17408]
    t.LNL = g2[0:1, 30720:32768]
    t.LNL_K = keys2k("G2", 30720, 32768)
    t.WQH = [g2[:, 18432:20480], g2[:, 20480:22528]]
    t.WKH = [g2[:, 22528:24576], g2[:, 24576:26624]]
    t.QFM = g2[:, 26624:28672]
    t.KFM = g2[:, 28672:30720]
    t.WQH_K = [keys2k("G2", 18432, 20480), keys2k("G2", 20480, 22528)]
    t.WKH_K = [keys2k("G2", 22528, 24576), keys2k("G2", 24576, 26624)]
    t.QFM_K = keys2k("G2", 26624, 28672)
    t.KFM_K = keys2k("G2", 28672, 30720)
    t.VH = g3[:, 0:2048]
    t.YH = g3[:, 2048:4096]
    t.RBT = g3[:, 4096:6144]
    t.GT = [g3[:, 6144:8192], g3[:, 8192:10240]]
    t.WVC = g3[:, 10240:12288]
    t.VT = g3[:, 12288:12800]
    t.VH_K, t.YH_K, t.RBT_K = ["G3#0"], ["G3#1"], ["G3#2"]
    t.GT_K = [["G3#3"], ["G3#4"]]
    t.WVC_K, t.VT_K = ["G3#5"], ["G3#6"]
    t.SH = g3f[:, 5120:5376]
    t.PH = g3f[:, 5376:5632]
    t.PN = [g3f[:, 5632 + i * 256: 5888 + i * 256] for i in range(4)]
    t.OB = g3f[:, 6656:7168]
    t.OBB = g3[:, 14336:14848]
    t.S1T = g3[:, 14848:15360]
    t.SCRB = g3[:, 15360:15488]
    t.MX = g3f[:, 7744:7760]
    t.SM = g3f[:, 7760:7776]
    t.BS_K = keys2k("G3", 10240, 15552)
    t.SGM = g3f[:, 2816:3840]
    t.SGM_K = keys2k("G3", 5632, 7680)
    t.GFT = g3[:, 10240:11264]
    t.GFT_K = keys2k("G3", 10240, 11264)
    return t


def prologue_consts(P, t):
    P.op("gpsimd", lambda e: e.memset(t.ones_r[:, :], 1.0), w=["ones_r"])
    P.op("gpsimd", lambda e: e.memset(t.ones_c[:, :], 1.0), w=["ones_c"])
    P.op("gpsimd", lambda e: e.memset(t.ones_f[:, :], 1.0), w=["ones_f"])
    P.op("gpsimd", lambda e: e.memset(t.ones_fr[:, :], 1.0), w=["ones_fr"])


def ln_tile(P, t, xt, xt_key, xnt, xnt_key, pp):
    negm, accq, tmp1, rstd = t.negm[pp], t.accq[pp], t.tmp1[pp], t.rstd[pp]
    kn, ka, kt, kr = f"negm{pp}", f"accq{pp}", f"tmp1{pp}", f"rstd{pp}"
    grp, gk = t.rot()
    P.op("vector", lambda e: e.reduce_sum(negm[:, :], xt[:, :], axis=AX.X, negate=True),
         r=[xt_key], w=[kn])
    P.op("vector", lambda e: e.tensor_scalar(negm[:, :], negm[:, :], 1.0 / C, None, op0=ALU.mult),
         r=[kn], w=[kn])
    P.op("scalar", lambda e: e.activation(grp[:, :], xt[:, 0:512], AF.Square, accum_out=accq[:, :]),
         r=[xt_key], w=[gk, ka])
    for j in range(1, 4):
        P.op("scalar", lambda e, j=j: e.activation(grp[:, :], xt[:, j * 512:(j + 1) * 512],
                                                   AF.Square, accum_out=tmp1[:, :]),
             r=[xt_key], w=[gk, kt])
        P.op("vector", lambda e: e.tensor_tensor(accq[:, :], accq[:, :], tmp1[:, :], op=ALU.add),
             r=[ka, kt], w=[ka])
    P.op("vector", lambda e: e.tensor_tensor(tmp1[:, :], negm[:, :], negm[:, :], op=ALU.mult),
         r=[kn], w=[kt])
    P.op("vector", lambda e: e.tensor_scalar(tmp1[:, :], tmp1[:, :], -1.0, 1e-5,
                                             op0=ALU.mult, op1=ALU.add),
         r=[kt], w=[kt])
    P.op("scalar", lambda e: e.activation(rstd[:, :], accq[:, :], AF.Sqrt, bias=tmp1[:, :],
                                          scale=1.0 / C),
         r=[ka, kt], w=[kr])
    P.op("vector", lambda e: e.reciprocal(rstd[:, :], rstd[:, :]), r=[kr], w=[kr])
    P.op("vector", lambda e: e.tensor_scalar(xnt[:, :], xt[:, :], negm[:, :], rstd[:, :],
                                             op0=ALU.add, op1=ALU.mult),
         r=[xt_key, kn, kr], w=[xnt_key])


def build_stage_a(nc, P, t, io):
    x_h, x1dst = io["x"], io["x1dst"]
    wq_h, wk_h, wv_h = io["wq"], io["wk"], io["wv"]
    bq_h, bk_h, bv_h, bo_h, bf_h = io["bq"], io["bk"], io["bv"], io["bo"], io["bf"]
    wo_h, wf_h = io["wo"], io["wf"]
    xn_d, bw_d, g_d, v_d, y_d = io["xn_d"], io["bw_d"], io["g_d"], io["v_d"], io["y_d"]

    for (row, src, nm) in ((t.BQ, bq_h, "BQ"), (t.BK, bk_h, "BK"), (t.BV, bv_h, "BV")):
        P.op("gpsimd", lambda e, row=row, src=src: e.dma_start(
            out=row[0:1, :], in_=src.rearrange("h d -> (h d)").unsqueeze(0)),
            w=[nm], dma=True)
    P.op("gpsimd", lambda e: e.dma_start(out=t.BO[0:1, :], in_=bo_h.unsqueeze(0)), w=["BO"], dma=True)
    P.op("gpsimd", lambda e: e.dma_start(out=t.BF_[0:1, :], in_=bf_h.unsqueeze(0)), w=["BF_"], dma=True)
    P.op("gpsimd", lambda e: e.memset(t.MASKT[:, :], 0.0), w=["MASKT"])
    P.op("gpsimd", lambda e: e.affine_select(out=t.MASKT[:, :], in_=t.MASKT[:, :],
                                             compare_op=ALU.is_ge, fill=NEG, base=0,
                                             pattern=[[1, 128]], channel_multiplier=-1),
         r=["MASKT"], w=["MASKT"])

    for oc in range(16):
        pp = oc % 2
        xt, gt, gtk = t.XT[pp], t.GT[pp], t.GT_K[pp]
        P.op("sync", lambda e, oc=oc, xt=xt: e.dma_start(
            out=xt[:, :], in_=wf_h[oc * 128:(oc + 1) * 128, :]), w=[f"xt{pp}"], dma=True)
        P.op("vector", lambda e, xt=xt, pp=pp: e.reduce_sum(
            t.tmp1[pp][:, :], xt[:, :], axis=AX.X, apply_absolute_value=True),
            r=[f"xt{pp}"], w=[f"tmp1{pp}"])
        P.op("vector", lambda e, pp=pp: e.tensor_scalar(
            t.tmp1[pp][:, :], t.tmp1[pp][:, :], 1.0 / C, None, op0=ALU.mult),
            r=[f"tmp1{pp}"], w=[f"tmp1{pp}"])
        P.op("scalar", lambda e, xt=xt: e.sign(t.XNT[:, :], xt[:, :]), r=[f"xt{pp}"], w=["XNT"])
        P.op("vector", lambda e, gt=gt, pp=pp: e.tensor_scalar(
            gt[:, :], t.XNT[:, :], t.tmp1[pp][:, :], None, op0=ALU.mult),
            r=["XNT", f"tmp1{pp}"], w=gtk)
        P.op("gpsimd", lambda e, gt=gt, oc=oc: e.dma_start(
            out=bw_d[oc * 128:(oc + 1) * 128, :], in_=gt[:, :]),
            r=gtk, w=[f"bwd.{oc}"], dma=True)
    bwd_all = [f"bwd.{oc}" for oc in range(16)]

    for b in range(2):
        for i in range(16):
            pp = i % 2
            xt = t.XT[pp]
            P.op("sync", lambda e, b=b, i=i, xt=xt: e.dma_start(
                out=xt[:, :], in_=x_h[b, i * 128:(i + 1) * 128, :]), w=[f"xt{pp}"], dma=True)
            ln_tile(P, t, xt, f"xt{pp}", t.XNT, "XNT", pp)
            P.op("gpsimd", lambda e, i=i: e.dma_start(
                out=xn_d[i * 128:(i + 1) * 128, :], in_=t.XNT[:, :]),
                r=["XNT"], w=[f"xnd.{i}"], dma=True)
        xnd_all = [f"xnd.{i}" for i in range(16)]
        for fc in range(FC):
            P.op("scalar", lambda e, fc=fc: e.dma_start(
                out=t.XNFM[:, fc * 2048:(fc + 1) * 2048],
                in_=xn_d[:, fc * 128:(fc + 1) * 128], transpose=True),
                r=xnd_all, w=[f"G1#{fc}"], dma=True)

        for half in range(2):
            for ic in range(FC):
                c0 = ic * 1024
                P.op("scalar", lambda e, ic=ic, half=half, c0=c0: e.dma_start(
                    out=t.PT[:, c0:c0 + 1024],
                    in_=bw_d[half * 1024:(half + 1) * 1024, ic * 128:(ic + 1) * 128],
                    transpose=True),
                    r=bwd_all, w=keys2k("G2", c0, c0 + 1024), dma=True)
            for i in range(16):
                pp = i % 2
                gt, gtk = t.GT[pp], t.GT_K[pp]
                for cg in range(2):
                    grp, gk = t.rot()
                    ocol = half * 1024 + cg * 512
                    for ic in range(FC):
                        P.op("tensor", lambda e, grp=grp, ic=ic, i=i, cg=cg: e.matmul(
                            grp[:, :], t.XNFM[:, ic * 2048 + i * 128: ic * 2048 + (i + 1) * 128],
                            t.PT[:, ic * 1024 + cg * 512: ic * 1024 + (cg + 1) * 512],
                            start=(ic == 0), stop=False),
                            r=[f"G1#{ic}"] + keys2k("G2", ic * 1024, ic * 1024 + 1024), w=[gk])
                    P.op("tensor", lambda e, grp=grp, ocol=ocol: e.matmul(
                        grp[:, :], t.ones_r[0:1, 0:128], t.BF_[0:1, ocol:ocol + 512],
                        start=False, stop=True),
                        r=["ones_r", "BF_"], w=[gk])
                    P.op("scalar", lambda e, grp=grp, gt=gt, cg=cg: e.activation(
                        gt[:, cg * 512:(cg + 1) * 512], grp[:, :], AF.Silu),
                        r=[gk], w=gtk)
                P.op("gpsimd", lambda e, gt=gt, i=i, half=half: e.dma_start(
                    out=g_d[i * 128:(i + 1) * 128, half * 1024:(half + 1) * 1024],
                    in_=gt[:, 0:1024]),
                    r=gtk, w=[f"gd.{i}.{half}"], dma=True)

        for hgv in range(4):
            for ic in range(FC):
                c0 = ic * 512
                P.op("gpsimd", lambda e, hgv=hgv, ic=ic, c0=c0: e.dma_start(
                    out=t.PT[:, c0:c0 + 512],
                    in_=wv_h.rearrange("h i d -> i h d")[ic * 128:(ic + 1) * 128,
                                                         4 * hgv:4 * hgv + 4, :]),
                    w=keys2k("G2", c0, c0 + 512), dma=True)
            for i in range(16):
                grp, gk = t.rot()
                for ic in range(FC):
                    P.op("tensor", lambda e, grp=grp, ic=ic, i=i: e.matmul(
                        grp[:, :], t.XNFM[:, ic * 2048 + i * 128: ic * 2048 + (i + 1) * 128],
                        t.PT[:, ic * 512:(ic + 1) * 512], start=(ic == 0), stop=False),
                        r=[f"G1#{ic}"] + keys2k("G2", ic * 512, ic * 512 + 512), w=[gk])
                P.op("tensor", lambda e, grp=grp, hgv=hgv: e.matmul(
                    grp[:, :], t.ones_r[0:1, 0:128], t.BV[0:1, hgv * 512:(hgv + 1) * 512],
                    start=False, stop=True),
                    r=["ones_r", "BV"], w=[gk])
                P.op("scalar", lambda e, grp=grp: e.activation(t.VT[:, :], grp[:, :], AF.Copy),
                     r=[gk], w=t.VT_K)
                P.op("gpsimd", lambda e, i=i, hgv=hgv: e.dma_start(
                    out=v_d.rearrange("h t d -> t h d")[i * 128:(i + 1) * 128,
                                                        4 * hgv:4 * hgv + 4, :],
                    in_=t.VT[:, :]),
                    r=t.VT_K, w=[f"vd.{hgv}.{i}"], dma=True)

        for h in range(H):
            pp = h % 2
            wq, wk = t.WQH[pp], t.WKH[pp]
            wqk, wkk = t.WQH_K[pp], t.WKH_K[pp]
            P.op("gpsimd", lambda e, h=h, wq=wq: e.dma_start(
                out=wq[:, :], in_=wq_h[h].rearrange("(fc f) d -> f fc d", f=128)),
                w=wqk, dma=True)
            P.op("gpsimd", lambda e, h=h, wk=wk: e.dma_start(
                out=wk[:, :], in_=wk_h[h].rearrange("(fc f) d -> f fc d", f=128)),
                w=wkk, dma=True)
            P.op("sync", lambda e, h=h: e.dma_start(
                out=t.VH[:, :].rearrange("t (c d) -> t c d", d=128),
                in_=v_d[h].rearrange("(c t) d -> t c d", t=128)),
                r=[f"vd.{h // 4}.{i}" for i in range(16)], w=t.VH_K, dma=True)
            for dst, dkey, wgt, wkey, brow, bkey, scl in (
                    (t.QFM, t.QFM_K, wq, wqk, t.BQ, "BQ", SCALE),
                    (t.KFM, t.KFM_K, wk, wkk, t.BK, "BK", None)):
                for tg in range(4):
                    grp, gk = t.rot()
                    for fc in range(FC):
                        P.op("tensor", lambda e, grp=grp, wgt=wgt, fc=fc, tg=tg: e.matmul(
                            grp[:, :], wgt[:, fc * 128:(fc + 1) * 128],
                            t.XNFM[:, fc * 2048 + tg * 512: fc * 2048 + (tg + 1) * 512],
                            start=(fc == 0), stop=False),
                            r=wkey + [f"G1#{fc}"], w=[gk])
                    P.op("tensor", lambda e, grp=grp, brow=brow, h=h: e.matmul(
                        grp[:, :], brow[0:1, h * 128:(h + 1) * 128], t.ones_r[0:1, 0:512],
                        start=False, stop=True),
                        r=[bkey, "ones_r"], w=[gk])
                    if scl is not None:
                        P.op("scalar", lambda e, grp=grp, dst=dst, tg=tg, scl=scl: e.activation(
                            dst[:, tg * 512:(tg + 1) * 512], grp[:, :], AF.Copy, scale=scl),
                            r=[gk], w=dkey)
                    else:
                        P.op("scalar", lambda e, grp=grp, dst=dst, tg=tg: e.activation(
                            dst[:, tg * 512:(tg + 1) * 512], grp[:, :], AF.Copy),
                            r=[gk], w=dkey)
            for c in range(16):
                Lc = T - c * 128
                for g in range((Lc + 511) // 512):
                    N = min(512, Lc - g * 512)
                    grp, gk = t.rot()
                    P.op("tensor", lambda e, grp=grp, c=c, g=g, N=N: e.matmul(
                        grp[:, 0:N], t.KFM[:, c * 128:(c + 1) * 128],
                        t.QFM[:, c * 128 + g * 512: c * 128 + g * 512 + N],
                        start=True, stop=True),
                        r=t.KFM_K + t.QFM_K, w=[gk])
                    if g == 0:
                        P.op("vector", lambda e, grp=grp: e.tensor_tensor(
                            grp[:, 0:128], grp[:, 0:128], t.MASKT[:, :], op=ALU.add),
                            r=[gk, "MASKT"], w=[gk])
                    p0 = t.PTOFF[c] + g * 512
                    P.op("scalar", lambda e, grp=grp, p0=p0, N=N: e.activation(
                        t.PT[:, p0:p0 + N], grp[:, 0:N], AF.Exp),
                        r=[gk], w=keys2k("G2", p0, p0 + N))
            for tg in range(4):
                t0 = tg * 512
                cmax = min(15, (t0 + 511) // 128)
                for c in range(cmax + 1):
                    lo = max(t0, c * 128)
                    N = t0 + 512 - lo
                    src = t.PTOFF[c] + lo - c * 128
                    P.op("tensor", lambda e, lo=lo, t0=t0, N=N, src=src, c=c, cmax=cmax: e.matmul(
                        t.psl[0:1, lo - t0:lo - t0 + N], t.ones_c[:, :], t.PT[:, src:src + N],
                        start=(c == 0), stop=(c == cmax)),
                        r=["ones_c"] + keys2k("G2", src, src + N), w=["PSL"])
                P.op("scalar", lambda e, tg=tg: e.activation(
                    t.LNL[0:1, tg * 512:(tg + 1) * 512], t.psl[0:1, :], AF.Ln),
                    r=["PSL"], w=t.LNL_K)
            for c in range(16):
                for tg in range((c * 128) // 512, 4):
                    t0 = tg * 512
                    lo = max(t0, c * 128)
                    N = t0 + 512 - lo
                    src = t.PTOFF[c] + lo - c * 128
                    cmax = tg * 4 + 3
                    P.op("tensor", lambda e, c=c, tg=tg, lo=lo, t0=t0, N=N, src=src, cmax=cmax:
                         e.matmul(t.ps[tg][:, lo - t0:lo - t0 + N], t.VH[:, c * 128:(c + 1) * 128],
                                  t.PT[:, src:src + N], start=(c == 0), stop=(c == cmax)),
                         r=t.VH_K + keys2k("G2", src, src + N), w=[f"PS.{tg}"])
            for tg in range(4):
                gi = 4 + (tg % 3)
                grp = t.ps[gi]
                gk = f"PS.{gi}"
                P.op("tensor", lambda e, grp=grp, tg=tg: e.matmul(
                    grp[:, :], t.ones_r[0:1, 0:128], t.LNL[0:1, tg * 512:(tg + 1) * 512],
                    start=True, stop=True),
                    r=["ones_r"] + t.LNL_K, w=[gk])
                P.op("scalar", lambda e, grp=grp, tg=tg: e.activation(
                    t.RBT[:, tg * 512:(tg + 1) * 512], grp[:, :], AF.Exp, scale=-1.0),
                    r=[gk], w=t.RBT_K)
                P.op("vector", lambda e, tg=tg: e.tensor_tensor(
                    t.YH[:, tg * 512:(tg + 1) * 512], t.ps[tg][:, :],
                    t.RBT[:, tg * 512:(tg + 1) * 512], op=ALU.mult),
                    r=[f"PS.{tg}"] + t.RBT_K, w=t.YH_K)
            P.op("gpsimd", lambda e, h=h: e.dma_start(out=y_d[h], in_=t.YH[:, :]),
                 r=t.YH_K, w=[f"yd.{h}"], dma=True)

        for fc in range(FC):
            P.op("gpsimd", lambda e, fc=fc: e.dma_start(
                out=t.XNFM[:, fc * 2048:(fc + 1) * 2048], in_=wo_h[fc * 128:(fc + 1) * 128, :]),
                w=[f"G1#{fc}"], dma=True)
        yd_all = [f"yd.{h}" for h in range(H)]
        for i in range(16):
            pp = i % 2
            xt = t.XT[pp]
            gt, gtk = t.GT[pp], t.GT_K[pp]
            yt = t.XNT if pp == 0 else t.WVC
            ytk = ["XNT"] if pp == 0 else t.WVC_K
            P.op("sync", lambda e, b=b, i=i, xt=xt: e.dma_start(
                out=xt[:, :], in_=x_h[b, i * 128:(i + 1) * 128, :]), w=[f"xt{pp}"], dma=True)
            P.op("sync", lambda e, i=i, yt=yt: e.dma_start(
                out=yt[:, :].rearrange("d (h t) -> d h t", h=H),
                in_=y_d.rearrange("h d t -> d h t")[:, :, i * 128:(i + 1) * 128]),
                r=yd_all, w=ytk, dma=True)
            P.op("gpsimd", lambda e, i=i, gt=gt: e.dma_start(
                out=gt[:, :], in_=g_d[i * 128:(i + 1) * 128, :]),
                r=[f"gd.{i}.0", f"gd.{i}.1"], w=gtk, dma=True)
            for cg in range(4):
                grp, gk = t.rot()
                for fc in range(FC):
                    P.op("tensor", lambda e, grp=grp, yt=yt, fc=fc, cg=cg: e.matmul(
                        grp[:, :], yt[:, fc * 128:(fc + 1) * 128],
                        t.XNFM[:, fc * 2048 + cg * 512: fc * 2048 + (cg + 1) * 512],
                        start=(fc == 0), stop=False),
                        r=ytk + [f"G1#{fc}"], w=[gk])
                P.op("tensor", lambda e, grp=grp, cg=cg: e.matmul(
                    grp[:, :], t.ones_r[0:1, 0:128], t.BO[0:1, cg * 512:(cg + 1) * 512],
                    start=False, stop=True),
                    r=["ones_r", "BO"], w=[gk])
                P.op("vector", lambda e, grp=grp, gt=gt, cg=cg: e.scalar_tensor_tensor(
                    t.SCR[:, 0:512], grp[:, :], 1.0, gt[:, cg * 512:(cg + 1) * 512],
                    op0=ALU.mult, op1=ALU.mult),
                    r=[gk] + gtk, w=["SCR"])
                P.op("vector", lambda e, xt=xt, cg=cg: e.tensor_tensor(
                    xt[:, cg * 512:(cg + 1) * 512], xt[:, cg * 512:(cg + 1) * 512],
                    t.SCR[:, 0:512], op=ALU.add),
                    r=[f"xt{pp}", "SCR"], w=[f"xt{pp}"])
            dst_ap, dst_key = x1dst(b, i)
            P.op("sync", lambda e, xt=xt, dst_ap=dst_ap: e.dma_start(out=dst_ap, in_=xt[:, :]),
                 r=[f"xt{pp}"], w=[dst_key], dma=True)


def build_stage_b(nc, P, t, io):
    x1src = io["x1src"]
    wqkv_h, wout_h = io["wqkv"], io["wout"]
    w1_h, w2_h, w3_h = io["w1"], io["w2"], io["w3"]
    out_h = io["out"]
    wqkv_bf, wout_bf = io["wqkv_bf"], io["wout_bf"]
    q1_d, q2_d, q3_d = io["q1_d"], io["q2_d"], io["q3_d"]
    q1t_d, q2t_d, q3t_d = io["q1t_d"], io["q2t_d"], io["q3t_d"]
    xn_d, y1_d, s1_d, gfm_d = io["xn1_d"], io["y1_d"], io["s1_d"], io["gfm_d"]
    G1, G2, G3 = t.G1, t.G2, t.G3

    for j in range(12):
        P.op("gpsimd", lambda e, j=j: e.dma_start(
            out=wqkv_bf[j * 512:(j + 1) * 512, :], in_=wqkv_h[j * 512:(j + 1) * 512, :]),
            w=[f"wqkvbf.{j}"], dma=True)
    for j in range(4):
        P.op("gpsimd", lambda e, j=j: e.dma_start(
            out=wout_bf[j * 512:(j + 1) * 512, :], in_=wout_h[j * 512:(j + 1) * 512, :]),
            w=[f"woutbf.{j}"], dma=True)

    b0_steps = []

    def quantize(w_hbm, q_d, rows, cols, tagw):
        nchunk = rows // 128
        nparts = (cols + 2047) // 2048
        b0_steps.append((None, lambda: P.op("gpsimd", lambda e: e.memset(t.GRW[:, :], 0.0), w=["GRW"])))

        def gamma_load(ch, part):
            pp = (ch * nparts + part) % 2
            xt = t.XT[pp]
            c0, c1 = part * 2048, min(cols, (part + 1) * 2048)
            P.op("sync", lambda e, ch=ch, xt=xt, c0=c0, c1=c1: e.dma_start(
                out=xt[:, 0:c1 - c0], in_=w_hbm[ch * 128:(ch + 1) * 128, c0:c1]),
                w=[f"xt{pp}"], dma=True)

        def gamma_comp(ch, part):
            pp = (ch * nparts + part) % 2
            xt = t.XT[pp]
            c0, c1 = part * 2048, min(cols, (part + 1) * 2048)
            P.op("vector", lambda e, xt=xt, pp=pp, n=c1 - c0: e.reduce_sum(
                t.tmp1[pp][:, :], xt[:, 0:n], axis=AX.X, apply_absolute_value=True),
                r=[f"xt{pp}"], w=[f"tmp1{pp}"])
            P.op("gpsimd", lambda e, pp=pp: e.tensor_tensor(
                t.GRW[:, :], t.GRW[:, :], t.tmp1[pp][:, :], op=ALU.add),
                r=["GRW", f"tmp1{pp}"], w=["GRW"])
        for ch in range(nchunk):
            for part in range(nparts):
                b0_steps.append((lambda ch=ch, part=part: gamma_load(ch, part),
                                 lambda ch=ch, part=part: gamma_comp(ch, part)))

        def finalize():
            P.op("tensor", lambda e: e.matmul(t.psl[0:1, 0:1], t.GRW[:, :], t.ones_f[:, :],
                                              start=True, stop=True),
                 r=["GRW", "ones_f"], w=["PSL"])
            P.op("vector", lambda e: e.tensor_scalar(
                t.T11[:, :], t.psl[0:1, 0:1], 1.0 / (rows * cols), 1e-8,
                op0=ALU.mult, op1=ALU.add),
                r=["PSL"], w=["T11"])
            P.op("vector", lambda e: e.reciprocal(t.T11[:, :], t.T11[:, :]), r=["T11"], w=["T11"])
            grp, gk = t.rot()
            P.op("tensor", lambda e, grp=grp: e.matmul(grp[:, 0:1], t.ones_fr[:, :], t.T11[:, :],
                                                       start=True, stop=True),
                 r=["ones_fr", "T11"], w=[gk])
            P.op("vector", lambda e, grp=grp: e.tensor_copy(t.INV[:, :], grp[:, 0:1]),
                 r=[gk], w=["INV"])
        b0_steps.append((None, finalize))

        def quant_load(ch, part):
            pp = (ch * nparts + part) % 2
            xt = t.XT[pp]
            c0, c1 = part * 2048, min(cols, (part + 1) * 2048)
            P.op("sync", lambda e, ch=ch, xt=xt, c0=c0, c1=c1: e.dma_start(
                out=xt[:, 0:c1 - c0], in_=w_hbm[ch * 128:(ch + 1) * 128, c0:c1]),
                w=[f"xt{pp}"], dma=True)

        def quant_comp(ch, part):
            pp = (ch * nparts + part) % 2
            xt = t.XT[pp]
            c0, c1 = part * 2048, min(cols, (part + 1) * 2048)
            n = c1 - c0
            P.op("scalar", lambda e, xt=xt, n=n: e.activation(t.XNT[:, 0:n], xt[:, 0:n], AF.Sign),
                 r=[f"xt{pp}"], w=["XNT"])
            P.op("gpsimd", lambda e, xt=xt, n=n: e.tensor_scalar(
                xt[:, 0:n], xt[:, 0:n], t.INV[:, :], None, op0=ALU.mult),
                r=[f"xt{pp}", "INV"], w=[f"xt{pp}"])
            for sg in range(0, n, 512):
                grp, gk = t.rot()
                P.op("scalar", lambda e, xt=xt, sg=sg, grp=grp: e.activation(
                    grp[:, :], xt[:, sg:sg + 512], AF.Square),
                    r=[f"xt{pp}"], w=[gk])
                P.op("vector", lambda e, sg=sg, grp=grp: e.scalar_tensor_tensor(
                    t.XNT[:, sg:sg + 512], grp[:, :], 0.25, t.XNT[:, sg:sg + 512],
                    op0=ALU.is_gt, op1=ALU.mult),
                    r=[gk, "XNT"], w=["XNT"])
            P.op("gpsimd", lambda e, ch=ch, c0=c0, c1=c1: e.dma_start(
                out=q_d[ch * 128:(ch + 1) * 128, c0:c1], in_=t.XNT[:, 0:c1 - c0]),
                r=["XNT"], w=[f"{tagw}.{ch}.{c0}"], dma=True)
        for ch in range(nchunk):
            for part in range(nparts):
                b0_steps.append((lambda ch=ch, part=part: quant_load(ch, part),
                                 lambda ch=ch, part=part: quant_comp(ch, part)))

    q1keys = [f"q1.{ch}.0" for ch in range(HC)]
    q2keys = [f"q2.{ch}.0" for ch in range(HC)]
    q3keys = [f"q3.{ch}.{c0}" for ch in range(FC) for c0 in range(0, FH, 2048)]

    def emit_b0():
        quantize(w1_h, q1_d, FH, C, "q1")
        quantize(w2_h, q2_d, FH, C, "q2")
        quantize(w3_h, q3_d, C, FH, "q3")

        def ptq_step(qd, qtd, rk, tag, ec):
            P.op("scalar", lambda e, qd=qd, ec=ec: e.dma_start(
                out=G3[:, 0:FH], in_=qd[:, ec * 128:(ec + 1) * 128], transpose=True),
                r=rk, w=keys2k("G3", 0, FH), dma=True)
            P.op("sync", lambda e, qtd=qtd, ec=ec: e.dma_start(
                out=qtd[ec * 128:(ec + 1) * 128, :], in_=G3[:, 0:FH]),
                r=keys2k("G3", 0, FH), w=[f"{tag}.{ec}"], dma=True)
        for (qd, qtd, rk, tag) in ((q1_d, q1t_d, q1keys, "q1t"), (q2_d, q2t_d, q2keys, "q2t")):
            for ec in range(FC):
                b0_steps.append((None, lambda a=qd, b2=qtd, c=rk, d=tag, ec=ec:
                                 ptq_step(a, b2, c, d, ec)))

        def pt3_step(hc):
            P.op("scalar", lambda e, hc=hc: e.dma_start(
                out=G3[:, 0:C], in_=q3_d[:, hc * 128:(hc + 1) * 128], transpose=True),
                r=q3keys, w=keys2k("G3", 0, C), dma=True)
            P.op("sync", lambda e, hc=hc: e.dma_start(
                out=q3t_d[hc * 128:(hc + 1) * 128, :], in_=G3[:, 0:C]),
                r=keys2k("G3", 0, C), w=[f"q3t.{hc}"], dma=True)
        for hc in range(HC):
            b0_steps.append((None, lambda hc=hc: pt3_step(hc)))

    b0_ix = {"load": 0, "comp": 0}

    def b0_tick(n=1):
        for _ in range(n):
            import os
            LA = int(os.environ.get("B0_LA", "2"))
            while b0_ix["load"] < min(len(b0_steps), b0_ix["comp"] + LA):
                fn = b0_steps[b0_ix["load"]][0]
                if fn is not None:
                    fn()
                b0_ix["load"] += 1
            if b0_ix["comp"] < len(b0_steps):
                b0_steps[b0_ix["comp"]][1]()
                b0_ix["comp"] += 1

    def b0_flush():
        while b0_ix["comp"] < len(b0_steps):
            b0_tick(1)

    q1t_all = [f"q1t.{ec}" for ec in range(FC)]
    q2t_all = [f"q2t.{ec}" for ec in range(FC)]
    q3t_all = [f"q3t.{hc}" for hc in range(HC)]

    for ncx in range(2):
        for l in range(16):
            pp = l % 2
            xt = t.XT[pp]
            src_ap, src_keys = x1src(l, ncx)
            P.op("sync", lambda e, xt=xt, src_ap=src_ap: e.dma_start(out=xt[:, :], in_=src_ap),
                 r=src_keys, w=[f"xt{pp}"], dma=True)
            ln_tile(P, t, xt, f"xt{pp}", t.XNT, "XNT", pp)
            P.op("gpsimd", lambda e, l=l, ncx=ncx: e.dma_start(
                out=xn_d[ncx, l * 128:(l + 1) * 128, :], in_=t.XNT[:, :]),
                r=["XNT"], w=[f"xnd.{ncx}.{l}"], dma=True)
    emit_b0()
    import os as _os
    if _os.environ.get("B0_INLINE", "0") == "1":
        b0_flush()

    for ncx in range(2):
        xnd_nc = [f"xnd.{ncx}.{l}" for l in range(16)]
        for ec in range(FC):
            P.op("scalar", lambda e, ec=ec, ncx=ncx: e.dma_start(
                out=G1[:, ec * 2048:(ec + 1) * 2048],
                in_=xn_d[ncx][:, ec * 128:(ec + 1) * 128], transpose=True),
                r=xnd_nc, w=[f"G1#{ec}"], dma=True)
        for hg in range(4):
            bases = [hg * 512, 2048 + hg * 512, 4096 + hg * 512]
            for part, base in enumerate(bases):
                for ec in range(FC):
                    c0 = 24576 + ec * 512
                    P.op("scalar", lambda e, base=base, ec=ec, c0=c0: e.dma_start(
                        out=G2[:, c0:c0 + 512],
                        in_=wqkv_bf[base:base + 512, ec * 128:(ec + 1) * 128], transpose=True),
                        r=[f"wqkvbf.{base // 512}"], w=keys2k("G2", c0, c0 + 512), dma=True)
                for l in range(16):
                    grp, gk = t.rot()
                    for ec in range(FC):
                        P.op("tensor", lambda e, grp=grp, ec=ec, l=l: e.matmul(
                            grp[:, :], G1[:, ec * 2048 + l * 128: ec * 2048 + (l + 1) * 128],
                            G2[:, 24576 + ec * 512: 24576 + (ec + 1) * 512],
                            start=(ec == 0), stop=(ec == FC - 1)),
                            r=[f"G1#{ec}"] + keys2k("G2", 24576 + ec * 512, 25088 + ec * 512),
                            w=[gk])
                    d0 = part * 8192 + l * 512
                    P.op("scalar", lambda e, grp=grp, d0=d0: e.activation(
                        G2[:, d0:d0 + 512], grp[:, :], AF.Copy),
                        r=[gk], w=keys2k("G2", d0, d0 + 512))
                    b0_tick(1)
            qkkeys = keys2k("G2", 0, 16384)
            vkeys = keys2k("G2", 16384, 24576)
            for hl in range(4):
                P.op("gpsimd", lambda e: e.memset(t.SH[:, :], NEG), w=t.BS_K)
                for l in range(16):
                    qc = l * 512 + hl * 128
                    for m in range(l + 1):
                        kc = 8192 + m * 512 + hl * 128
                        P.op("vector", lambda e, qc=qc, kc=kc, l=l, m=m: e.scalar_tensor_tensor(
                            t.SCRB[:, :], G2[:, qc:qc + 128], SCALE, G2[:, kc:kc + 128],
                            op0=ALU.mult, op1=ALU.mult,
                            accum_out=t.SH[:, l * 16 + m: l * 16 + m + 1]),
                            r=qkkeys + t.BS_K, w=t.BS_K)
                P.op("vector", lambda e: e.reduce_max(
                    t.MX[:, :], t.SH[:, :].rearrange("p (l m) -> p l m", m=16), axis=AX.X),
                    r=t.BS_K, w=t.BS_K)
                P.op("vector", lambda e: e.tensor_tensor(
                    t.PH[:, :].rearrange("p (l m) -> p l m", m=16),
                    t.SH[:, :].rearrange("p (l m) -> p l m", m=16),
                    t.MX[:, :].unsqueeze(2).broadcast_to([128, 16, 16]), op=ALU.subtract),
                    r=t.BS_K, w=t.BS_K)
                P.op("scalar", lambda e: e.activation(t.PH[:, :], t.PH[:, :], AF.Exp),
                     r=t.BS_K, w=t.BS_K)
                P.op("vector", lambda e: e.reduce_sum(
                    t.SM[:, :], t.PH[:, :].rearrange("p (l m) -> p l m", m=16), axis=AX.X),
                    r=t.BS_K, w=t.BS_K)
                P.op("vector", lambda e: e.reciprocal(t.SM[:, :], t.SM[:, :]), r=t.BS_K, w=t.BS_K)
                pn = t.PN[hl]
                P.op("vector", lambda e, pn=pn: e.tensor_tensor(
                    pn[:, :].rearrange("p (l m) -> p l m", m=16),
                    t.PH[:, :].rearrange("p (l m) -> p l m", m=16),
                    t.SM[:, :].unsqueeze(2).broadcast_to([128, 16, 16]), op=ALU.mult),
                    r=t.BS_K, w=t.BS_K)
                b0_tick(1)
            for l in range(16):
                for hl in range(4):
                    pn = t.PN[hl]
                    for m in range(l + 1):
                        vc = 16384 + m * 512 + hl * 128
                        if m == 0:
                            P.op("vector", lambda e, vc=vc, hl=hl, pn=pn, l=l: e.tensor_scalar(
                                t.OB[:, hl * 128:(hl + 1) * 128], G2[:, vc:vc + 128],
                                pn[:, l * 16: l * 16 + 1], None, op0=ALU.mult),
                                r=vkeys + t.BS_K, w=t.BS_K)
                        else:
                            P.op("vector", lambda e, vc=vc, hl=hl, pn=pn, l=l, m=m:
                                 e.scalar_tensor_tensor(
                                     t.OB[:, hl * 128:(hl + 1) * 128], G2[:, vc:vc + 128],
                                     pn[:, l * 16 + m: l * 16 + m + 1],
                                     t.OB[:, hl * 128:(hl + 1) * 128],
                                     op0=ALU.mult, op1=ALU.add),
                                 r=vkeys + t.BS_K, w=t.BS_K)
                P.op("vector", lambda e: e.tensor_copy(t.OBB[:, :], t.OB[:, :]),
                     r=t.BS_K, w=t.BS_K)
                P.op("gpsimd", lambda e, ncx=ncx, l=l, hg=hg: e.dma_start(
                    out=y1_d[ncx, l * 128:(l + 1) * 128, hg * 512:(hg + 1) * 512],
                    in_=t.OBB[:, :]),
                    r=t.BS_K, w=[f"y1.{ncx}.{l}.{hg}"], dma=True)
                b0_tick(1)
        y1keys = [f"y1.{ncx}.{l}.{hg}" for l in range(16) for hg in range(4)]
        for ec in range(FC):
            P.op("scalar", lambda e, ec=ec, ncx=ncx: e.dma_start(
                out=G1[:, ec * 2048:(ec + 1) * 2048],
                in_=y1_d[ncx][:, ec * 128:(ec + 1) * 128], transpose=True),
                r=y1keys, w=[f"G1#{ec}"], dma=True)
        for cg in range(4):
            for ec in range(FC):
                c0 = 24576 + ec * 512
                P.op("scalar", lambda e, cg=cg, ec=ec, c0=c0: e.dma_start(
                    out=G2[:, c0:c0 + 512],
                    in_=wout_bf[cg * 512:(cg + 1) * 512, ec * 128:(ec + 1) * 128],
                    transpose=True),
                    r=[f"woutbf.{cg}"], w=keys2k("G2", c0, c0 + 512), dma=True)
            for l in range(16):
                grp, gk = t.rot()
                for ec in range(FC):
                    P.op("tensor", lambda e, grp=grp, ec=ec, l=l: e.matmul(
                        grp[:, :], G1[:, ec * 2048 + l * 128: ec * 2048 + (l + 1) * 128],
                        G2[:, 24576 + ec * 512: 24576 + (ec + 1) * 512],
                        start=(ec == 0), stop=(ec == FC - 1)),
                        r=[f"G1#{ec}"] + keys2k("G2", 24576 + ec * 512, 25088 + ec * 512),
                        w=[gk])
                P.op("scalar", lambda e, grp=grp: e.activation(t.S1T[:, :], grp[:, :], AF.Copy),
                     r=[gk], w=t.BS_K)
                P.op("gpsimd", lambda e, ncx=ncx, l=l, cg=cg: e.dma_start(
                    out=s1_d[ncx, l * 128:(l + 1) * 128, cg * 512:(cg + 1) * 512],
                    in_=t.S1T[:, :]),
                    r=t.BS_K, w=[f"s1.{ncx}.{l}.{cg}"], dma=True)

    b0_flush()
    for rg in range(4):
        l0 = rg * 4
        xnd_rg = [f"xnd.{ncx}.{l}" for ncx in range(2) for l in range(l0, l0 + 4)]
        for ec in range(FC):
            for ncx in range(2):
                c0 = 12288 + ec * 1024 + ncx * 512
                P.op("scalar", lambda e, ec=ec, ncx=ncx, l0=l0, c0=c0: e.dma_start(
                    out=G2[:, c0:c0 + 512],
                    in_=xn_d[ncx, l0 * 128:(l0 + 4) * 128, ec * 128:(ec + 1) * 128],
                    transpose=True),
                    r=xnd_rg, w=keys2k("G2", c0, c0 + 512), dma=True)
        for hc in range(HC):
            for (qtd, qk, c0) in ((q1t_d, q1t_all, 28672), (q2t_d, q2t_all, 30720)):
                P.op("sync", lambda e, qtd=qtd, hc=hc, c0=c0: e.dma_start(
                    out=G2[:, c0:c0 + 2048].rearrange("p (ec d) -> p ec d", d=128),
                    in_=qtd[:, hc * 128:(hc + 1) * 128].rearrange("(ec e) d -> e ec d", e=128)),
                    r=qk, w=keys2k("G2", c0, c0 + 2048), dma=True)
            pa = hc % 2
            gsets = ((t.ps[pa * 2], f"PS.{pa * 2}"), (t.ps[pa * 2 + 1], f"PS.{pa * 2 + 1}"))
            gsets2 = ((t.ps[4 + pa], f"PS.{4 + pa}"), (t.ps[6], "PS.6"))
            for (wc0, gs) in ((28672, gsets), (30720, gsets2)):
                for half, (gg, kk) in enumerate(gs):
                    for ec in range(FC):
                        P.op("tensor", lambda e, gg=gg, wc0=wc0, ec=ec, half=half: e.matmul(
                            gg[:, :], G2[:, wc0 + ec * 128: wc0 + (ec + 1) * 128],
                            G2[:, 12288 + ec * 1024 + half * 512: 12800 + ec * 1024 + half * 512],
                            start=(ec == 0), stop=(ec == FC - 1)),
                            r=keys2k("G2", wc0, wc0 + 2048) +
                              keys2k("G2", 12288 + ec * 1024, 12288 + (ec + 1) * 1024),
                            w=[kk])
            for half, (gg, kk) in enumerate(gsets):
                P.op("scalar", lambda e, gg=gg, half=half: e.activation(
                    t.SGM[:, half * 512:(half + 1) * 512], gg[:, :], AF.Sigmoid),
                    r=[kk], w=t.SGM_K)
            hcol = hc * 1024
            hidt = G1 if hcol + 1024 <= 32768 else G2
            hbase = hcol if hcol + 1024 <= 32768 else hcol - 32768
            hname = "G1" if hcol + 1024 <= 32768 else "G2"
            for half, (gg, kk) in enumerate(gsets2):
                P.op("vector", lambda e, hidt=hidt, hbase=hbase, half=half, gg=gg: e.tensor_tensor(
                    hidt[:, hbase + half * 512: hbase + (half + 1) * 512],
                    t.SGM[:, half * 512:(half + 1) * 512], gg[:, :], op=ALU.mult),
                    r=t.SGM_K + [kk], w=keys2k(hname, hbase + half * 512, hbase + (half + 1) * 512))
        for ec in range(FC):
            P.op("sync", lambda e, ec=ec: e.dma_start(
                out=G3[:, 0:FH].rearrange("p (hcc d) -> p hcc d", d=128),
                in_=q3t_d[:, ec * 128:(ec + 1) * 128].rearrange("(hcc h) d -> h hcc d", h=128)),
                r=q3t_all, w=keys2k("G3", 0, FH), dma=True)
            pa = ec % 2
            for half in range(2):
                gg = t.ps[pa * 2 + half]
                kk = f"PS.{pa * 2 + half}"
                for hc in range(HC):
                    hcol = hc * 1024
                    hidt = G1 if hcol + 1024 <= 32768 else G2
                    hbase = hcol if hcol + 1024 <= 32768 else hcol - 32768
                    hname = "G1" if hcol + 1024 <= 32768 else "G2"
                    P.op("tensor", lambda e, gg=gg, hc=hc, hidt=hidt, hbase=hbase, half=half:
                         e.matmul(gg[:, :], G3[:, hc * 128:(hc + 1) * 128],
                                  hidt[:, hbase + half * 512: hbase + half * 512 + 512],
                                  start=(hc == 0), stop=(hc == HC - 1)),
                         r=keys2k("G3", hc * 128, (hc + 1) * 128) +
                           keys2k(hname, hbase, hbase + 1024), w=[kk])
                P.op("scalar", lambda e, gg=gg, half=half: e.activation(
                    t.GFT[:, half * 512:(half + 1) * 512], gg[:, :], AF.Silu),
                    r=[kk], w=t.GFT_K)
            P.op("gpsimd", lambda e, ec=ec: e.dma_start(
                out=gfm_d[ec * 128:(ec + 1) * 128, :], in_=t.GFT[:, :]),
                r=t.GFT_K, w=[f"gfm.{ec}"], dma=True)
        gfm_all = [f"gfm.{ec}" for ec in range(FC)]
        for rt in range(8):
            ncx = rt // 4
            l = l0 + rt % 4
            pp = rt % 2
            xt = t.XT[pp]
            src_ap, src_keys = x1src(l, ncx)
            P.op("sync", lambda e, xt=xt, src_ap=src_ap: e.dma_start(out=xt[:, :], in_=src_ap),
                 r=src_keys, w=[f"xt{pp}"], dma=True)
            P.op("scalar", lambda e, rt=rt: e.dma_start(
                out=t.XNT[:, :], in_=gfm_d[:, rt * 128:(rt + 1) * 128], transpose=True),
                r=gfm_all, w=["XNT"], dma=True)
            P.op("sync", lambda e, ncx=ncx, l=l: e.dma_start(
                out=G3[:, 8192:8192 + 2048], in_=s1_d[ncx, l * 128:(l + 1) * 128, :]),
                r=[f"s1.{ncx}.{l}.{cg}" for cg in range(4)],
                w=keys2k("G3", 8192, 8192 + 2048), dma=True)
            for cg in range(4):
                P.op("vector", lambda e, cg=cg: e.tensor_tensor(
                    t.SCR[:, 0:512],
                    G3[:, 8192 + cg * 512:8192 + (cg + 1) * 512],
                    t.XNT[:, cg * 512:(cg + 1) * 512], op=ALU.mult),
                    r=keys2k("G3", 8192, 10240) + ["XNT"], w=["SCR"])
                P.op("vector", lambda e, xt=xt, cg=cg: e.tensor_tensor(
                    xt[:, cg * 512:(cg + 1) * 512], xt[:, cg * 512:(cg + 1) * 512],
                    t.SCR[:, 0:512], op=ALU.add),
                    r=[f"xt{pp}", "SCR"], w=[f"xt{pp}"])
            P.op("sync", lambda e, l=l, ncx=ncx, xt=xt: e.dma_start(
                out=out_h[l, ncx * 128:(ncx + 1) * 128, :], in_=xt[:, :]),
                r=[f"xt{pp}"], w=[f"out.{l}.{ncx}"], dma=True)


def a_scratch(nc):
    return dict(
        xn_d=nc.dram_tensor("xn_d", [T, C], BF16).ap(),
        bw_d=nc.dram_tensor("bw_d", [C, C], BF16).ap(),
        g_d=nc.dram_tensor("g_d", [T, C], BF16).ap(),
        v_d=nc.dram_tensor("v_d", [H, T, DH], BF16).ap(),
        y_d=nc.dram_tensor("y_d", [H, DH, T], BF16).ap(),
    )


def b_scratch(nc):
    return dict(
        wqkv_bf=nc.dram_tensor("wqkv_bf", [3 * C, C], BF16).ap(),
        wout_bf=nc.dram_tensor("wout_bf", [C, C], BF16).ap(),
        q1_d=nc.dram_tensor("q1_d", [FH, C], BF16).ap(),
        q2_d=nc.dram_tensor("q2_d", [FH, C], BF16).ap(),
        q3_d=nc.dram_tensor("q3_d", [C, FH], BF16).ap(),
        q1t_d=nc.dram_tensor("q1t_d", [C, FH], BF16).ap(),
        q2t_d=nc.dram_tensor("q2t_d", [C, FH], BF16).ap(),
        q3t_d=nc.dram_tensor("q3t_d", [FH, C], BF16).ap(),
        xn1_d=nc.dram_tensor("xn1_d", [2, 2048, C], BF16).ap(),
        y1_d=nc.dram_tensor("y1_d", [2, 2048, C], BF16).ap(),
        s1_d=nc.dram_tensor("s1_d", [2, 2048, C], BF16).ap(),
        gfm_d=nc.dram_tensor("gfm_d", [C, 1024], BF16).ap(),
    )


def a_inputs(nc):
    return dict(
        x=nc.dram_tensor("xa", [2, T, C], F32, kind="ExternalInput").ap(),
        wq=nc.dram_tensor("wq", [H, C, DH], F32, kind="ExternalInput").ap(),
        wk=nc.dram_tensor("wk", [H, C, DH], F32, kind="ExternalInput").ap(),
        wv=nc.dram_tensor("wv", [H, C, DH], F32, kind="ExternalInput").ap(),
        bq=nc.dram_tensor("bq", [H, DH], F32, kind="ExternalInput").ap(),
        bk=nc.dram_tensor("bk", [H, DH], F32, kind="ExternalInput").ap(),
        bv=nc.dram_tensor("bv", [H, DH], F32, kind="ExternalInput").ap(),
        wo=nc.dram_tensor("wo", [C, C], F32, kind="ExternalInput").ap(),
        bo=nc.dram_tensor("bo", [C], F32, kind="ExternalInput").ap(),
        wf=nc.dram_tensor("wf", [C, C], F32, kind="ExternalInput").ap(),
        bf=nc.dram_tensor("bf", [C], F32, kind="ExternalInput").ap(),
    )


def b_inputs(nc):
    return dict(
        wqkv=nc.dram_tensor("wqkv", [3 * C, C], F32, kind="ExternalInput").ap(),
        wout=nc.dram_tensor("wout", [C, C], F32, kind="ExternalInput").ap(),
        w1=nc.dram_tensor("w1", [FH, C], F32, kind="ExternalInput").ap(),
        w2=nc.dram_tensor("w2", [FH, C], F32, kind="ExternalInput").ap(),
        w3=nc.dram_tensor("w3", [C, FH], F32, kind="ExternalInput").ap(),
    )


def build_a_program():
    nc = bass.Bass("TRN2", target_bir_lowering=False)
    es = contextlib.ExitStack()
    t = alloc_all(nc, es)
    io = a_inputs(nc)
    io.update(a_scratch(nc))
    x1_h = nc.dram_tensor("x1out", [2, T, C], F32, kind="ExternalOutput").ap()
    io["x1dst"] = lambda b, i: (x1_h[b, i * 128:(i + 1) * 128, :], f"x1.{b}.{i}")
    P = Plan(nc)
    prologue_consts(P, t)
    build_stage_a(nc, P, t, io)
    with es:
        P.build(es)
    return nc


def build_b_program():
    nc = bass.Bass("TRN2", target_bir_lowering=False)
    es = contextlib.ExitStack()
    t = alloc_all(nc, es)
    io = b_inputs(nc)
    io.update(b_scratch(nc))
    x1_h = nc.dram_tensor("x1c", [B, NB, C], F32, kind="ExternalInput").ap()
    io["x1src"] = lambda l, ncx: (x1_h[l, ncx * 128:(ncx + 1) * 128, :], [])
    io["out"] = nc.dram_tensor("outc", [B, NB, C], F32, kind="ExternalOutput").ap()
    P = Plan(nc)
    prologue_consts(P, t)
    build_stage_b(nc, P, t, io)
    with es:
        P.build(es)
    return nc


def build_fused_program():
    nc = bass.Bass("TRN2", target_bir_lowering=False)
    es = contextlib.ExitStack()
    t = alloc_all(nc, es)
    io = a_inputs(nc)
    io.update(b_inputs(nc))
    io.update(a_scratch(nc))
    io.update(b_scratch(nc))
    x1s = [nc.dram_tensor(f"x1s{b}", [T, C], F32).ap() for b in range(2)]
    xall = [nc.dram_tensor(f"xall{b}", [NCORES * NB, C], F32).ap() for b in range(2)]
    io["x1dst"] = lambda b, i: (x1s[b][i * 128:(i + 1) * 128, :], f"x1.{b}.{i}")
    io["x1src"] = lambda l, ncx: (
        xall[l % 2][(l // 2) * NB + ncx * 128: (l // 2) * NB + (ncx + 1) * 128, :],
        [f"xall.{l % 2}"])
    io["out"] = nc.dram_tensor("outc", [B, NB, C], F32, kind="ExternalOutput").ap()

    P = Plan(nc)
    prologue_consts(P, t)
    build_stage_a(nc, P, t, io)
    for b in range(2):
        def cc(e, b=b):
            with allow_ici_all_to_all():
                return e.collective_compute(
                    "AllToAll", ALU.bypass,
                    replica_groups=[list(range(NCORES))],
                    ins=[x1s[b].opt()],
                    outs=[xall[b].opt()],
                )
        P.op("gpsimd", cc, r=[f"x1.{b}.{i}" for i in range(16)], w=[f"xall.{b}"], cc=True)
    P.barrier()
    build_stage_b(nc, P, t, io)
    with es:
        P.build(es)
    return nc


def kernel(L0_attn_x, L0_attn_Wq, L0_attn_bq, L0_attn_Wk, L0_attn_bk, L0_attn_Wv,
           L0_attn_bv, L0_attn_Wo, L0_attn_bo, L0_ffn_weight, L0_ffn_bias,
           L1_attn_weight_qkv, L1_attn_weight_out, L1_ffn_w1, L1_ffn_w2, L1_ffn_w3):
    f = lambda a: np.ascontiguousarray(np.asarray(a, np.float32))
    x = f(L0_attn_x)
    wA = {"wq": f(L0_attn_Wq), "wk": f(L0_attn_Wk), "wv": f(L0_attn_Wv),
          "bq": f(L0_attn_bq), "bk": f(L0_attn_bk), "bv": f(L0_attn_bv),
          "wo": f(L0_attn_Wo), "bo": f(L0_attn_bo),
          "wf": f(L0_ffn_weight), "bf": f(L0_ffn_bias)}
    wB = {"wqkv": f(L1_attn_weight_qkv), "wout": f(L1_attn_weight_out),
          "w1": f(L1_ffn_w1), "w2": f(L1_ffn_w2), "w3": f(L1_ffn_w3)}
    if FUSED:
        ncF = build_fused_program()
        in_maps = [dict(wA, **wB, xa=np.ascontiguousarray(x[2 * c:2 * c + 2]))
                   for c in range(NCORES)]
        res = run_bass_kernel_spmd(ncF, in_maps, core_ids=list(range(NCORES)))
        return np.concatenate([res.results[c]["outc"] for c in range(NCORES)], axis=1)

    ncA = build_a_program()
    in_maps = [dict(wA, xa=np.ascontiguousarray(x[2 * c:2 * c + 2])) for c in range(NCORES)]
    resA = run_bass_kernel_spmd(ncA, in_maps, core_ids=list(range(NCORES)))
    x1 = np.concatenate([resA.results[c]["x1out"] for c in range(NCORES)], axis=0)
    ncB = build_b_program()
    in_maps_b = [dict(wB, x1c=np.ascontiguousarray(x1[:, c * NB:(c + 1) * NB, :]))
                 for c in range(NCORES)]
    resB = run_bass_kernel_spmd(ncB, in_maps_b, core_ids=list(range(NCORES)))
    return np.concatenate([resB.results[c]["outc"] for c in range(NCORES)], axis=1)


# version 14
# speedup vs baseline: 1.1695x; 1.1503x over previous
import contextlib
import numpy as np
import concourse.bass as bass
import concourse.mybir as mybir
from concourse.bass_utils import run_bass_kernel_spmd
from concourse.aot_env import allow_ici_all_to_all

F32 = mybir.dt.float32
BF16 = mybir.dt.bfloat16
AF = mybir.ActivationFunctionType
ALU = mybir.AluOpType
AX = mybir.AxisListType

NCORES = 8
B, T, C = 16, 2048, 2048
H, DH = 16, 128
FC = C // 128
SCALE = float(DH) ** -0.5
NEG = -1.0e30
FH = 5632
HC = FH // 128
NB = 256

FUSED = True

ENGS = ("tensor", "vector", "scalar", "gpsimd", "sync")
DMA_POOL = 8


class _Op:
    __slots__ = ("eng", "fn", "reads", "writes", "dma", "idx", "need_inc", "token",
                 "deps", "sem", "selfwait", "barrier", "cc")


class Plan:

    def __init__(self, nc):
        self.nc = nc
        self.ops = []
        self.last_w = {}
        self.readers = {}
        self.dma_idx = {e: 0 for e in ENGS}
        self.cc_idx = 0

    def op(self, eng, fn, r=(), w=(), dma=False, cc=False):
        o = _Op()
        o.eng, o.fn, o.reads, o.writes, o.dma = eng, fn, list(r), list(w), dma
        o.cc = cc
        o.need_inc = False
        o.token = None
        o.sem = None
        o.selfwait = None
        o.barrier = False
        o.idx = len(self.ops)
        if cc:
            o.sem = ("cc", self.cc_idx)
            self.cc_idx += 1
        elif dma:
            o.sem = (eng, self.dma_idx[eng] % DMA_POOL)
            self.dma_idx[eng] += 1
        deps = {}

        def add_dep(p):
            if p is None or p is o:
                return
            if p.eng == o.eng and not p.dma and not o.dma and p.eng == "tensor":
                return
            deps[p.idx] = p

        for k in o.reads:
            add_dep(self.last_w.get(k))
        for k in o.writes:
            add_dep(self.last_w.get(k))
            for rd in self.readers.get(k, ()):
                add_dep(rd)
        best = {}
        for p in deps.values():
            sk = p.sem if (p.dma or p.cc) else p.eng
            q = best.get(sk)
            if q is None or p.idx > q.idx:
                best[sk] = p
        o.deps = list(best.values())
        for p in o.deps:
            p.need_inc = True
        for k in o.writes:
            self.last_w[k] = o
            self.readers[k] = []
        for k in o.reads:
            self.readers.setdefault(k, []).append(o)
        self.ops.append(o)
        return o

    def barrier(self):
        last_tensor = None
        for o in reversed(self.ops):
            if o.eng == "tensor" and not o.barrier:
                last_tensor = o
                break
        if last_tensor is not None:
            last_tensor.need_inc = True
        for e in ENGS:
            o = _Op()
            o.eng, o.fn, o.reads, o.writes, o.dma = e, (lambda en: en.nop()), [], [], False
            if e == "tensor":
                o.fn = None
            o.cc = False
            o.need_inc = e != "tensor"
            o.token = None
            o.sem = None
            o.selfwait = None
            o.barrier = True
            o.idx = len(self.ops)
            o.deps = []
            self.ops.append(o)
        self.last_w = {}
        self.readers = {}

    def build(self, es):
        nc = self.nc
        sems = {e: es.enter_context(nc.semaphore(f"sem_{e}")) for e in ENGS}
        dsems = {}
        for e in ENGS:
            n = min(self.dma_idx[e], DMA_POOL)
            for i in range(n):
                dsems[(e, i)] = es.enter_context(nc.semaphore(f"dsem_{e}{i}"))
        for i in range(self.cc_idx):
            dsems[("cc", i)] = es.enter_context(nc.semaphore(f"ccsem{i}"))
        counts = {e: 0 for e in ENGS}
        dcounts = {k: 0 for k in dsems}
        barrier_waits = {}
        pending_barrier = []
        for o in self.ops:
            if o.barrier:
                pending_barrier.append(o)
                if o.need_inc:
                    counts[o.eng] += 1
                o.token = counts[o.eng]
                if (o.idx + 1 >= len(self.ops) or not getattr(self.ops[o.idx + 1], "barrier", False)):
                    import os
                    snap = {}
                    for e2 in ENGS:
                        if counts[e2]:
                            snap[e2] = counts[e2]
                    if os.environ.get("BARRIER_NODMA", "0") != "1":
                        for k2, v2 in dcounts.items():
                            if v2:
                                snap[k2] = v2
                    for bo in pending_barrier:
                        barrier_waits[bo.idx] = snap
                    pending_barrier = []
                continue
            if o.cc:
                o.need_inc = True
                dcounts[o.sem] = 1
                o.token = 1
            elif o.dma:
                o.need_inc = True
                o.selfwait = dcounts[o.sem]
                dcounts[o.sem] += 16
                o.token = dcounts[o.sem]
            elif o.need_inc:
                counts[o.eng] += 1
                o.token = counts[o.eng]
        totals = dict(counts)
        dtotals = dict(dcounts)

        def semref_key(k):
            return dsems[k] if isinstance(k, tuple) else sems[k]

        def emit_for(eng_name, e):
            fl = {}
            for o in self.ops:
                if o.eng != eng_name:
                    continue
                if o.barrier:
                    snap = barrier_waits[o.idx]
                    if o.fn is not None:
                        ins = o.fn(e)
                        ins.then_inc(sems[o.eng], 1)
                    for k, v in snap.items():
                        if k == eng_name:
                            continue
                        if v > fl.get(k, 0):
                            e.wait_ge(semref_key(k), v)
                            fl[k] = v
                    continue
                if o.dma and o.selfwait > fl.get(o.sem, 0):
                    e.wait_ge(dsems[o.sem], o.selfwait)
                    fl[o.sem] = o.selfwait
                for p in o.deps:
                    k = p.sem if (p.dma or p.cc) else p.eng
                    if p.token > fl.get(k, 0):
                        e.wait_ge(semref_key(k), p.token)
                        fl[k] = p.token
                ins = o.fn(e)
                if o.need_inc:
                    if o.cc:
                        ins.then_inc(dsems[o.sem], 1)
                    else:
                        ins.then_inc(dsems[o.sem] if o.dma else sems[o.eng], 16 if o.dma else 1)
            if eng_name == "sync":
                for e2 in ENGS:
                    if totals[e2] > fl.get(e2, 0):
                        e.wait_ge(sems[e2], totals[e2])
                for k, v in dtotals.items():
                    if v > fl.get(k, 0):
                        e.wait_ge(dsems[k], v)

        with nc.Block() as block:
            for e in ENGS:
                getattr(block, e)((lambda en: (lambda eng_obj: emit_for(en, eng_obj)))(e))


def keys2k(name, c0, c1):
    return [f"{name}#{i}" for i in range(c0 // 2048, (c1 + 2047) // 2048)]


class NS:
    pass


def alloc_all(nc, es):
    t = NS()
    t.ps = [es.enter_context(nc.psum_tensor(f"ps{i}", [128, 512], F32)) for i in range(7)]
    t.psl = es.enter_context(nc.psum_tensor("psl", [1, 512], F32))
    t.ps_i = 0

    def rot():
        t.ps_i = (t.ps_i + 1) % 7
        return t.ps[t.ps_i], f"PS.{t.ps_i}"
    t.rot = rot
    sb = lambda n, sh, dt: es.enter_context(nc.sbuf_tensor(n, sh, dt))
    t.G1t = sb("G1", [128, 32768], BF16)
    t.G2t = sb("G2", [128, 32768], BF16)
    t.G3t = sb("G3", [128, 16384], BF16)
    t.XT = [sb(f"XT{i}", [128, C], F32) for i in range(2)]
    t.XNT = sb("XNT", [128, C], BF16)
    t.SCR = sb("SCR", [128, 512], F32)
    t.ones_r = sb("ones_r", [1, 512], BF16)
    t.ones_c = sb("ones_c", [128, 1], BF16)
    t.ones_f = sb("ones_f", [128, 1], F32)
    t.ones_fr = sb("ones_fr", [1, 128], F32)
    t.negm = [sb(f"negm{i}", [128, 1], F32) for i in range(2)]
    t.accq = [sb(f"accq{i}", [128, 1], F32) for i in range(2)]
    t.tmp1 = [sb(f"tmp1_{i}", [128, 1], F32) for i in range(2)]
    t.rstd = [sb(f"rstd{i}", [128, 1], F32) for i in range(2)]
    t.BQ = sb("BQ", [1, 2048], BF16)
    t.BK = sb("BK", [1, 2048], BF16)
    t.BV = sb("BV", [1, 2048], BF16)
    t.BO = sb("BO", [1, 2048], BF16)
    t.BF_ = sb("BF_", [1, 2048], BF16)
    
    t.MASKT = sb("MASKT", [128, 128], BF16)
    t.GRW = sb("GRW", [128, 1], F32)
    t.INV = sb("INV", [128, 1], F32)
    t.T11 = sb("T11", [1, 1], F32)

    t.G1 = t.G1t.ap()
    t.G2 = t.G2t.ap()
    t.G3 = t.G3t.ap()
    g2, g3 = t.G2, t.G3
    g3f = t.G3t.ap().bitcast(F32)
    t.XNFM = t.G1
    t.PTOFF = []
    off = 0
    for c in range(16):
        t.PTOFF.append(off)
        off += T - c * 128
    t.PT = g2[:, 0:17408]
    t.LNL = g2[0:1, 30720:32768]
    t.LNL_K = keys2k("G2", 30720, 32768)
    t.WQH = [g2[:, 18432:20480], g2[:, 20480:22528]]
    t.WKH = [g2[:, 22528:24576], g2[:, 24576:26624]]
    t.QFM = g2[:, 26624:28672]
    t.KFM = g2[:, 28672:30720]
    t.WQH_K = [keys2k("G2", 18432, 20480), keys2k("G2", 20480, 22528)]
    t.WKH_K = [keys2k("G2", 22528, 24576), keys2k("G2", 24576, 26624)]
    t.QFM_K = keys2k("G2", 26624, 28672)
    t.KFM_K = keys2k("G2", 28672, 30720)
    t.VH = g3[:, 0:2048]
    t.YH = g3[:, 2048:4096]
    t.RBT = g3[:, 4096:6144]
    t.GT = [g3[:, 6144:8192], g3[:, 8192:10240]]
    t.WVC = g3[:, 10240:12288]
    t.VT = g3[:, 12288:12800]
    t.VH_K, t.YH_K, t.RBT_K = ["G3#0"], ["G3#1"], ["G3#2"]
    t.GT_K = [["G3#3"], ["G3#4"]]
    t.WVC_K, t.VT_K = ["G3#5"], ["G3#6"]
    t.SH = g3f[:, 5120:5376]
    t.PH = g3f[:, 5376:5632]
    t.PN = [g3f[:, 5632 + i * 256: 5888 + i * 256] for i in range(4)]
    t.OB = g3f[:, 6656:7168]
    t.OBB = g3[:, 14336:14848]
    t.S1T = g3[:, 14848:15360]
    t.SCRB = g3[:, 15360:15488]
    t.MX = g3f[:, 7744:7760]
    t.SM = g3f[:, 7760:7776]
    t.BS_K = keys2k("G3", 10240, 15552)
    t.SGM = g3f[:, 2816:3840]
    t.SGM_K = keys2k("G3", 5632, 7680)
    t.GFT = g3[:, 10240:11264]
    t.GFT_K = keys2k("G3", 10240, 11264)
    return t


def prologue_consts(P, t):
    P.op("gpsimd", lambda e: e.memset(t.ones_r[:, :], 1.0), w=["ones_r"])
    P.op("gpsimd", lambda e: e.memset(t.ones_c[:, :], 1.0), w=["ones_c"])
    P.op("gpsimd", lambda e: e.memset(t.ones_f[:, :], 1.0), w=["ones_f"])
    P.op("gpsimd", lambda e: e.memset(t.ones_fr[:, :], 1.0), w=["ones_fr"])


def ln_tile(P, t, xt, xt_key, xnt, xnt_key, pp):
    negm, accq, tmp1, rstd = t.negm[pp], t.accq[pp], t.tmp1[pp], t.rstd[pp]
    kn, ka, kt, kr = f"negm{pp}", f"accq{pp}", f"tmp1{pp}", f"rstd{pp}"
    grp, gk = t.rot()
    P.op("vector", lambda e: e.reduce_sum(negm[:, :], xt[:, :], axis=AX.X, negate=True),
         r=[xt_key], w=[kn])
    P.op("vector", lambda e: e.tensor_scalar(negm[:, :], negm[:, :], 1.0 / C, None, op0=ALU.mult),
         r=[kn], w=[kn])
    P.op("scalar", lambda e: e.activation(grp[:, :], xt[:, 0:512], AF.Square, accum_out=accq[:, :]),
         r=[xt_key], w=[gk, ka])
    for j in range(1, 4):
        P.op("scalar", lambda e, j=j: e.activation(grp[:, :], xt[:, j * 512:(j + 1) * 512],
                                                   AF.Square, accum_out=tmp1[:, :]),
             r=[xt_key], w=[gk, kt])
        P.op("vector", lambda e: e.tensor_tensor(accq[:, :], accq[:, :], tmp1[:, :], op=ALU.add),
             r=[ka, kt], w=[ka])
    P.op("vector", lambda e: e.tensor_tensor(tmp1[:, :], negm[:, :], negm[:, :], op=ALU.mult),
         r=[kn], w=[kt])
    P.op("vector", lambda e: e.tensor_scalar(tmp1[:, :], tmp1[:, :], -1.0, 1e-5,
                                             op0=ALU.mult, op1=ALU.add),
         r=[kt], w=[kt])
    P.op("scalar", lambda e: e.activation(rstd[:, :], accq[:, :], AF.Sqrt, bias=tmp1[:, :],
                                          scale=1.0 / C),
         r=[ka, kt], w=[kr])
    P.op("vector", lambda e: e.reciprocal(rstd[:, :], rstd[:, :]), r=[kr], w=[kr])
    P.op("vector", lambda e: e.tensor_scalar(xnt[:, :], xt[:, :], negm[:, :], rstd[:, :],
                                             op0=ALU.add, op1=ALU.mult),
         r=[xt_key, kn, kr], w=[xnt_key])


def build_stage_a(nc, P, t, io):
    x_h, x1dst = io["x"], io["x1dst"]
    wq_h, wk_h, wv_h = io["wq"], io["wk"], io["wv"]
    bq_h, bk_h, bv_h, bo_h, bf_h = io["bq"], io["bk"], io["bv"], io["bo"], io["bf"]
    wo_h, wf_h = io["wo"], io["wf"]
    xn_d, bw_d, g_d, v_d, y_d = io["xn_d"], io["bw_d"], io["g_d"], io["v_d"], io["y_d"]

    for (row, src, nm) in ((t.BQ, bq_h, "BQ"), (t.BK, bk_h, "BK"), (t.BV, bv_h, "BV")):
        P.op("gpsimd", lambda e, row=row, src=src: e.dma_start(
            out=row[0:1, :], in_=src.rearrange("h d -> (h d)").unsqueeze(0)),
            w=[nm], dma=True)
    P.op("gpsimd", lambda e: e.dma_start(out=t.BO[0:1, :], in_=bo_h.unsqueeze(0)), w=["BO"], dma=True)
    P.op("gpsimd", lambda e: e.dma_start(out=t.BF_[0:1, :], in_=bf_h.unsqueeze(0)), w=["BF_"], dma=True)
    P.op("gpsimd", lambda e: e.memset(t.MASKT[:, :], 0.0), w=["MASKT"])
    P.op("gpsimd", lambda e: e.affine_select(out=t.MASKT[:, :], in_=t.MASKT[:, :],
                                             compare_op=ALU.is_ge, fill=NEG, base=0,
                                             pattern=[[1, 128]], channel_multiplier=-1),
         r=["MASKT"], w=["MASKT"])

    for oc in range(16):
        pp = oc % 2
        xt, gt, gtk = t.XT[pp], t.GT[pp], t.GT_K[pp]
        P.op("sync", lambda e, oc=oc, xt=xt: e.dma_start(
            out=xt[:, :], in_=wf_h[oc * 128:(oc + 1) * 128, :]), w=[f"xt{pp}"], dma=True)
        P.op("vector", lambda e, xt=xt, pp=pp: e.reduce_sum(
            t.tmp1[pp][:, :], xt[:, :], axis=AX.X, apply_absolute_value=True),
            r=[f"xt{pp}"], w=[f"tmp1{pp}"])
        P.op("vector", lambda e, pp=pp: e.tensor_scalar(
            t.tmp1[pp][:, :], t.tmp1[pp][:, :], 1.0 / C, None, op0=ALU.mult),
            r=[f"tmp1{pp}"], w=[f"tmp1{pp}"])
        P.op("scalar", lambda e, xt=xt: e.sign(t.XNT[:, :], xt[:, :]), r=[f"xt{pp}"], w=["XNT"])
        P.op("vector", lambda e, gt=gt, pp=pp: e.tensor_scalar(
            gt[:, :], t.XNT[:, :], t.tmp1[pp][:, :], None, op0=ALU.mult),
            r=["XNT", f"tmp1{pp}"], w=gtk)
        P.op("gpsimd", lambda e, gt=gt, oc=oc: e.dma_start(
            out=bw_d[oc * 128:(oc + 1) * 128, :], in_=gt[:, :]),
            r=gtk, w=[f"bwd.{oc}"], dma=True)
    bwd_all = [f"bwd.{oc}" for oc in range(16)]

    for b in range(2):
        for i in range(16):
            pp = i % 2
            xt = t.XT[pp]
            P.op("sync", lambda e, b=b, i=i, xt=xt: e.dma_start(
                out=xt[:, :], in_=x_h[b, i * 128:(i + 1) * 128, :]), w=[f"xt{pp}"], dma=True)
            ln_tile(P, t, xt, f"xt{pp}", t.XNT, "XNT", pp)
            P.op("gpsimd", lambda e, i=i: e.dma_start(
                out=xn_d[i * 128:(i + 1) * 128, :], in_=t.XNT[:, :]),
                r=["XNT"], w=[f"xnd.{i}"], dma=True)
        xnd_all = [f"xnd.{i}" for i in range(16)]
        for fc in range(FC):
            P.op("scalar", lambda e, fc=fc: e.dma_start(
                out=t.XNFM[:, fc * 2048:(fc + 1) * 2048],
                in_=xn_d[:, fc * 128:(fc + 1) * 128], transpose=True),
                r=xnd_all, w=[f"G1#{fc}"], dma=True)

        for half in range(2):
            for ic in range(FC):
                c0 = ic * 1024
                P.op("scalar", lambda e, ic=ic, half=half, c0=c0: e.dma_start(
                    out=t.PT[:, c0:c0 + 1024],
                    in_=bw_d[half * 1024:(half + 1) * 1024, ic * 128:(ic + 1) * 128],
                    transpose=True),
                    r=bwd_all, w=keys2k("G2", c0, c0 + 1024), dma=True)
            for i in range(16):
                pp = i % 2
                gt, gtk = t.GT[pp], t.GT_K[pp]
                for cg in range(2):
                    grp, gk = t.rot()
                    ocol = half * 1024 + cg * 512
                    for ic in range(FC):
                        P.op("tensor", lambda e, grp=grp, ic=ic, i=i, cg=cg: e.matmul(
                            grp[:, :], t.XNFM[:, ic * 2048 + i * 128: ic * 2048 + (i + 1) * 128],
                            t.PT[:, ic * 1024 + cg * 512: ic * 1024 + (cg + 1) * 512],
                            start=(ic == 0), stop=False),
                            r=[f"G1#{ic}"] + keys2k("G2", ic * 1024, ic * 1024 + 1024), w=[gk])
                    P.op("tensor", lambda e, grp=grp, ocol=ocol: e.matmul(
                        grp[:, :], t.ones_r[0:1, 0:128], t.BF_[0:1, ocol:ocol + 512],
                        start=False, stop=True),
                        r=["ones_r", "BF_"], w=[gk])
                    P.op("scalar", lambda e, grp=grp, gt=gt, cg=cg: e.activation(
                        gt[:, cg * 512:(cg + 1) * 512], grp[:, :], AF.Silu),
                        r=[gk], w=gtk)
                P.op("gpsimd", lambda e, gt=gt, i=i, half=half: e.dma_start(
                    out=g_d[i * 128:(i + 1) * 128, half * 1024:(half + 1) * 1024],
                    in_=gt[:, 0:1024]),
                    r=gtk, w=[f"gd.{i}.{half}"], dma=True)

        for hgv in range(4):
            for ic in range(FC):
                c0 = ic * 512
                P.op("gpsimd", lambda e, hgv=hgv, ic=ic, c0=c0: e.dma_start(
                    out=t.PT[:, c0:c0 + 512],
                    in_=wv_h.rearrange("h i d -> i h d")[ic * 128:(ic + 1) * 128,
                                                         4 * hgv:4 * hgv + 4, :]),
                    w=keys2k("G2", c0, c0 + 512), dma=True)
            for i in range(16):
                grp, gk = t.rot()
                for ic in range(FC):
                    P.op("tensor", lambda e, grp=grp, ic=ic, i=i: e.matmul(
                        grp[:, :], t.XNFM[:, ic * 2048 + i * 128: ic * 2048 + (i + 1) * 128],
                        t.PT[:, ic * 512:(ic + 1) * 512], start=(ic == 0), stop=False),
                        r=[f"G1#{ic}"] + keys2k("G2", ic * 512, ic * 512 + 512), w=[gk])
                P.op("tensor", lambda e, grp=grp, hgv=hgv: e.matmul(
                    grp[:, :], t.ones_r[0:1, 0:128], t.BV[0:1, hgv * 512:(hgv + 1) * 512],
                    start=False, stop=True),
                    r=["ones_r", "BV"], w=[gk])
                P.op("scalar", lambda e, grp=grp: e.activation(t.VT[:, :], grp[:, :], AF.Copy),
                     r=[gk], w=t.VT_K)
                P.op("gpsimd", lambda e, i=i, hgv=hgv: e.dma_start(
                    out=v_d.rearrange("h t d -> t h d")[i * 128:(i + 1) * 128,
                                                        4 * hgv:4 * hgv + 4, :],
                    in_=t.VT[:, :]),
                    r=t.VT_K, w=[f"vd.{hgv}.{i}"], dma=True)

        for h in range(H):
            pp = h % 2
            wq, wk = t.WQH[pp], t.WKH[pp]
            wqk, wkk = t.WQH_K[pp], t.WKH_K[pp]
            P.op("gpsimd", lambda e, h=h, wq=wq: e.dma_start(
                out=wq[:, :], in_=wq_h[h].rearrange("(fc f) d -> f fc d", f=128)),
                w=wqk, dma=True)
            P.op("gpsimd", lambda e, h=h, wk=wk: e.dma_start(
                out=wk[:, :], in_=wk_h[h].rearrange("(fc f) d -> f fc d", f=128)),
                w=wkk, dma=True)
            P.op("sync", lambda e, h=h: e.dma_start(
                out=t.VH[:, :].rearrange("t (c d) -> t c d", d=128),
                in_=v_d[h].rearrange("(c t) d -> t c d", t=128)),
                r=[f"vd.{h // 4}.{i}" for i in range(16)], w=t.VH_K, dma=True)
            for dst, dkey, wgt, wkey, brow, bkey, scl in (
                    (t.QFM, t.QFM_K, wq, wqk, t.BQ, "BQ", SCALE),
                    (t.KFM, t.KFM_K, wk, wkk, t.BK, "BK", None)):
                for tg in range(4):
                    grp, gk = t.rot()
                    for fc in range(FC):
                        P.op("tensor", lambda e, grp=grp, wgt=wgt, fc=fc, tg=tg: e.matmul(
                            grp[:, :], wgt[:, fc * 128:(fc + 1) * 128],
                            t.XNFM[:, fc * 2048 + tg * 512: fc * 2048 + (tg + 1) * 512],
                            start=(fc == 0), stop=False),
                            r=wkey + [f"G1#{fc}"], w=[gk])
                    P.op("tensor", lambda e, grp=grp, brow=brow, h=h: e.matmul(
                        grp[:, :], brow[0:1, h * 128:(h + 1) * 128], t.ones_r[0:1, 0:512],
                        start=False, stop=True),
                        r=[bkey, "ones_r"], w=[gk])
                    if scl is not None:
                        P.op("scalar", lambda e, grp=grp, dst=dst, tg=tg, scl=scl: e.activation(
                            dst[:, tg * 512:(tg + 1) * 512], grp[:, :], AF.Copy, scale=scl),
                            r=[gk], w=dkey)
                    else:
                        P.op("scalar", lambda e, grp=grp, dst=dst, tg=tg: e.activation(
                            dst[:, tg * 512:(tg + 1) * 512], grp[:, :], AF.Copy),
                            r=[gk], w=dkey)
            for c in range(16):
                Lc = T - c * 128
                for g in range((Lc + 511) // 512):
                    N = min(512, Lc - g * 512)
                    grp, gk = t.rot()
                    P.op("tensor", lambda e, grp=grp, c=c, g=g, N=N: e.matmul(
                        grp[:, 0:N], t.KFM[:, c * 128:(c + 1) * 128],
                        t.QFM[:, c * 128 + g * 512: c * 128 + g * 512 + N],
                        start=True, stop=True),
                        r=t.KFM_K + t.QFM_K, w=[gk])
                    if g == 0:
                        P.op("vector", lambda e, grp=grp: e.tensor_tensor(
                            grp[:, 0:128], grp[:, 0:128], t.MASKT[:, :], op=ALU.add),
                            r=[gk, "MASKT"], w=[gk])
                    p0 = t.PTOFF[c] + g * 512
                    P.op("scalar", lambda e, grp=grp, p0=p0, N=N: e.activation(
                        t.PT[:, p0:p0 + N], grp[:, 0:N], AF.Exp),
                        r=[gk], w=keys2k("G2", p0, p0 + N))
            for tg in range(4):
                t0 = tg * 512
                cmax = min(15, (t0 + 511) // 128)
                for c in range(cmax + 1):
                    lo = max(t0, c * 128)
                    N = t0 + 512 - lo
                    src = t.PTOFF[c] + lo - c * 128
                    P.op("tensor", lambda e, lo=lo, t0=t0, N=N, src=src, c=c, cmax=cmax: e.matmul(
                        t.psl[0:1, lo - t0:lo - t0 + N], t.ones_c[:, :], t.PT[:, src:src + N],
                        start=(c == 0), stop=(c == cmax)),
                        r=["ones_c"] + keys2k("G2", src, src + N), w=["PSL"])
                P.op("scalar", lambda e, tg=tg: e.activation(
                    t.LNL[0:1, tg * 512:(tg + 1) * 512], t.psl[0:1, :], AF.Ln),
                    r=["PSL"], w=t.LNL_K)
            for c in range(16):
                for tg in range((c * 128) // 512, 4):
                    t0 = tg * 512
                    lo = max(t0, c * 128)
                    N = t0 + 512 - lo
                    src = t.PTOFF[c] + lo - c * 128
                    cmax = tg * 4 + 3
                    P.op("tensor", lambda e, c=c, tg=tg, lo=lo, t0=t0, N=N, src=src, cmax=cmax:
                         e.matmul(t.ps[tg][:, lo - t0:lo - t0 + N], t.VH[:, c * 128:(c + 1) * 128],
                                  t.PT[:, src:src + N], start=(c == 0), stop=(c == cmax)),
                         r=t.VH_K + keys2k("G2", src, src + N), w=[f"PS.{tg}"])
            for tg in range(4):
                gi = 4 + (tg % 3)
                grp = t.ps[gi]
                gk = f"PS.{gi}"
                P.op("tensor", lambda e, grp=grp, tg=tg: e.matmul(
                    grp[:, :], t.ones_r[0:1, 0:128], t.LNL[0:1, tg * 512:(tg + 1) * 512],
                    start=True, stop=True),
                    r=["ones_r"] + t.LNL_K, w=[gk])
                P.op("scalar", lambda e, grp=grp, tg=tg: e.activation(
                    t.RBT[:, tg * 512:(tg + 1) * 512], grp[:, :], AF.Exp, scale=-1.0),
                    r=[gk], w=t.RBT_K)
                P.op("vector", lambda e, tg=tg: e.tensor_tensor(
                    t.YH[:, tg * 512:(tg + 1) * 512], t.ps[tg][:, :],
                    t.RBT[:, tg * 512:(tg + 1) * 512], op=ALU.mult),
                    r=[f"PS.{tg}"] + t.RBT_K, w=t.YH_K)
            P.op("gpsimd", lambda e, h=h: e.dma_start(out=y_d[h], in_=t.YH[:, :]),
                 r=t.YH_K, w=[f"yd.{h}"], dma=True)

        for fc in range(FC):
            P.op("gpsimd", lambda e, fc=fc: e.dma_start(
                out=t.XNFM[:, fc * 2048:(fc + 1) * 2048], in_=wo_h[fc * 128:(fc + 1) * 128, :]),
                w=[f"G1#{fc}"], dma=True)
        yd_all = [f"yd.{h}" for h in range(H)]
        for i in range(16):
            pp = i % 2
            xt = t.XT[pp]
            gt, gtk = t.GT[pp], t.GT_K[pp]
            yt = t.XNT if pp == 0 else t.WVC
            ytk = ["XNT"] if pp == 0 else t.WVC_K
            P.op("sync", lambda e, b=b, i=i, xt=xt: e.dma_start(
                out=xt[:, :], in_=x_h[b, i * 128:(i + 1) * 128, :]), w=[f"xt{pp}"], dma=True)
            P.op("sync", lambda e, i=i, yt=yt: e.dma_start(
                out=yt[:, :].rearrange("d (h t) -> d h t", h=H),
                in_=y_d.rearrange("h d t -> d h t")[:, :, i * 128:(i + 1) * 128]),
                r=yd_all, w=ytk, dma=True)
            P.op("gpsimd", lambda e, i=i, gt=gt: e.dma_start(
                out=gt[:, :], in_=g_d[i * 128:(i + 1) * 128, :]),
                r=[f"gd.{i}.0", f"gd.{i}.1"], w=gtk, dma=True)
            for cg in range(4):
                grp, gk = t.rot()
                for fc in range(FC):
                    P.op("tensor", lambda e, grp=grp, yt=yt, fc=fc, cg=cg: e.matmul(
                        grp[:, :], yt[:, fc * 128:(fc + 1) * 128],
                        t.XNFM[:, fc * 2048 + cg * 512: fc * 2048 + (cg + 1) * 512],
                        start=(fc == 0), stop=False),
                        r=ytk + [f"G1#{fc}"], w=[gk])
                P.op("tensor", lambda e, grp=grp, cg=cg: e.matmul(
                    grp[:, :], t.ones_r[0:1, 0:128], t.BO[0:1, cg * 512:(cg + 1) * 512],
                    start=False, stop=True),
                    r=["ones_r", "BO"], w=[gk])
                P.op("vector", lambda e, grp=grp, gt=gt, cg=cg: e.scalar_tensor_tensor(
                    t.SCR[:, 0:512], grp[:, :], 1.0, gt[:, cg * 512:(cg + 1) * 512],
                    op0=ALU.mult, op1=ALU.mult),
                    r=[gk] + gtk, w=["SCR"])
                P.op("vector", lambda e, xt=xt, cg=cg: e.tensor_tensor(
                    xt[:, cg * 512:(cg + 1) * 512], xt[:, cg * 512:(cg + 1) * 512],
                    t.SCR[:, 0:512], op=ALU.add),
                    r=[f"xt{pp}", "SCR"], w=[f"xt{pp}"])
            dst_ap, dst_key = x1dst(b, i)
            P.op("sync", lambda e, xt=xt, dst_ap=dst_ap: e.dma_start(out=dst_ap, in_=xt[:, :]),
                 r=[f"xt{pp}"], w=[dst_key], dma=True)


def build_stage_b(nc, P, t, io):
    x1src = io["x1src"]
    wqkv_h, wout_h = io["wqkv"], io["wout"]
    w1_h, w2_h, w3_h = io["w1"], io["w2"], io["w3"]
    out_h = io["out"]
    wqkv_bf, wout_bf = io["wqkv_bf"], io["wout_bf"]
    q1_d, q2_d, q3_d = io["q1_d"], io["q2_d"], io["q3_d"]
    q1t_d, q2t_d, q3t_d = io["q1t_d"], io["q2t_d"], io["q3t_d"]
    xn_d, y1_d, s1_d, gfm_d = io["xn1_d"], io["y1_d"], io["s1_d"], io["gfm_d"]
    G1, G2, G3 = t.G1, t.G2, t.G3

    for j in range(12):
        P.op("gpsimd", lambda e, j=j: e.dma_start(
            out=wqkv_bf[j * 512:(j + 1) * 512, :], in_=wqkv_h[j * 512:(j + 1) * 512, :]),
            w=[f"wqkvbf.{j}"], dma=True)
    for j in range(4):
        P.op("gpsimd", lambda e, j=j: e.dma_start(
            out=wout_bf[j * 512:(j + 1) * 512, :], in_=wout_h[j * 512:(j + 1) * 512, :]),
            w=[f"woutbf.{j}"], dma=True)

    b0_steps = []

    def quantize(w_hbm, q_d, rows, cols, tagw):
        nchunk = rows // 128
        nparts = (cols + 2047) // 2048
        b0_steps.append((None, lambda: P.op("gpsimd", lambda e: e.memset(t.GRW[:, :], 0.0), w=["GRW"])))

        def gamma_load(ch, part):
            pp = (ch * nparts + part) % 2
            xt = t.XT[pp]
            c0, c1 = part * 2048, min(cols, (part + 1) * 2048)
            P.op("sync", lambda e, ch=ch, xt=xt, c0=c0, c1=c1: e.dma_start(
                out=xt[:, 0:c1 - c0], in_=w_hbm[ch * 128:(ch + 1) * 128, c0:c1]),
                w=[f"xt{pp}"], dma=True)

        def gamma_comp(ch, part):
            pp = (ch * nparts + part) % 2
            xt = t.XT[pp]
            c0, c1 = part * 2048, min(cols, (part + 1) * 2048)
            P.op("vector", lambda e, xt=xt, pp=pp, n=c1 - c0: e.reduce_sum(
                t.tmp1[pp][:, :], xt[:, 0:n], axis=AX.X, apply_absolute_value=True),
                r=[f"xt{pp}"], w=[f"tmp1{pp}"])
            P.op("gpsimd", lambda e, pp=pp: e.tensor_tensor(
                t.GRW[:, :], t.GRW[:, :], t.tmp1[pp][:, :], op=ALU.add),
                r=["GRW", f"tmp1{pp}"], w=["GRW"])
        for ch in range(nchunk):
            for part in range(nparts):
                b0_steps.append((lambda ch=ch, part=part: gamma_load(ch, part),
                                 lambda ch=ch, part=part: gamma_comp(ch, part)))

        def finalize():
            P.op("tensor", lambda e: e.matmul(t.psl[0:1, 0:1], t.GRW[:, :], t.ones_f[:, :],
                                              start=True, stop=True),
                 r=["GRW", "ones_f"], w=["PSL"])
            P.op("vector", lambda e: e.tensor_scalar(
                t.T11[:, :], t.psl[0:1, 0:1], 0.5 / (rows * cols), 0.5e-8,
                op0=ALU.mult, op1=ALU.add),
                r=["PSL"], w=["T11"])
            grp, gk = t.rot()
            P.op("tensor", lambda e, grp=grp: e.matmul(grp[:, 0:1], t.ones_fr[:, :], t.T11[:, :],
                                                       start=True, stop=True),
                 r=["ones_fr", "T11"], w=[gk])
            P.op("vector", lambda e, grp=grp: e.tensor_copy(t.INV[:, :], grp[:, 0:1]),
                 r=[gk], w=["INV"])
            P.op("vector", lambda e: e.tensor_scalar(t.GRW[:, :], t.INV[:, :], -1.0, None,
                                                     op0=ALU.mult),
                 r=["INV"], w=["GRW"])
        b0_steps.append((None, finalize))

        def quant_load(ch, part):
            pp = (ch * nparts + part) % 2
            xt = t.XT[pp]
            c0, c1 = part * 2048, min(cols, (part + 1) * 2048)
            P.op("sync", lambda e, ch=ch, xt=xt, c0=c0, c1=c1: e.dma_start(
                out=xt[:, 0:c1 - c0], in_=w_hbm[ch * 128:(ch + 1) * 128, c0:c1]),
                w=[f"xt{pp}"], dma=True)

        def quant_comp(ch, part):
            pp = (ch * nparts + part) % 2
            xt = t.XT[pp]
            c0, c1 = part * 2048, min(cols, (part + 1) * 2048)
            n = c1 - c0
            P.op("vector", lambda e, xt=xt, n=n: e.tensor_scalar(
                t.XNT[:, 0:n], xt[:, 0:n], t.INV[:, :], None, op0=ALU.is_gt),
                r=[f"xt{pp}", "INV"], w=["XNT"])
            P.op("vector", lambda e, xt=xt, n=n: e.tensor_scalar(
                t.GT[0][:, 0:n], xt[:, 0:n], t.GRW[:, :], None, op0=ALU.is_lt),
                r=[f"xt{pp}", "GRW"], w=t.GT_K[0])
            P.op("gpsimd", lambda e, n=n: e.tensor_tensor(
                t.XNT[:, 0:n], t.XNT[:, 0:n], t.GT[0][:, 0:n], op=ALU.subtract),
                r=["XNT"] + t.GT_K[0], w=["XNT"])
            P.op("gpsimd", lambda e, ch=ch, c0=c0, c1=c1: e.dma_start(
                out=q_d[ch * 128:(ch + 1) * 128, c0:c1], in_=t.XNT[:, 0:c1 - c0]),
                r=["XNT"], w=[f"{tagw}.{ch}.{c0}"], dma=True)
        for ch in range(nchunk):
            for part in range(nparts):
                b0_steps.append((lambda ch=ch, part=part: quant_load(ch, part),
                                 lambda ch=ch, part=part: quant_comp(ch, part)))

    q1keys = [f"q1.{ch}.0" for ch in range(HC)]
    q2keys = [f"q2.{ch}.0" for ch in range(HC)]
    q3keys = [f"q3.{ch}.{c0}" for ch in range(FC) for c0 in range(0, FH, 2048)]

    def emit_b0():
        def interleave(into, extra):
            base = len(into)
            pass

        quantize(w1_h, q1_d, FH, C, "q1")
        n1 = len(b0_steps)
        quantize(w2_h, q2_d, FH, C, "q2")
        n2 = len(b0_steps)
        quantize(w3_h, q3_d, C, FH, "q3")
        n3 = len(b0_steps)

        def ptq_step(qd, qtd, rk, tag, ec):
            P.op("scalar", lambda e, qd=qd, ec=ec: e.dma_start(
                out=G3[:, 0:FH], in_=qd[:, ec * 128:(ec + 1) * 128], transpose=True),
                r=rk, w=keys2k("G3", 0, FH), dma=True)
            P.op("sync", lambda e, qtd=qtd, ec=ec: e.dma_start(
                out=qtd[ec * 128:(ec + 1) * 128, :], in_=G3[:, 0:FH]),
                r=keys2k("G3", 0, FH), w=[f"{tag}.{ec}"], dma=True)
        ptq1 = [(None, lambda a=q1_d, b2=q1t_d, c=q1keys, d="q1t", ec=ec:
                 ptq_step(a, b2, c, d, ec)) for ec in range(FC)]
        ptq2 = [(None, lambda a=q2_d, b2=q2t_d, c=q2keys, d="q2t", ec=ec:
                 ptq_step(a, b2, c, d, ec)) for ec in range(FC)]

        def pt3_step(hc):
            P.op("scalar", lambda e, hc=hc: e.dma_start(
                out=G3[:, 0:C], in_=q3_d[:, hc * 128:(hc + 1) * 128], transpose=True),
                r=q3keys, w=keys2k("G3", 0, C), dma=True)
            P.op("sync", lambda e, hc=hc: e.dma_start(
                out=q3t_d[hc * 128:(hc + 1) * 128, :], in_=G3[:, 0:C]),
                r=keys2k("G3", 0, C), w=[f"q3t.{hc}"], dma=True)
        ptq3 = [(None, lambda hc=hc: pt3_step(hc)) for hc in range(HC)]
        def weave(lo, hi, extra):
            seg = b0_steps[lo:hi]
            out = []
            k = max(1, len(seg) // max(1, len(extra)))
            ei = 0
            for i, s in enumerate(seg):
                out.append(s)
                if i % k == k - 1 and ei < len(extra):
                    out.append(extra[ei])
                    ei += 1
            out.extend(extra[ei:])
            b0_steps[lo:hi] = out
        weave(n2, n3, ptq2)
        weave(n1, n2, ptq1)
        b0_steps.extend(ptq3)

    b0_ix = {"load": 0, "comp": 0}

    def b0_tick(n=1):
        for _ in range(n):
            import os
            LA = int(os.environ.get("B0_LA", "2"))
            while b0_ix["load"] < min(len(b0_steps), b0_ix["comp"] + LA):
                fn = b0_steps[b0_ix["load"]][0]
                if fn is not None:
                    fn()
                b0_ix["load"] += 1
            if b0_ix["comp"] < len(b0_steps):
                b0_steps[b0_ix["comp"]][1]()
                b0_ix["comp"] += 1

    def b0_flush():
        while b0_ix["comp"] < len(b0_steps):
            b0_tick(1)

    q1t_all = [f"q1t.{ec}" for ec in range(FC)]
    q2t_all = [f"q2t.{ec}" for ec in range(FC)]
    q3t_all = [f"q3t.{hc}" for hc in range(HC)]

    for ncx in range(2):
        for l in range(16):
            pp = l % 2
            xt = t.XT[pp]
            src_ap, src_keys = x1src(l, ncx)
            P.op("sync", lambda e, xt=xt, src_ap=src_ap: e.dma_start(out=xt[:, :], in_=src_ap),
                 r=src_keys, w=[f"xt{pp}"], dma=True)
            ln_tile(P, t, xt, f"xt{pp}", t.XNT, "XNT", pp)
            P.op("gpsimd", lambda e, l=l, ncx=ncx: e.dma_start(
                out=xn_d[ncx, l * 128:(l + 1) * 128, :], in_=t.XNT[:, :]),
                r=["XNT"], w=[f"xnd.{ncx}.{l}"], dma=True)
    emit_b0()
    import os as _os
    if _os.environ.get("B0_INLINE", "1") == "1":
        b0_flush()

    for ncx in range(2):
        xnd_nc = [f"xnd.{ncx}.{l}" for l in range(16)]
        for ec in range(FC):
            P.op("scalar", lambda e, ec=ec, ncx=ncx: e.dma_start(
                out=G1[:, ec * 2048:(ec + 1) * 2048],
                in_=xn_d[ncx][:, ec * 128:(ec + 1) * 128], transpose=True),
                r=xnd_nc, w=[f"G1#{ec}"], dma=True)
        for hg in range(4):
            bases = [hg * 512, 2048 + hg * 512, 4096 + hg * 512]
            for part, base in enumerate(bases):
                for ec in range(FC):
                    c0 = 24576 + ec * 512
                    P.op("scalar", lambda e, base=base, ec=ec, c0=c0: e.dma_start(
                        out=G2[:, c0:c0 + 512],
                        in_=wqkv_bf[base:base + 512, ec * 128:(ec + 1) * 128], transpose=True),
                        r=[f"wqkvbf.{base // 512}"], w=keys2k("G2", c0, c0 + 512), dma=True)
                for l in range(16):
                    grp, gk = t.rot()
                    for ec in range(FC):
                        P.op("tensor", lambda e, grp=grp, ec=ec, l=l: e.matmul(
                            grp[:, :], G1[:, ec * 2048 + l * 128: ec * 2048 + (l + 1) * 128],
                            G2[:, 24576 + ec * 512: 24576 + (ec + 1) * 512],
                            start=(ec == 0), stop=(ec == FC - 1)),
                            r=[f"G1#{ec}"] + keys2k("G2", 24576 + ec * 512, 25088 + ec * 512),
                            w=[gk])
                    d0 = part * 8192 + l * 512
                    P.op("scalar", lambda e, grp=grp, d0=d0: e.activation(
                        G2[:, d0:d0 + 512], grp[:, :], AF.Copy),
                        r=[gk], w=keys2k("G2", d0, d0 + 512))
                    b0_tick(1)
            qkkeys = keys2k("G2", 0, 16384)
            vkeys = keys2k("G2", 16384, 24576)
            for hl in range(4):
                P.op("gpsimd", lambda e: e.memset(t.SH[:, :], NEG), w=t.BS_K)
                for l in range(16):
                    qc = l * 512 + hl * 128
                    for m in range(l + 1):
                        kc = 8192 + m * 512 + hl * 128
                        P.op("vector", lambda e, qc=qc, kc=kc, l=l, m=m: e.scalar_tensor_tensor(
                            t.SCRB[:, :], G2[:, qc:qc + 128], SCALE, G2[:, kc:kc + 128],
                            op0=ALU.mult, op1=ALU.mult,
                            accum_out=t.SH[:, l * 16 + m: l * 16 + m + 1]),
                            r=qkkeys + t.BS_K, w=t.BS_K)
                P.op("vector", lambda e: e.reduce_max(
                    t.MX[:, :], t.SH[:, :].rearrange("p (l m) -> p l m", m=16), axis=AX.X),
                    r=t.BS_K, w=t.BS_K)
                P.op("vector", lambda e: e.tensor_tensor(
                    t.PH[:, :].rearrange("p (l m) -> p l m", m=16),
                    t.SH[:, :].rearrange("p (l m) -> p l m", m=16),
                    t.MX[:, :].unsqueeze(2).broadcast_to([128, 16, 16]), op=ALU.subtract),
                    r=t.BS_K, w=t.BS_K)
                P.op("scalar", lambda e: e.activation(t.PH[:, :], t.PH[:, :], AF.Exp),
                     r=t.BS_K, w=t.BS_K)
                P.op("vector", lambda e: e.reduce_sum(
                    t.SM[:, :], t.PH[:, :].rearrange("p (l m) -> p l m", m=16), axis=AX.X),
                    r=t.BS_K, w=t.BS_K)
                P.op("vector", lambda e: e.reciprocal(t.SM[:, :], t.SM[:, :]), r=t.BS_K, w=t.BS_K)
                pn = t.PN[hl]
                P.op("vector", lambda e, pn=pn: e.tensor_tensor(
                    pn[:, :].rearrange("p (l m) -> p l m", m=16),
                    t.PH[:, :].rearrange("p (l m) -> p l m", m=16),
                    t.SM[:, :].unsqueeze(2).broadcast_to([128, 16, 16]), op=ALU.mult),
                    r=t.BS_K, w=t.BS_K)
                b0_tick(1)
            for l in range(16):
                for hl in range(4):
                    pn = t.PN[hl]
                    for m in range(l + 1):
                        vc = 16384 + m * 512 + hl * 128
                        if m == 0:
                            P.op("vector", lambda e, vc=vc, hl=hl, pn=pn, l=l: e.tensor_scalar(
                                t.OB[:, hl * 128:(hl + 1) * 128], G2[:, vc:vc + 128],
                                pn[:, l * 16: l * 16 + 1], None, op0=ALU.mult),
                                r=vkeys + t.BS_K, w=t.BS_K)
                        else:
                            P.op("vector", lambda e, vc=vc, hl=hl, pn=pn, l=l, m=m:
                                 e.scalar_tensor_tensor(
                                     t.OB[:, hl * 128:(hl + 1) * 128], G2[:, vc:vc + 128],
                                     pn[:, l * 16 + m: l * 16 + m + 1],
                                     t.OB[:, hl * 128:(hl + 1) * 128],
                                     op0=ALU.mult, op1=ALU.add),
                                 r=vkeys + t.BS_K, w=t.BS_K)
                P.op("vector", lambda e: e.tensor_copy(t.OBB[:, :], t.OB[:, :]),
                     r=t.BS_K, w=t.BS_K)
                P.op("gpsimd", lambda e, ncx=ncx, l=l, hg=hg: e.dma_start(
                    out=y1_d[ncx, l * 128:(l + 1) * 128, hg * 512:(hg + 1) * 512],
                    in_=t.OBB[:, :]),
                    r=t.BS_K, w=[f"y1.{ncx}.{l}.{hg}"], dma=True)
                b0_tick(1)
        y1keys = [f"y1.{ncx}.{l}.{hg}" for l in range(16) for hg in range(4)]
        for ec in range(FC):
            P.op("scalar", lambda e, ec=ec, ncx=ncx: e.dma_start(
                out=G1[:, ec * 2048:(ec + 1) * 2048],
                in_=y1_d[ncx][:, ec * 128:(ec + 1) * 128], transpose=True),
                r=y1keys, w=[f"G1#{ec}"], dma=True)
        for cg in range(4):
            for ec in range(FC):
                c0 = 24576 + ec * 512
                P.op("scalar", lambda e, cg=cg, ec=ec, c0=c0: e.dma_start(
                    out=G2[:, c0:c0 + 512],
                    in_=wout_bf[cg * 512:(cg + 1) * 512, ec * 128:(ec + 1) * 128],
                    transpose=True),
                    r=[f"woutbf.{cg}"], w=keys2k("G2", c0, c0 + 512), dma=True)
            for l in range(16):
                grp, gk = t.rot()
                for ec in range(FC):
                    P.op("tensor", lambda e, grp=grp, ec=ec, l=l: e.matmul(
                        grp[:, :], G1[:, ec * 2048 + l * 128: ec * 2048 + (l + 1) * 128],
                        G2[:, 24576 + ec * 512: 24576 + (ec + 1) * 512],
                        start=(ec == 0), stop=(ec == FC - 1)),
                        r=[f"G1#{ec}"] + keys2k("G2", 24576 + ec * 512, 25088 + ec * 512),
                        w=[gk])
                P.op("scalar", lambda e, grp=grp: e.activation(t.S1T[:, :], grp[:, :], AF.Copy),
                     r=[gk], w=t.BS_K)
                P.op("gpsimd", lambda e, ncx=ncx, l=l, cg=cg: e.dma_start(
                    out=s1_d[ncx, l * 128:(l + 1) * 128, cg * 512:(cg + 1) * 512],
                    in_=t.S1T[:, :]),
                    r=t.BS_K, w=[f"s1.{ncx}.{l}.{cg}"], dma=True)

    b0_flush()
    for rg in range(4):
        l0 = rg * 4
        xnd_rg = [f"xnd.{ncx}.{l}" for ncx in range(2) for l in range(l0, l0 + 4)]
        for ec in range(FC):
            for ncx in range(2):
                c0 = 12288 + ec * 1024 + ncx * 512
                P.op("scalar", lambda e, ec=ec, ncx=ncx, l0=l0, c0=c0: e.dma_start(
                    out=G2[:, c0:c0 + 512],
                    in_=xn_d[ncx, l0 * 128:(l0 + 4) * 128, ec * 128:(ec + 1) * 128],
                    transpose=True),
                    r=xnd_rg, w=keys2k("G2", c0, c0 + 512), dma=True)
        for hc in range(HC):
            for (qtd, qk, c0) in ((q1t_d, q1t_all, 28672), (q2t_d, q2t_all, 30720)):
                P.op("sync", lambda e, qtd=qtd, hc=hc, c0=c0: e.dma_start(
                    out=G2[:, c0:c0 + 2048].rearrange("p (ec d) -> p ec d", d=128),
                    in_=qtd[:, hc * 128:(hc + 1) * 128].rearrange("(ec e) d -> e ec d", e=128)),
                    r=qk, w=keys2k("G2", c0, c0 + 2048), dma=True)
            pa = hc % 2
            gsets = ((t.ps[pa * 2], f"PS.{pa * 2}"), (t.ps[pa * 2 + 1], f"PS.{pa * 2 + 1}"))
            gsets2 = ((t.ps[4 + pa], f"PS.{4 + pa}"), (t.ps[6], "PS.6"))
            for (wc0, gs) in ((28672, gsets), (30720, gsets2)):
                for half, (gg, kk) in enumerate(gs):
                    for ec in range(FC):
                        P.op("tensor", lambda e, gg=gg, wc0=wc0, ec=ec, half=half: e.matmul(
                            gg[:, :], G2[:, wc0 + ec * 128: wc0 + (ec + 1) * 128],
                            G2[:, 12288 + ec * 1024 + half * 512: 12800 + ec * 1024 + half * 512],
                            start=(ec == 0), stop=(ec == FC - 1)),
                            r=keys2k("G2", wc0, wc0 + 2048) +
                              keys2k("G2", 12288 + ec * 1024, 12288 + (ec + 1) * 1024),
                            w=[kk])
            for half, (gg, kk) in enumerate(gsets):
                P.op("scalar", lambda e, gg=gg, half=half: e.activation(
                    t.SGM[:, half * 512:(half + 1) * 512], gg[:, :], AF.Sigmoid),
                    r=[kk], w=t.SGM_K)
            hcol = hc * 1024
            hidt = G1 if hcol + 1024 <= 32768 else G2
            hbase = hcol if hcol + 1024 <= 32768 else hcol - 32768
            hname = "G1" if hcol + 1024 <= 32768 else "G2"
            for half, (gg, kk) in enumerate(gsets2):
                P.op("vector", lambda e, hidt=hidt, hbase=hbase, half=half, gg=gg: e.tensor_tensor(
                    hidt[:, hbase + half * 512: hbase + (half + 1) * 512],
                    t.SGM[:, half * 512:(half + 1) * 512], gg[:, :], op=ALU.mult),
                    r=t.SGM_K + [kk], w=keys2k(hname, hbase + half * 512, hbase + (half + 1) * 512))
        for ec in range(FC):
            P.op("sync", lambda e, ec=ec: e.dma_start(
                out=G3[:, 0:FH].rearrange("p (hcc d) -> p hcc d", d=128),
                in_=q3t_d[:, ec * 128:(ec + 1) * 128].rearrange("(hcc h) d -> h hcc d", h=128)),
                r=q3t_all, w=keys2k("G3", 0, FH), dma=True)
            pa = ec % 2
            for half in range(2):
                gg = t.ps[pa * 2 + half]
                kk = f"PS.{pa * 2 + half}"
                for hc in range(HC):
                    hcol = hc * 1024
                    hidt = G1 if hcol + 1024 <= 32768 else G2
                    hbase = hcol if hcol + 1024 <= 32768 else hcol - 32768
                    hname = "G1" if hcol + 1024 <= 32768 else "G2"
                    P.op("tensor", lambda e, gg=gg, hc=hc, hidt=hidt, hbase=hbase, half=half:
                         e.matmul(gg[:, :], G3[:, hc * 128:(hc + 1) * 128],
                                  hidt[:, hbase + half * 512: hbase + half * 512 + 512],
                                  start=(hc == 0), stop=(hc == HC - 1)),
                         r=keys2k("G3", hc * 128, (hc + 1) * 128) +
                           keys2k(hname, hbase, hbase + 1024), w=[kk])
                P.op("scalar", lambda e, gg=gg, half=half: e.activation(
                    t.GFT[:, half * 512:(half + 1) * 512], gg[:, :], AF.Silu),
                    r=[kk], w=t.GFT_K)
            P.op("gpsimd", lambda e, ec=ec: e.dma_start(
                out=gfm_d[ec * 128:(ec + 1) * 128, :], in_=t.GFT[:, :]),
                r=t.GFT_K, w=[f"gfm.{ec}"], dma=True)
        gfm_all = [f"gfm.{ec}" for ec in range(FC)]
        for rt in range(8):
            ncx = rt // 4
            l = l0 + rt % 4
            pp = rt % 2
            xt = t.XT[pp]
            src_ap, src_keys = x1src(l, ncx)
            P.op("sync", lambda e, xt=xt, src_ap=src_ap: e.dma_start(out=xt[:, :], in_=src_ap),
                 r=src_keys, w=[f"xt{pp}"], dma=True)
            P.op("scalar", lambda e, rt=rt: e.dma_start(
                out=t.XNT[:, :], in_=gfm_d[:, rt * 128:(rt + 1) * 128], transpose=True),
                r=gfm_all, w=["XNT"], dma=True)
            P.op("sync", lambda e, ncx=ncx, l=l: e.dma_start(
                out=G3[:, 8192:8192 + 2048], in_=s1_d[ncx, l * 128:(l + 1) * 128, :]),
                r=[f"s1.{ncx}.{l}.{cg}" for cg in range(4)],
                w=keys2k("G3", 8192, 8192 + 2048), dma=True)
            for cg in range(4):
                P.op("vector", lambda e, cg=cg: e.tensor_tensor(
                    t.SCR[:, 0:512],
                    G3[:, 8192 + cg * 512:8192 + (cg + 1) * 512],
                    t.XNT[:, cg * 512:(cg + 1) * 512], op=ALU.mult),
                    r=keys2k("G3", 8192, 10240) + ["XNT"], w=["SCR"])
                P.op("vector", lambda e, xt=xt, cg=cg: e.tensor_tensor(
                    xt[:, cg * 512:(cg + 1) * 512], xt[:, cg * 512:(cg + 1) * 512],
                    t.SCR[:, 0:512], op=ALU.add),
                    r=[f"xt{pp}", "SCR"], w=[f"xt{pp}"])
            P.op("sync", lambda e, l=l, ncx=ncx, xt=xt: e.dma_start(
                out=out_h[l, ncx * 128:(ncx + 1) * 128, :], in_=xt[:, :]),
                r=[f"xt{pp}"], w=[f"out.{l}.{ncx}"], dma=True)


def a_scratch(nc):
    return dict(
        xn_d=nc.dram_tensor("xn_d", [T, C], BF16).ap(),
        bw_d=nc.dram_tensor("bw_d", [C, C], BF16).ap(),
        g_d=nc.dram_tensor("g_d", [T, C], BF16).ap(),
        v_d=nc.dram_tensor("v_d", [H, T, DH], BF16).ap(),
        y_d=nc.dram_tensor("y_d", [H, DH, T], BF16).ap(),
    )


def b_scratch(nc):
    return dict(
        wqkv_bf=nc.dram_tensor("wqkv_bf", [3 * C, C], BF16).ap(),
        wout_bf=nc.dram_tensor("wout_bf", [C, C], BF16).ap(),
        q1_d=nc.dram_tensor("q1_d", [FH, C], BF16).ap(),
        q2_d=nc.dram_tensor("q2_d", [FH, C], BF16).ap(),
        q3_d=nc.dram_tensor("q3_d", [C, FH], BF16).ap(),
        q1t_d=nc.dram_tensor("q1t_d", [C, FH], BF16).ap(),
        q2t_d=nc.dram_tensor("q2t_d", [C, FH], BF16).ap(),
        q3t_d=nc.dram_tensor("q3t_d", [FH, C], BF16).ap(),
        xn1_d=nc.dram_tensor("xn1_d", [2, 2048, C], BF16).ap(),
        y1_d=nc.dram_tensor("y1_d", [2, 2048, C], BF16).ap(),
        s1_d=nc.dram_tensor("s1_d", [2, 2048, C], BF16).ap(),
        gfm_d=nc.dram_tensor("gfm_d", [C, 1024], BF16).ap(),
    )


def a_inputs(nc):
    return dict(
        x=nc.dram_tensor("xa", [2, T, C], F32, kind="ExternalInput").ap(),
        wq=nc.dram_tensor("wq", [H, C, DH], F32, kind="ExternalInput").ap(),
        wk=nc.dram_tensor("wk", [H, C, DH], F32, kind="ExternalInput").ap(),
        wv=nc.dram_tensor("wv", [H, C, DH], F32, kind="ExternalInput").ap(),
        bq=nc.dram_tensor("bq", [H, DH], F32, kind="ExternalInput").ap(),
        bk=nc.dram_tensor("bk", [H, DH], F32, kind="ExternalInput").ap(),
        bv=nc.dram_tensor("bv", [H, DH], F32, kind="ExternalInput").ap(),
        wo=nc.dram_tensor("wo", [C, C], F32, kind="ExternalInput").ap(),
        bo=nc.dram_tensor("bo", [C], F32, kind="ExternalInput").ap(),
        wf=nc.dram_tensor("wf", [C, C], F32, kind="ExternalInput").ap(),
        bf=nc.dram_tensor("bf", [C], F32, kind="ExternalInput").ap(),
    )


def b_inputs(nc):
    return dict(
        wqkv=nc.dram_tensor("wqkv", [3 * C, C], F32, kind="ExternalInput").ap(),
        wout=nc.dram_tensor("wout", [C, C], F32, kind="ExternalInput").ap(),
        w1=nc.dram_tensor("w1", [FH, C], F32, kind="ExternalInput").ap(),
        w2=nc.dram_tensor("w2", [FH, C], F32, kind="ExternalInput").ap(),
        w3=nc.dram_tensor("w3", [C, FH], F32, kind="ExternalInput").ap(),
    )


def build_a_program():
    nc = bass.Bass("TRN2", target_bir_lowering=False)
    es = contextlib.ExitStack()
    t = alloc_all(nc, es)
    io = a_inputs(nc)
    io.update(a_scratch(nc))
    x1_h = nc.dram_tensor("x1out", [2, T, C], F32, kind="ExternalOutput").ap()
    io["x1dst"] = lambda b, i: (x1_h[b, i * 128:(i + 1) * 128, :], f"x1.{b}.{i}")
    P = Plan(nc)
    prologue_consts(P, t)
    build_stage_a(nc, P, t, io)
    with es:
        P.build(es)
    return nc


def build_b_program():
    nc = bass.Bass("TRN2", target_bir_lowering=False)
    es = contextlib.ExitStack()
    t = alloc_all(nc, es)
    io = b_inputs(nc)
    io.update(b_scratch(nc))
    x1_h = nc.dram_tensor("x1c", [B, NB, C], F32, kind="ExternalInput").ap()
    io["x1src"] = lambda l, ncx: (x1_h[l, ncx * 128:(ncx + 1) * 128, :], [])
    io["out"] = nc.dram_tensor("outc", [B, NB, C], F32, kind="ExternalOutput").ap()
    P = Plan(nc)
    prologue_consts(P, t)
    build_stage_b(nc, P, t, io)
    with es:
        P.build(es)
    return nc


def build_fused_program():
    nc = bass.Bass("TRN2", target_bir_lowering=False)
    es = contextlib.ExitStack()
    t = alloc_all(nc, es)
    io = a_inputs(nc)
    io.update(b_inputs(nc))
    io.update(a_scratch(nc))
    io.update(b_scratch(nc))
    x1s = [nc.dram_tensor(f"x1s{b}", [T, C], F32).ap() for b in range(2)]
    xall = [nc.dram_tensor(f"xall{b}", [NCORES * NB, C], F32).ap() for b in range(2)]
    io["x1dst"] = lambda b, i: (x1s[b][i * 128:(i + 1) * 128, :], f"x1.{b}.{i}")
    io["x1src"] = lambda l, ncx: (
        xall[l % 2][(l // 2) * NB + ncx * 128: (l // 2) * NB + (ncx + 1) * 128, :],
        [f"xall.{l % 2}"])
    io["out"] = nc.dram_tensor("outc", [B, NB, C], F32, kind="ExternalOutput").ap()

    P = Plan(nc)
    prologue_consts(P, t)
    build_stage_a(nc, P, t, io)
    for b in range(2):
        def cc(e, b=b):
            with allow_ici_all_to_all():
                return e.collective_compute(
                    "AllToAll", ALU.bypass,
                    replica_groups=[list(range(NCORES))],
                    ins=[x1s[b].opt()],
                    outs=[xall[b].opt()],
                )
        P.op("gpsimd", cc, r=[f"x1.{b}.{i}" for i in range(16)], w=[f"xall.{b}"], cc=True)
    P.barrier()
    build_stage_b(nc, P, t, io)
    with es:
        P.build(es)
    return nc


def kernel(L0_attn_x, L0_attn_Wq, L0_attn_bq, L0_attn_Wk, L0_attn_bk, L0_attn_Wv,
           L0_attn_bv, L0_attn_Wo, L0_attn_bo, L0_ffn_weight, L0_ffn_bias,
           L1_attn_weight_qkv, L1_attn_weight_out, L1_ffn_w1, L1_ffn_w2, L1_ffn_w3):
    f = lambda a: np.ascontiguousarray(np.asarray(a, np.float32))
    x = f(L0_attn_x)
    wA = {"wq": f(L0_attn_Wq), "wk": f(L0_attn_Wk), "wv": f(L0_attn_Wv),
          "bq": f(L0_attn_bq), "bk": f(L0_attn_bk), "bv": f(L0_attn_bv),
          "wo": f(L0_attn_Wo), "bo": f(L0_attn_bo),
          "wf": f(L0_ffn_weight), "bf": f(L0_ffn_bias)}
    wB = {"wqkv": f(L1_attn_weight_qkv), "wout": f(L1_attn_weight_out),
          "w1": f(L1_ffn_w1), "w2": f(L1_ffn_w2), "w3": f(L1_ffn_w3)}
    if FUSED:
        ncF = build_fused_program()
        in_maps = [dict(wA, **wB, xa=np.ascontiguousarray(x[2 * c:2 * c + 2]))
                   for c in range(NCORES)]
        res = run_bass_kernel_spmd(ncF, in_maps, core_ids=list(range(NCORES)))
        return np.concatenate([res.results[c]["outc"] for c in range(NCORES)], axis=1)

    ncA = build_a_program()
    in_maps = [dict(wA, xa=np.ascontiguousarray(x[2 * c:2 * c + 2])) for c in range(NCORES)]
    resA = run_bass_kernel_spmd(ncA, in_maps, core_ids=list(range(NCORES)))
    x1 = np.concatenate([resA.results[c]["x1out"] for c in range(NCORES)], axis=0)
    ncB = build_b_program()
    in_maps_b = [dict(wB, x1c=np.ascontiguousarray(x1[:, c * NB:(c + 1) * NB, :]))
                 for c in range(NCORES)]
    resB = run_bass_kernel_spmd(ncB, in_maps_b, core_ids=list(range(NCORES)))
    return np.concatenate([resB.results[c]["outc"] for c in range(NCORES)], axis=1)


# version 15 (final)
# speedup vs baseline: 1.2408x; 1.0610x over previous
import contextlib
import numpy as np
import concourse.bass as bass
import concourse.mybir as mybir
from concourse.bass_utils import run_bass_kernel_spmd
from concourse.aot_env import allow_ici_all_to_all

F32 = mybir.dt.float32
BF16 = mybir.dt.bfloat16
AF = mybir.ActivationFunctionType
ALU = mybir.AluOpType
AX = mybir.AxisListType

NCORES = 8
B, T, C = 16, 2048, 2048
H, DH = 16, 128
FC = C // 128
SCALE = float(DH) ** -0.5
NEG = -1.0e30
FH = 5632
HC = FH // 128
NB = 256

FUSED = True

ENGS = ("tensor", "vector", "scalar", "gpsimd", "sync")
DMA_POOL = 8


class _Op:
    __slots__ = ("eng", "fn", "reads", "writes", "dma", "idx", "need_inc", "token",
                 "deps", "sem", "selfwait", "barrier", "cc")


class Plan:

    def __init__(self, nc):
        self.nc = nc
        self.ops = []
        self.last_w = {}
        self.readers = {}
        self.dma_idx = {e: 0 for e in ENGS}
        self.cc_idx = 0

    def op(self, eng, fn, r=(), w=(), dma=False, cc=False):
        o = _Op()
        o.eng, o.fn, o.reads, o.writes, o.dma = eng, fn, list(r), list(w), dma
        o.cc = cc
        o.need_inc = False
        o.token = None
        o.sem = None
        o.selfwait = None
        o.barrier = False
        o.idx = len(self.ops)
        if cc:
            o.sem = ("cc", self.cc_idx)
            self.cc_idx += 1
        elif dma:
            o.sem = (eng, self.dma_idx[eng] % DMA_POOL)
            self.dma_idx[eng] += 1
        deps = {}

        def add_dep(p):
            if p is None or p is o:
                return
            if p.eng == o.eng and not p.dma and not o.dma and p.eng == "tensor":
                return
            deps[p.idx] = p

        for k in o.reads:
            add_dep(self.last_w.get(k))
        for k in o.writes:
            add_dep(self.last_w.get(k))
            for rd in self.readers.get(k, ()):
                add_dep(rd)
        best = {}
        for p in deps.values():
            sk = p.sem if (p.dma or p.cc) else p.eng
            q = best.get(sk)
            if q is None or p.idx > q.idx:
                best[sk] = p
        o.deps = list(best.values())
        for p in o.deps:
            p.need_inc = True
        for k in o.writes:
            self.last_w[k] = o
            self.readers[k] = []
        for k in o.reads:
            self.readers.setdefault(k, []).append(o)
        self.ops.append(o)
        return o

    def barrier(self):
        last_tensor = None
        for o in reversed(self.ops):
            if o.eng == "tensor" and not o.barrier:
                last_tensor = o
                break
        if last_tensor is not None:
            last_tensor.need_inc = True
        for e in ENGS:
            o = _Op()
            o.eng, o.fn, o.reads, o.writes, o.dma = e, (lambda en: en.nop()), [], [], False
            if e == "tensor":
                o.fn = None
            o.cc = False
            o.need_inc = e != "tensor"
            o.token = None
            o.sem = None
            o.selfwait = None
            o.barrier = True
            o.idx = len(self.ops)
            o.deps = []
            self.ops.append(o)
        self.last_w = {}
        self.readers = {}

    def build(self, es):
        nc = self.nc
        sems = {e: es.enter_context(nc.semaphore(f"sem_{e}")) for e in ENGS}
        dsems = {}
        for e in ENGS:
            n = min(self.dma_idx[e], DMA_POOL)
            for i in range(n):
                dsems[(e, i)] = es.enter_context(nc.semaphore(f"dsem_{e}{i}"))
        for i in range(self.cc_idx):
            dsems[("cc", i)] = es.enter_context(nc.semaphore(f"ccsem{i}"))
        counts = {e: 0 for e in ENGS}
        dcounts = {k: 0 for k in dsems}
        barrier_waits = {}
        pending_barrier = []
        for o in self.ops:
            if o.barrier:
                pending_barrier.append(o)
                if o.need_inc:
                    counts[o.eng] += 1
                o.token = counts[o.eng]
                if (o.idx + 1 >= len(self.ops) or not getattr(self.ops[o.idx + 1], "barrier", False)):
                    import os
                    snap = {}
                    for e2 in ENGS:
                        if counts[e2]:
                            snap[e2] = counts[e2]
                    if os.environ.get("BARRIER_NODMA", "0") != "1":
                        for k2, v2 in dcounts.items():
                            if v2:
                                snap[k2] = v2
                    for bo in pending_barrier:
                        barrier_waits[bo.idx] = snap
                    pending_barrier = []
                continue
            if o.cc:
                o.need_inc = True
                dcounts[o.sem] = 1
                o.token = 1
            elif o.dma:
                o.need_inc = True
                o.selfwait = dcounts[o.sem]
                dcounts[o.sem] += 16
                o.token = dcounts[o.sem]
            elif o.need_inc:
                counts[o.eng] += 1
                o.token = counts[o.eng]
        totals = dict(counts)
        dtotals = dict(dcounts)

        def semref_key(k):
            return dsems[k] if isinstance(k, tuple) else sems[k]

        def emit_for(eng_name, e):
            fl = {}
            for o in self.ops:
                if o.eng != eng_name:
                    continue
                if o.barrier:
                    snap = barrier_waits[o.idx]
                    if o.fn is not None:
                        ins = o.fn(e)
                        ins.then_inc(sems[o.eng], 1)
                    for k, v in snap.items():
                        if k == eng_name:
                            continue
                        if v > fl.get(k, 0):
                            e.wait_ge(semref_key(k), v)
                            fl[k] = v
                    continue
                if o.dma and o.selfwait > fl.get(o.sem, 0):
                    e.wait_ge(dsems[o.sem], o.selfwait)
                    fl[o.sem] = o.selfwait
                for p in o.deps:
                    k = p.sem if (p.dma or p.cc) else p.eng
                    if p.token > fl.get(k, 0):
                        e.wait_ge(semref_key(k), p.token)
                        fl[k] = p.token
                ins = o.fn(e)
                if o.need_inc:
                    if o.cc:
                        ins.then_inc(dsems[o.sem], 1)
                    else:
                        ins.then_inc(dsems[o.sem] if o.dma else sems[o.eng], 16 if o.dma else 1)
            if eng_name == "sync":
                for e2 in ENGS:
                    if totals[e2] > fl.get(e2, 0):
                        e.wait_ge(sems[e2], totals[e2])
                for k, v in dtotals.items():
                    if v > fl.get(k, 0):
                        e.wait_ge(dsems[k], v)

        with nc.Block() as block:
            for e in ENGS:
                getattr(block, e)((lambda en: (lambda eng_obj: emit_for(en, eng_obj)))(e))


def keys2k(name, c0, c1):
    return [f"{name}#{i}" for i in range(c0 // 2048, (c1 + 2047) // 2048)]


class NS:
    pass


def alloc_all(nc, es):
    t = NS()
    t.ps = [es.enter_context(nc.psum_tensor(f"ps{i}", [128, 512], F32)) for i in range(7)]
    t.psl = es.enter_context(nc.psum_tensor("psl", [1, 512], F32))
    t.ps_i = 0

    def rot():
        t.ps_i = (t.ps_i + 1) % 7
        return t.ps[t.ps_i], f"PS.{t.ps_i}"
    t.rot = rot
    sb = lambda n, sh, dt: es.enter_context(nc.sbuf_tensor(n, sh, dt))
    t.G1t = sb("G1", [128, 32768], BF16)
    t.G2t = sb("G2", [128, 32768], BF16)
    t.G3t = sb("G3", [128, 16384], BF16)
    t.XT = [sb(f"XT{i}", [128, C], F32) for i in range(2)]
    t.XNT = sb("XNT", [128, C], BF16)
    t.SCR = sb("SCR", [128, 512], F32)
    t.ones_r = sb("ones_r", [1, 512], BF16)
    t.ones_c = sb("ones_c", [128, 1], BF16)
    t.ones_f = sb("ones_f", [128, 1], F32)
    t.ones_fr = sb("ones_fr", [1, 128], F32)
    t.negm = [sb(f"negm{i}", [128, 1], F32) for i in range(2)]
    t.accq = [sb(f"accq{i}", [128, 1], F32) for i in range(2)]
    t.tmp1 = [sb(f"tmp1_{i}", [128, 1], F32) for i in range(2)]
    t.rstd = [sb(f"rstd{i}", [128, 1], F32) for i in range(2)]
    t.BQ = sb("BQ", [1, 2048], BF16)
    t.BK = sb("BK", [1, 2048], BF16)
    t.BV = sb("BV", [1, 2048], BF16)
    t.BO = sb("BO", [1, 2048], BF16)
    t.BF_ = sb("BF_", [1, 2048], BF16)
    
    t.MASKT = sb("MASKT", [128, 128], BF16)
    t.GRW = sb("GRW", [128, 1], F32)
    t.INV = sb("INV", [128, 1], F32)
    t.T11 = sb("T11", [1, 1], F32)

    t.G1 = t.G1t.ap()
    t.G2 = t.G2t.ap()
    t.G3 = t.G3t.ap()
    g2, g3 = t.G2, t.G3
    g3f = t.G3t.ap().bitcast(F32)
    t.XNFM = t.G1
    t.PTOFF = []
    off = 0
    for c in range(16):
        t.PTOFF.append(off)
        off += T - c * 128
    t.PT = g2[:, 0:17408]
    t.LNL = g2[0:1, 30720:32768]
    t.LNL_K = keys2k("G2", 30720, 32768)
    t.WQH = [g2[:, 18432:20480], g2[:, 20480:22528]]
    t.WKH = [g2[:, 22528:24576], g2[:, 24576:26624]]
    t.QFM = g2[:, 26624:28672]
    t.KFM = g2[:, 28672:30720]
    t.WQH_K = [keys2k("G2", 18432, 20480), keys2k("G2", 20480, 22528)]
    t.WKH_K = [keys2k("G2", 22528, 24576), keys2k("G2", 24576, 26624)]
    t.QFM_K = keys2k("G2", 26624, 28672)
    t.KFM_K = keys2k("G2", 28672, 30720)
    t.VH = g3[:, 0:2048]
    t.YH = g3[:, 2048:4096]
    t.RBT = g3[:, 4096:6144]
    t.GT = [g3[:, 6144:8192], g3[:, 8192:10240]]
    t.WVC = g3[:, 10240:12288]
    t.VT = g3[:, 12288:12800]
    t.VH_K, t.YH_K, t.RBT_K = ["G3#0"], ["G3#1"], ["G3#2"]
    t.GT_K = [["G3#3"], ["G3#4"]]
    t.WVC_K, t.VT_K = ["G3#5"], ["G3#6"]
    t.SH = g3f[:, 5120:5376]
    t.PH = g3f[:, 5376:5632]
    t.PN = [g3f[:, 5632 + i * 256: 5888 + i * 256] for i in range(4)]
    t.OB = g3f[:, 6656:7168]
    t.OBB = g3[:, 14336:14848]
    t.S1T = g3[:, 14848:15360]
    t.SCRB = g3[:, 15360:15488]
    t.MX = g3f[:, 7744:7760]
    t.SM = g3f[:, 7760:7776]
    t.BS_K = keys2k("G3", 10240, 15552)
    t.SGM = g3f[:, 2816:3840]
    t.SGM_K = keys2k("G3", 5632, 7680)
    t.GFT = g3[:, 10240:11264]
    t.GFT_K = keys2k("G3", 10240, 11264)
    return t


def prologue_consts(P, t):
    P.op("gpsimd", lambda e: e.memset(t.ones_r[:, :], 1.0), w=["ones_r"])
    P.op("gpsimd", lambda e: e.memset(t.ones_c[:, :], 1.0), w=["ones_c"])
    P.op("gpsimd", lambda e: e.memset(t.ones_f[:, :], 1.0), w=["ones_f"])
    P.op("gpsimd", lambda e: e.memset(t.ones_fr[:, :], 1.0), w=["ones_fr"])


def ln_tile(P, t, xt, xt_key, xnt, xnt_key, pp):
    negm, accq, tmp1, rstd = t.negm[pp], t.accq[pp], t.tmp1[pp], t.rstd[pp]
    kn, ka, kt, kr = f"negm{pp}", f"accq{pp}", f"tmp1{pp}", f"rstd{pp}"
    grp, gk = t.rot()
    P.op("vector", lambda e: e.reduce_sum(negm[:, :], xt[:, :], axis=AX.X, negate=True),
         r=[xt_key], w=[kn])
    P.op("vector", lambda e: e.tensor_scalar(negm[:, :], negm[:, :], 1.0 / C, None, op0=ALU.mult),
         r=[kn], w=[kn])
    P.op("scalar", lambda e: e.activation(grp[:, :], xt[:, 0:512], AF.Square, accum_out=accq[:, :]),
         r=[xt_key], w=[gk, ka])
    for j in range(1, 4):
        P.op("scalar", lambda e, j=j: e.activation(grp[:, :], xt[:, j * 512:(j + 1) * 512],
                                                   AF.Square, accum_out=tmp1[:, :]),
             r=[xt_key], w=[gk, kt])
        P.op("vector", lambda e: e.tensor_tensor(accq[:, :], accq[:, :], tmp1[:, :], op=ALU.add),
             r=[ka, kt], w=[ka])
    P.op("vector", lambda e: e.tensor_tensor(tmp1[:, :], negm[:, :], negm[:, :], op=ALU.mult),
         r=[kn], w=[kt])
    P.op("vector", lambda e: e.tensor_scalar(tmp1[:, :], tmp1[:, :], -1.0, 1e-5,
                                             op0=ALU.mult, op1=ALU.add),
         r=[kt], w=[kt])
    P.op("scalar", lambda e: e.activation(rstd[:, :], accq[:, :], AF.Sqrt, bias=tmp1[:, :],
                                          scale=1.0 / C),
         r=[ka, kt], w=[kr])
    P.op("vector", lambda e: e.reciprocal(rstd[:, :], rstd[:, :]), r=[kr], w=[kr])
    P.op("vector", lambda e: e.tensor_scalar(xnt[:, :], xt[:, :], negm[:, :], rstd[:, :],
                                             op0=ALU.add, op1=ALU.mult),
         r=[xt_key, kn, kr], w=[xnt_key])


def build_stage_a(nc, P, t, io):
    x_h, x1dst = io["x"], io["x1dst"]
    wq_h, wk_h, wv_h = io["wq"], io["wk"], io["wv"]
    bq_h, bk_h, bv_h, bo_h, bf_h = io["bq"], io["bk"], io["bv"], io["bo"], io["bf"]
    wo_h, wf_h = io["wo"], io["wf"]
    xn_d, bw_d, g_d, v_d, y_d = io["xn_d"], io["bw_d"], io["g_d"], io["v_d"], io["y_d"]

    for (row, src, nm) in ((t.BQ, bq_h, "BQ"), (t.BK, bk_h, "BK"), (t.BV, bv_h, "BV")):
        P.op("gpsimd", lambda e, row=row, src=src: e.dma_start(
            out=row[0:1, :], in_=src.rearrange("h d -> (h d)").unsqueeze(0)),
            w=[nm], dma=True)
    P.op("gpsimd", lambda e: e.dma_start(out=t.BO[0:1, :], in_=bo_h.unsqueeze(0)), w=["BO"], dma=True)
    P.op("gpsimd", lambda e: e.dma_start(out=t.BF_[0:1, :], in_=bf_h.unsqueeze(0)), w=["BF_"], dma=True)
    P.op("gpsimd", lambda e: e.memset(t.MASKT[:, :], 0.0), w=["MASKT"])
    P.op("gpsimd", lambda e: e.affine_select(out=t.MASKT[:, :], in_=t.MASKT[:, :],
                                             compare_op=ALU.is_ge, fill=NEG, base=0,
                                             pattern=[[1, 128]], channel_multiplier=-1),
         r=["MASKT"], w=["MASKT"])

    for oc in range(16):
        pp = oc % 2
        xt, gt, gtk = t.XT[pp], t.GT[pp], t.GT_K[pp]
        P.op("sync", lambda e, oc=oc, xt=xt: e.dma_start(
            out=xt[:, :], in_=wf_h[oc * 128:(oc + 1) * 128, :]), w=[f"xt{pp}"], dma=True)
        P.op("vector", lambda e, xt=xt, pp=pp: e.reduce_sum(
            t.tmp1[pp][:, :], xt[:, :], axis=AX.X, apply_absolute_value=True),
            r=[f"xt{pp}"], w=[f"tmp1{pp}"])
        P.op("vector", lambda e, pp=pp: e.tensor_scalar(
            t.tmp1[pp][:, :], t.tmp1[pp][:, :], 1.0 / C, None, op0=ALU.mult),
            r=[f"tmp1{pp}"], w=[f"tmp1{pp}"])
        P.op("scalar", lambda e, xt=xt: e.sign(t.XNT[:, :], xt[:, :]), r=[f"xt{pp}"], w=["XNT"])
        P.op("vector", lambda e, gt=gt, pp=pp: e.tensor_scalar(
            gt[:, :], t.XNT[:, :], t.tmp1[pp][:, :], None, op0=ALU.mult),
            r=["XNT", f"tmp1{pp}"], w=gtk)
        P.op("gpsimd", lambda e, gt=gt, oc=oc: e.dma_start(
            out=bw_d[oc * 128:(oc + 1) * 128, :], in_=gt[:, :]),
            r=gtk, w=[f"bwd.{oc}"], dma=True)
    bwd_all = [f"bwd.{oc}" for oc in range(16)]

    for b in range(2):
        for i in range(16):
            pp = i % 2
            xt = t.XT[pp]
            P.op("sync", lambda e, b=b, i=i, xt=xt: e.dma_start(
                out=xt[:, :], in_=x_h[b, i * 128:(i + 1) * 128, :]), w=[f"xt{pp}"], dma=True)
            ln_tile(P, t, xt, f"xt{pp}", t.XNT, "XNT", pp)
            P.op("gpsimd", lambda e, i=i: e.dma_start(
                out=xn_d[i * 128:(i + 1) * 128, :], in_=t.XNT[:, :]),
                r=["XNT"], w=[f"xnd.{i}"], dma=True)
        xnd_all = [f"xnd.{i}" for i in range(16)]
        for fc in range(FC):
            P.op("scalar", lambda e, fc=fc: e.dma_start(
                out=t.XNFM[:, fc * 2048:(fc + 1) * 2048],
                in_=xn_d[:, fc * 128:(fc + 1) * 128], transpose=True),
                r=xnd_all, w=[f"G1#{fc}"], dma=True)

        for half in range(2):
            for ic in range(FC):
                c0 = ic * 1024
                P.op("scalar", lambda e, ic=ic, half=half, c0=c0: e.dma_start(
                    out=t.PT[:, c0:c0 + 1024],
                    in_=bw_d[half * 1024:(half + 1) * 1024, ic * 128:(ic + 1) * 128],
                    transpose=True),
                    r=bwd_all, w=keys2k("G2", c0, c0 + 1024), dma=True)
            for i in range(16):
                pp = i % 2
                gt, gtk = t.GT[pp], t.GT_K[pp]
                for cg in range(2):
                    grp, gk = t.rot()
                    ocol = half * 1024 + cg * 512
                    for ic in range(FC):
                        P.op("tensor", lambda e, grp=grp, ic=ic, i=i, cg=cg: e.matmul(
                            grp[:, :], t.XNFM[:, ic * 2048 + i * 128: ic * 2048 + (i + 1) * 128],
                            t.PT[:, ic * 1024 + cg * 512: ic * 1024 + (cg + 1) * 512],
                            start=(ic == 0), stop=False),
                            r=[f"G1#{ic}"] + keys2k("G2", ic * 1024, ic * 1024 + 1024), w=[gk])
                    P.op("tensor", lambda e, grp=grp, ocol=ocol: e.matmul(
                        grp[:, :], t.ones_r[0:1, 0:128], t.BF_[0:1, ocol:ocol + 512],
                        start=False, stop=True),
                        r=["ones_r", "BF_"], w=[gk])
                    P.op("scalar", lambda e, grp=grp, gt=gt, cg=cg: e.activation(
                        gt[:, cg * 512:(cg + 1) * 512], grp[:, :], AF.Silu),
                        r=[gk], w=gtk)
                P.op("gpsimd", lambda e, gt=gt, i=i, half=half: e.dma_start(
                    out=g_d[i * 128:(i + 1) * 128, half * 1024:(half + 1) * 1024],
                    in_=gt[:, 0:1024]),
                    r=gtk, w=[f"gd.{i}.{half}"], dma=True)

        for hgv in range(4):
            for ic in range(FC):
                c0 = ic * 512
                P.op("gpsimd", lambda e, hgv=hgv, ic=ic, c0=c0: e.dma_start(
                    out=t.PT[:, c0:c0 + 512],
                    in_=wv_h.rearrange("h i d -> i h d")[ic * 128:(ic + 1) * 128,
                                                         4 * hgv:4 * hgv + 4, :]),
                    w=keys2k("G2", c0, c0 + 512), dma=True)
            for i in range(16):
                grp, gk = t.rot()
                for ic in range(FC):
                    P.op("tensor", lambda e, grp=grp, ic=ic, i=i: e.matmul(
                        grp[:, :], t.XNFM[:, ic * 2048 + i * 128: ic * 2048 + (i + 1) * 128],
                        t.PT[:, ic * 512:(ic + 1) * 512], start=(ic == 0), stop=False),
                        r=[f"G1#{ic}"] + keys2k("G2", ic * 512, ic * 512 + 512), w=[gk])
                P.op("tensor", lambda e, grp=grp, hgv=hgv: e.matmul(
                    grp[:, :], t.ones_r[0:1, 0:128], t.BV[0:1, hgv * 512:(hgv + 1) * 512],
                    start=False, stop=True),
                    r=["ones_r", "BV"], w=[gk])
                P.op("scalar", lambda e, grp=grp: e.activation(t.VT[:, :], grp[:, :], AF.Copy),
                     r=[gk], w=t.VT_K)
                P.op("gpsimd", lambda e, i=i, hgv=hgv: e.dma_start(
                    out=v_d.rearrange("h t d -> t h d")[i * 128:(i + 1) * 128,
                                                        4 * hgv:4 * hgv + 4, :],
                    in_=t.VT[:, :]),
                    r=t.VT_K, w=[f"vd.{hgv}.{i}"], dma=True)

        for h in range(H):
            pp = h % 2
            wq, wk = t.WQH[pp], t.WKH[pp]
            wqk, wkk = t.WQH_K[pp], t.WKH_K[pp]
            P.op("gpsimd", lambda e, h=h, wq=wq: e.dma_start(
                out=wq[:, :], in_=wq_h[h].rearrange("(fc f) d -> f fc d", f=128)),
                w=wqk, dma=True)
            P.op("gpsimd", lambda e, h=h, wk=wk: e.dma_start(
                out=wk[:, :], in_=wk_h[h].rearrange("(fc f) d -> f fc d", f=128)),
                w=wkk, dma=True)
            P.op("sync", lambda e, h=h: e.dma_start(
                out=t.VH[:, :].rearrange("t (c d) -> t c d", d=128),
                in_=v_d[h].rearrange("(c t) d -> t c d", t=128)),
                r=[f"vd.{h // 4}.{i}" for i in range(16)], w=t.VH_K, dma=True)
            for dst, dkey, wgt, wkey, brow, bkey, scl in (
                    (t.QFM, t.QFM_K, wq, wqk, t.BQ, "BQ", SCALE),
                    (t.KFM, t.KFM_K, wk, wkk, t.BK, "BK", None)):
                for tg in range(4):
                    grp, gk = t.rot()
                    for fc in range(FC):
                        P.op("tensor", lambda e, grp=grp, wgt=wgt, fc=fc, tg=tg: e.matmul(
                            grp[:, :], wgt[:, fc * 128:(fc + 1) * 128],
                            t.XNFM[:, fc * 2048 + tg * 512: fc * 2048 + (tg + 1) * 512],
                            start=(fc == 0), stop=False),
                            r=wkey + [f"G1#{fc}"], w=[gk])
                    P.op("tensor", lambda e, grp=grp, brow=brow, h=h: e.matmul(
                        grp[:, :], brow[0:1, h * 128:(h + 1) * 128], t.ones_r[0:1, 0:512],
                        start=False, stop=True),
                        r=[bkey, "ones_r"], w=[gk])
                    if scl is not None:
                        P.op("scalar", lambda e, grp=grp, dst=dst, tg=tg, scl=scl: e.activation(
                            dst[:, tg * 512:(tg + 1) * 512], grp[:, :], AF.Copy, scale=scl),
                            r=[gk], w=dkey)
                    else:
                        P.op("scalar", lambda e, grp=grp, dst=dst, tg=tg: e.activation(
                            dst[:, tg * 512:(tg + 1) * 512], grp[:, :], AF.Copy),
                            r=[gk], w=dkey)
            for c in range(16):
                Lc = T - c * 128
                for g in range((Lc + 511) // 512):
                    N = min(512, Lc - g * 512)
                    grp, gk = t.rot()
                    P.op("tensor", lambda e, grp=grp, c=c, g=g, N=N: e.matmul(
                        grp[:, 0:N], t.KFM[:, c * 128:(c + 1) * 128],
                        t.QFM[:, c * 128 + g * 512: c * 128 + g * 512 + N],
                        start=True, stop=True),
                        r=t.KFM_K + t.QFM_K, w=[gk])
                    if g == 0:
                        P.op("vector", lambda e, grp=grp: e.tensor_tensor(
                            grp[:, 0:128], grp[:, 0:128], t.MASKT[:, :], op=ALU.add),
                            r=[gk, "MASKT"], w=[gk])
                    p0 = t.PTOFF[c] + g * 512
                    P.op("scalar", lambda e, grp=grp, p0=p0, N=N: e.activation(
                        t.PT[:, p0:p0 + N], grp[:, 0:N], AF.Exp),
                        r=[gk], w=keys2k("G2", p0, p0 + N))
            for tg in range(4):
                t0 = tg * 512
                cmax = min(15, (t0 + 511) // 128)
                for c in range(cmax + 1):
                    lo = max(t0, c * 128)
                    N = t0 + 512 - lo
                    src = t.PTOFF[c] + lo - c * 128
                    P.op("tensor", lambda e, lo=lo, t0=t0, N=N, src=src, c=c, cmax=cmax: e.matmul(
                        t.psl[0:1, lo - t0:lo - t0 + N], t.ones_c[:, :], t.PT[:, src:src + N],
                        start=(c == 0), stop=(c == cmax)),
                        r=["ones_c"] + keys2k("G2", src, src + N), w=["PSL"])
                P.op("scalar", lambda e, tg=tg: e.activation(
                    t.LNL[0:1, tg * 512:(tg + 1) * 512], t.psl[0:1, :], AF.Ln),
                    r=["PSL"], w=t.LNL_K)
            for c in range(16):
                for tg in range((c * 128) // 512, 4):
                    t0 = tg * 512
                    lo = max(t0, c * 128)
                    N = t0 + 512 - lo
                    src = t.PTOFF[c] + lo - c * 128
                    cmax = tg * 4 + 3
                    P.op("tensor", lambda e, c=c, tg=tg, lo=lo, t0=t0, N=N, src=src, cmax=cmax:
                         e.matmul(t.ps[tg][:, lo - t0:lo - t0 + N], t.VH[:, c * 128:(c + 1) * 128],
                                  t.PT[:, src:src + N], start=(c == 0), stop=(c == cmax)),
                         r=t.VH_K + keys2k("G2", src, src + N), w=[f"PS.{tg}"])
            for tg in range(4):
                gi = 4 + (tg % 3)
                grp = t.ps[gi]
                gk = f"PS.{gi}"
                P.op("tensor", lambda e, grp=grp, tg=tg: e.matmul(
                    grp[:, :], t.ones_r[0:1, 0:128], t.LNL[0:1, tg * 512:(tg + 1) * 512],
                    start=True, stop=True),
                    r=["ones_r"] + t.LNL_K, w=[gk])
                P.op("scalar", lambda e, grp=grp, tg=tg: e.activation(
                    t.RBT[:, tg * 512:(tg + 1) * 512], grp[:, :], AF.Exp, scale=-1.0),
                    r=[gk], w=t.RBT_K)
                P.op("vector", lambda e, tg=tg: e.tensor_tensor(
                    t.YH[:, tg * 512:(tg + 1) * 512], t.ps[tg][:, :],
                    t.RBT[:, tg * 512:(tg + 1) * 512], op=ALU.mult),
                    r=[f"PS.{tg}"] + t.RBT_K, w=t.YH_K)
            P.op("gpsimd", lambda e, h=h: e.dma_start(out=y_d[h], in_=t.YH[:, :]),
                 r=t.YH_K, w=[f"yd.{h}"], dma=True)

        for fc in range(FC):
            P.op("gpsimd", lambda e, fc=fc: e.dma_start(
                out=t.XNFM[:, fc * 2048:(fc + 1) * 2048], in_=wo_h[fc * 128:(fc + 1) * 128, :]),
                w=[f"G1#{fc}"], dma=True)
        yd_all = [f"yd.{h}" for h in range(H)]
        for i in range(16):
            pp = i % 2
            xt = t.XT[pp]
            gt, gtk = t.GT[pp], t.GT_K[pp]
            yt = t.XNT if pp == 0 else t.WVC
            ytk = ["XNT"] if pp == 0 else t.WVC_K
            P.op("sync", lambda e, b=b, i=i, xt=xt: e.dma_start(
                out=xt[:, :], in_=x_h[b, i * 128:(i + 1) * 128, :]), w=[f"xt{pp}"], dma=True)
            P.op("sync", lambda e, i=i, yt=yt: e.dma_start(
                out=yt[:, :].rearrange("d (h t) -> d h t", h=H),
                in_=y_d.rearrange("h d t -> d h t")[:, :, i * 128:(i + 1) * 128]),
                r=yd_all, w=ytk, dma=True)
            P.op("gpsimd", lambda e, i=i, gt=gt: e.dma_start(
                out=gt[:, :], in_=g_d[i * 128:(i + 1) * 128, :]),
                r=[f"gd.{i}.0", f"gd.{i}.1"], w=gtk, dma=True)
            for cg in range(4):
                grp, gk = t.rot()
                for fc in range(FC):
                    P.op("tensor", lambda e, grp=grp, yt=yt, fc=fc, cg=cg: e.matmul(
                        grp[:, :], yt[:, fc * 128:(fc + 1) * 128],
                        t.XNFM[:, fc * 2048 + cg * 512: fc * 2048 + (cg + 1) * 512],
                        start=(fc == 0), stop=False),
                        r=ytk + [f"G1#{fc}"], w=[gk])
                P.op("tensor", lambda e, grp=grp, cg=cg: e.matmul(
                    grp[:, :], t.ones_r[0:1, 0:128], t.BO[0:1, cg * 512:(cg + 1) * 512],
                    start=False, stop=True),
                    r=["ones_r", "BO"], w=[gk])
                P.op("vector", lambda e, grp=grp, gt=gt, cg=cg: e.scalar_tensor_tensor(
                    t.SCR[:, 0:512], grp[:, :], 1.0, gt[:, cg * 512:(cg + 1) * 512],
                    op0=ALU.mult, op1=ALU.mult),
                    r=[gk] + gtk, w=["SCR"])
                P.op("vector", lambda e, xt=xt, cg=cg: e.tensor_tensor(
                    xt[:, cg * 512:(cg + 1) * 512], xt[:, cg * 512:(cg + 1) * 512],
                    t.SCR[:, 0:512], op=ALU.add),
                    r=[f"xt{pp}", "SCR"], w=[f"xt{pp}"])
            dst_ap, dst_key = x1dst(b, i)
            P.op("sync", lambda e, xt=xt, dst_ap=dst_ap: e.dma_start(out=dst_ap, in_=xt[:, :]),
                 r=[f"xt{pp}"], w=[dst_key], dma=True)


def build_stage_b(nc, P, t, io):
    x1src = io["x1src"]
    wqkv_h, wout_h = io["wqkv"], io["wout"]
    w1_h, w2_h, w3_h = io["w1"], io["w2"], io["w3"]
    out_h = io["out"]
    wqkv_bf, wout_bf = io["wqkv_bf"], io["wout_bf"]
    q1_d, q2_d, q3_d = io["q1_d"], io["q2_d"], io["q3_d"]
    q1t_d, q2t_d, q3t_d = io["q1t_d"], io["q2t_d"], io["q3t_d"]
    xn_d, y1_d, s1_d, gfm_d = io["xn1_d"], io["y1_d"], io["s1_d"], io["gfm_d"]
    G1, G2, G3 = t.G1, t.G2, t.G3

    for j in range(12):
        P.op("gpsimd", lambda e, j=j: e.dma_start(
            out=wqkv_bf[j * 512:(j + 1) * 512, :], in_=wqkv_h[j * 512:(j + 1) * 512, :]),
            w=[f"wqkvbf.{j}"], dma=True)
    for j in range(4):
        P.op("gpsimd", lambda e, j=j: e.dma_start(
            out=wout_bf[j * 512:(j + 1) * 512, :], in_=wout_h[j * 512:(j + 1) * 512, :]),
            w=[f"woutbf.{j}"], dma=True)

    b0_steps = []

    def quantize(w_hbm, q_d, rows, cols, tagw):
        nchunk = rows // 128
        nparts = (cols + 2047) // 2048
        b0_steps.append((None, lambda: P.op("gpsimd", lambda e: e.memset(t.GRW[:, :], 0.0), w=["GRW"])))

        def gamma_load(ch, part):
            pp = (ch * nparts + part) % 2
            xt = t.XT[pp]
            c0, c1 = part * 2048, min(cols, (part + 1) * 2048)
            P.op("sync", lambda e, ch=ch, xt=xt, c0=c0, c1=c1: e.dma_start(
                out=xt[:, 0:c1 - c0], in_=w_hbm[ch * 128:(ch + 1) * 128, c0:c1]),
                w=[f"xt{pp}"], dma=True)

        def gamma_comp(ch, part):
            pp = (ch * nparts + part) % 2
            xt = t.XT[pp]
            c0, c1 = part * 2048, min(cols, (part + 1) * 2048)
            P.op("vector", lambda e, xt=xt, pp=pp, n=c1 - c0: e.reduce_sum(
                t.tmp1[pp][:, :], xt[:, 0:n], axis=AX.X, apply_absolute_value=True),
                r=[f"xt{pp}"], w=[f"tmp1{pp}"])
            P.op("gpsimd", lambda e, pp=pp: e.tensor_tensor(
                t.GRW[:, :], t.GRW[:, :], t.tmp1[pp][:, :], op=ALU.add),
                r=["GRW", f"tmp1{pp}"], w=["GRW"])
        for ch in range(nchunk):
            for part in range(nparts):
                b0_steps.append((lambda ch=ch, part=part: gamma_load(ch, part),
                                 lambda ch=ch, part=part: gamma_comp(ch, part)))

        def finalize():
            P.op("tensor", lambda e: e.matmul(t.psl[0:1, 0:1], t.GRW[:, :], t.ones_f[:, :],
                                              start=True, stop=True),
                 r=["GRW", "ones_f"], w=["PSL"])
            P.op("vector", lambda e: e.tensor_scalar(
                t.T11[:, :], t.psl[0:1, 0:1], 0.5 / (rows * cols), 0.5e-8,
                op0=ALU.mult, op1=ALU.add),
                r=["PSL"], w=["T11"])
            grp, gk = t.rot()
            P.op("tensor", lambda e, grp=grp: e.matmul(grp[:, 0:1], t.ones_fr[:, :], t.T11[:, :],
                                                       start=True, stop=True),
                 r=["ones_fr", "T11"], w=[gk])
            P.op("vector", lambda e, grp=grp: e.tensor_copy(t.INV[:, :], grp[:, 0:1]),
                 r=[gk], w=["INV"])
            P.op("vector", lambda e: e.tensor_scalar(t.GRW[:, :], t.INV[:, :], -1.0, None,
                                                     op0=ALU.mult),
                 r=["INV"], w=["GRW"])
        b0_steps.append((None, finalize))

        def quant_load(ch, part):
            pp = (ch * nparts + part) % 2
            xt = t.XT[pp]
            c0, c1 = part * 2048, min(cols, (part + 1) * 2048)
            P.op("sync", lambda e, ch=ch, xt=xt, c0=c0, c1=c1: e.dma_start(
                out=xt[:, 0:c1 - c0], in_=w_hbm[ch * 128:(ch + 1) * 128, c0:c1]),
                w=[f"xt{pp}"], dma=True)

        def quant_comp(ch, part):
            pp = (ch * nparts + part) % 2
            xt = t.XT[pp]
            c0, c1 = part * 2048, min(cols, (part + 1) * 2048)
            n = c1 - c0
            P.op("vector", lambda e, xt=xt, n=n: e.tensor_scalar(
                t.XNT[:, 0:n], xt[:, 0:n], t.INV[:, :], None, op0=ALU.is_gt),
                r=[f"xt{pp}", "INV"], w=["XNT"])
            P.op("vector", lambda e, xt=xt, n=n: e.tensor_scalar(
                t.GT[0][:, 0:n], xt[:, 0:n], t.GRW[:, :], None, op0=ALU.is_lt),
                r=[f"xt{pp}", "GRW"], w=t.GT_K[0])
            P.op("gpsimd", lambda e, n=n: e.tensor_tensor(
                t.XNT[:, 0:n], t.XNT[:, 0:n], t.GT[0][:, 0:n], op=ALU.subtract),
                r=["XNT"] + t.GT_K[0], w=["XNT"])
            P.op("gpsimd", lambda e, ch=ch, c0=c0, c1=c1: e.dma_start(
                out=q_d[ch * 128:(ch + 1) * 128, c0:c1], in_=t.XNT[:, 0:c1 - c0]),
                r=["XNT"], w=[f"{tagw}.{ch}.{c0}"], dma=True)
        for ch in range(nchunk):
            for part in range(nparts):
                b0_steps.append((lambda ch=ch, part=part: quant_load(ch, part),
                                 lambda ch=ch, part=part: quant_comp(ch, part)))

    q1keys = [f"q1.{ch}.0" for ch in range(HC)]
    q2keys = [f"q2.{ch}.0" for ch in range(HC)]
    q3keys = [f"q3.{ch}.{c0}" for ch in range(FC) for c0 in range(0, FH, 2048)]

    def emit_b0():
        def interleave(into, extra):
            base = len(into)
            pass

        quantize(w1_h, q1_d, FH, C, "q1")
        n1 = len(b0_steps)
        quantize(w2_h, q2_d, FH, C, "q2")
        n2 = len(b0_steps)
        quantize(w3_h, q3_d, C, FH, "q3")
        n3 = len(b0_steps)

        def ptq_step(qd, qtd, rk, tag, ec):
            P.op("scalar", lambda e, qd=qd, ec=ec: e.dma_start(
                out=G3[:, 0:FH], in_=qd[:, ec * 128:(ec + 1) * 128], transpose=True),
                r=rk, w=keys2k("G3", 0, FH), dma=True)
            P.op("sync", lambda e, qtd=qtd, ec=ec: e.dma_start(
                out=qtd[ec * 128:(ec + 1) * 128, :], in_=G3[:, 0:FH]),
                r=keys2k("G3", 0, FH), w=[f"{tag}.{ec}"], dma=True)
        ptq1 = [(None, lambda a=q1_d, b2=q1t_d, c=q1keys, d="q1t", ec=ec:
                 ptq_step(a, b2, c, d, ec)) for ec in range(FC)]
        ptq2 = [(None, lambda a=q2_d, b2=q2t_d, c=q2keys, d="q2t", ec=ec:
                 ptq_step(a, b2, c, d, ec)) for ec in range(FC)]

        def pt3_step(hc):
            P.op("scalar", lambda e, hc=hc: e.dma_start(
                out=G3[:, 0:C], in_=q3_d[:, hc * 128:(hc + 1) * 128], transpose=True),
                r=q3keys, w=keys2k("G3", 0, C), dma=True)
            P.op("sync", lambda e, hc=hc: e.dma_start(
                out=q3t_d[hc * 128:(hc + 1) * 128, :], in_=G3[:, 0:C]),
                r=keys2k("G3", 0, C), w=[f"q3t.{hc}"], dma=True)
        ptq3 = [(None, lambda hc=hc: pt3_step(hc)) for hc in range(HC)]
        def weave(lo, hi, extra):
            seg = b0_steps[lo:hi]
            out = []
            k = max(1, len(seg) // max(1, len(extra)))
            ei = 0
            for i, s in enumerate(seg):
                out.append(s)
                if i % k == k - 1 and ei < len(extra):
                    out.append(extra[ei])
                    ei += 1
            out.extend(extra[ei:])
            b0_steps[lo:hi] = out
        weave(n2, n3, ptq2)
        weave(n1, n2, ptq1)
        b0_steps.extend(ptq3)

    b0_ix = {"load": 0, "comp": 0}

    def b0_tick(n=1):
        for _ in range(n):
            import os
            LA = int(os.environ.get("B0_LA", "2"))
            while b0_ix["load"] < min(len(b0_steps), b0_ix["comp"] + LA):
                fn = b0_steps[b0_ix["load"]][0]
                if fn is not None:
                    fn()
                b0_ix["load"] += 1
            if b0_ix["comp"] < len(b0_steps):
                b0_steps[b0_ix["comp"]][1]()
                b0_ix["comp"] += 1

    def b0_flush():
        while b0_ix["comp"] < len(b0_steps):
            b0_tick(1)

    q1t_all = [f"q1t.{ec}" for ec in range(FC)]
    q2t_all = [f"q2t.{ec}" for ec in range(FC)]
    q3t_all = [f"q3t.{hc}" for hc in range(HC)]

    for ncx in range(2):
        for l in range(16):
            pp = l % 2
            xt = t.XT[pp]
            src_ap, src_keys = x1src(l, ncx)
            P.op("sync", lambda e, xt=xt, src_ap=src_ap: e.dma_start(out=xt[:, :], in_=src_ap),
                 r=src_keys, w=[f"xt{pp}"], dma=True)
            ln_tile(P, t, xt, f"xt{pp}", t.XNT, "XNT", pp)
            P.op("gpsimd", lambda e, l=l, ncx=ncx: e.dma_start(
                out=xn_d[ncx, l * 128:(l + 1) * 128, :], in_=t.XNT[:, :]),
                r=["XNT"], w=[f"xnd.{ncx}.{l}"], dma=True)
    emit_b0()
    import os as _os
    if _os.environ.get("B0_INLINE", "1") == "1":
        b0_flush()

    for ncx in range(2):
        xnd_nc = [f"xnd.{ncx}.{l}" for l in range(16)]
        for ec in range(FC):
            P.op("scalar", lambda e, ec=ec, ncx=ncx: e.dma_start(
                out=G1[:, ec * 2048:(ec + 1) * 2048],
                in_=xn_d[ncx][:, ec * 128:(ec + 1) * 128], transpose=True),
                r=xnd_nc, w=[f"G1#{ec}"], dma=True)
        for hg in range(4):
            bases = [hg * 512, 2048 + hg * 512, 4096 + hg * 512]
            for part, base in enumerate(bases):
                for ec in range(FC):
                    c0 = 24576 + ec * 512
                    P.op("scalar", lambda e, base=base, ec=ec, c0=c0: e.dma_start(
                        out=G2[:, c0:c0 + 512],
                        in_=wqkv_bf[base:base + 512, ec * 128:(ec + 1) * 128], transpose=True),
                        r=[f"wqkvbf.{base // 512}"], w=keys2k("G2", c0, c0 + 512), dma=True)
                for l in range(16):
                    grp, gk = t.rot()
                    for ec in range(FC):
                        P.op("tensor", lambda e, grp=grp, ec=ec, l=l: e.matmul(
                            grp[:, :], G1[:, ec * 2048 + l * 128: ec * 2048 + (l + 1) * 128],
                            G2[:, 24576 + ec * 512: 24576 + (ec + 1) * 512],
                            start=(ec == 0), stop=(ec == FC - 1)),
                            r=[f"G1#{ec}"] + keys2k("G2", 24576 + ec * 512, 25088 + ec * 512),
                            w=[gk])
                    d0 = part * 8192 + l * 512
                    if part == 0:
                        P.op("scalar", lambda e, grp=grp, d0=d0: e.activation(
                            G2[:, d0:d0 + 512], grp[:, :], AF.Copy, scale=SCALE),
                            r=[gk], w=keys2k("G2", d0, d0 + 512))
                    else:
                        P.op("scalar", lambda e, grp=grp, d0=d0: e.activation(
                            G2[:, d0:d0 + 512], grp[:, :], AF.Copy),
                            r=[gk], w=keys2k("G2", d0, d0 + 512))
                    b0_tick(1)
            qkkeys = keys2k("G2", 0, 16384)
            vkeys = keys2k("G2", 16384, 24576)
            SMUL = t.G3[:, 6144:8192]
            SMUL_K = ["G3#3"]
            for hl in range(4):
                P.op("gpsimd", lambda e: e.memset(t.SH[:, :], NEG), w=t.BS_K)
                for l in range(16):
                    qc = l * 512 + hl * 128
                    kc0 = 8192 + hl * 128
                    n = l + 1
                    P.op("vector", lambda e, qc=qc, kc0=kc0, n=n: e.tensor_tensor(
                        SMUL[:, 0:n * 128].rearrange("p (m d) -> p m d", d=128),
                        G2[:, qc:qc + 128].unsqueeze(1).broadcast_to([128, n, 128]),
                        bass.AP(G2.tensor, G2.offset + kc0,
                                [[G2.ap[0][0], 128], [512, n], [1, 128]]),
                        op=ALU.mult),
                        r=qkkeys, w=SMUL_K)
                    P.op("vector", lambda e, l=l, n=n: e.reduce_sum(
                        t.SH[:, l * 16: l * 16 + n],
                        SMUL[:, 0:n * 128].rearrange("p (m d) -> p m d", d=128), axis=AX.X),
                        r=SMUL_K, w=t.BS_K)
                P.op("vector", lambda e: e.reduce_max(
                    t.MX[:, :], t.SH[:, :].rearrange("p (l m) -> p l m", m=16), axis=AX.X),
                    r=t.BS_K, w=t.BS_K)
                P.op("vector", lambda e: e.tensor_tensor(
                    t.PH[:, :].rearrange("p (l m) -> p l m", m=16),
                    t.SH[:, :].rearrange("p (l m) -> p l m", m=16),
                    t.MX[:, :].unsqueeze(2).broadcast_to([128, 16, 16]), op=ALU.subtract),
                    r=t.BS_K, w=t.BS_K)
                P.op("scalar", lambda e: e.activation(t.PH[:, :], t.PH[:, :], AF.Exp),
                     r=t.BS_K, w=t.BS_K)
                P.op("vector", lambda e: e.reduce_sum(
                    t.SM[:, :], t.PH[:, :].rearrange("p (l m) -> p l m", m=16), axis=AX.X),
                    r=t.BS_K, w=t.BS_K)
                P.op("vector", lambda e: e.reciprocal(t.SM[:, :], t.SM[:, :]), r=t.BS_K, w=t.BS_K)
                pnb = t.G3[:, 8192 + hl * 256: 8448 + hl * 256]
                P.op("vector", lambda e, pnb=pnb: e.tensor_tensor(
                    pnb[:, :].rearrange("p (l m) -> p l m", m=16),
                    t.PH[:, :].rearrange("p (l m) -> p l m", m=16),
                    t.SM[:, :].unsqueeze(2).broadcast_to([128, 16, 16]), op=ALU.mult),
                    r=t.BS_K, w=["G3#4"])
                b0_tick(1)
            for l in range(16):
                n = l + 1
                for hl in range(4):
                    vc0 = 16384 + hl * 128
                    pnb = t.G3[:, 8192 + hl * 256: 8448 + hl * 256]
                    P.op("vector", lambda e, vc0=vc0, pnb=pnb, l=l, n=n: e.tensor_tensor(
                        SMUL[:, 0:n * 128].rearrange("p (m d) -> p m d", d=128),
                        bass.AP(pnb.tensor, pnb.offset + l * 16, [[pnb.ap[0][0], 128], [1, n], [0, 128]]),
                        bass.AP(G2.tensor, G2.offset + vc0, [[G2.ap[0][0], 128], [512, n], [1, 128]]),
                        op=ALU.mult),
                        r=vkeys + ["G3#4"], w=SMUL_K)
                    P.op("vector", lambda e, hl=hl, n=n: e.reduce_sum(
                        t.OB[:, hl * 128:(hl + 1) * 128],
                        bass.AP(SMUL.tensor, SMUL.offset, [[SMUL.ap[0][0], 128], [1, 128], [128, n]]),
                        axis=AX.X),
                        r=SMUL_K, w=t.BS_K)
                P.op("vector", lambda e: e.tensor_copy(t.OBB[:, :], t.OB[:, :]),
                     r=t.BS_K, w=t.BS_K)
                P.op("gpsimd", lambda e, ncx=ncx, l=l, hg=hg: e.dma_start(
                    out=y1_d[ncx, l * 128:(l + 1) * 128, hg * 512:(hg + 1) * 512],
                    in_=t.OBB[:, :]),
                    r=t.BS_K, w=[f"y1.{ncx}.{l}.{hg}"], dma=True)
                b0_tick(1)
        y1keys = [f"y1.{ncx}.{l}.{hg}" for l in range(16) for hg in range(4)]
        for ec in range(FC):
            P.op("scalar", lambda e, ec=ec, ncx=ncx: e.dma_start(
                out=G1[:, ec * 2048:(ec + 1) * 2048],
                in_=y1_d[ncx][:, ec * 128:(ec + 1) * 128], transpose=True),
                r=y1keys, w=[f"G1#{ec}"], dma=True)
        for cg in range(4):
            for ec in range(FC):
                c0 = 24576 + ec * 512
                P.op("scalar", lambda e, cg=cg, ec=ec, c0=c0: e.dma_start(
                    out=G2[:, c0:c0 + 512],
                    in_=wout_bf[cg * 512:(cg + 1) * 512, ec * 128:(ec + 1) * 128],
                    transpose=True),
                    r=[f"woutbf.{cg}"], w=keys2k("G2", c0, c0 + 512), dma=True)
            for l in range(16):
                grp, gk = t.rot()
                for ec in range(FC):
                    P.op("tensor", lambda e, grp=grp, ec=ec, l=l: e.matmul(
                        grp[:, :], G1[:, ec * 2048 + l * 128: ec * 2048 + (l + 1) * 128],
                        G2[:, 24576 + ec * 512: 24576 + (ec + 1) * 512],
                        start=(ec == 0), stop=(ec == FC - 1)),
                        r=[f"G1#{ec}"] + keys2k("G2", 24576 + ec * 512, 25088 + ec * 512),
                        w=[gk])
                P.op("scalar", lambda e, grp=grp: e.activation(t.S1T[:, :], grp[:, :], AF.Copy),
                     r=[gk], w=t.BS_K)
                P.op("gpsimd", lambda e, ncx=ncx, l=l, cg=cg: e.dma_start(
                    out=s1_d[ncx, l * 128:(l + 1) * 128, cg * 512:(cg + 1) * 512],
                    in_=t.S1T[:, :]),
                    r=t.BS_K, w=[f"s1.{ncx}.{l}.{cg}"], dma=True)

    b0_flush()
    for rg in range(4):
        l0 = rg * 4
        xnd_rg = [f"xnd.{ncx}.{l}" for ncx in range(2) for l in range(l0, l0 + 4)]
        for ec in range(FC):
            for ncx in range(2):
                c0 = 12288 + ec * 1024 + ncx * 512
                P.op("scalar", lambda e, ec=ec, ncx=ncx, l0=l0, c0=c0: e.dma_start(
                    out=G2[:, c0:c0 + 512],
                    in_=xn_d[ncx, l0 * 128:(l0 + 4) * 128, ec * 128:(ec + 1) * 128],
                    transpose=True),
                    r=xnd_rg, w=keys2k("G2", c0, c0 + 512), dma=True)
        for hc in range(HC):
            for (qtd, qk, c0) in ((q1t_d, q1t_all, 28672), (q2t_d, q2t_all, 30720)):
                P.op("sync", lambda e, qtd=qtd, hc=hc, c0=c0: e.dma_start(
                    out=G2[:, c0:c0 + 2048].rearrange("p (ec d) -> p ec d", d=128),
                    in_=qtd[:, hc * 128:(hc + 1) * 128].rearrange("(ec e) d -> e ec d", e=128)),
                    r=qk, w=keys2k("G2", c0, c0 + 2048), dma=True)
            pa = hc % 2
            gsets = ((t.ps[pa * 2], f"PS.{pa * 2}"), (t.ps[pa * 2 + 1], f"PS.{pa * 2 + 1}"))
            gsets2 = ((t.ps[4 + pa], f"PS.{4 + pa}"), (t.ps[6], "PS.6"))
            for (wc0, gs) in ((28672, gsets), (30720, gsets2)):
                for half, (gg, kk) in enumerate(gs):
                    for ec in range(FC):
                        P.op("tensor", lambda e, gg=gg, wc0=wc0, ec=ec, half=half: e.matmul(
                            gg[:, :], G2[:, wc0 + ec * 128: wc0 + (ec + 1) * 128],
                            G2[:, 12288 + ec * 1024 + half * 512: 12800 + ec * 1024 + half * 512],
                            start=(ec == 0), stop=(ec == FC - 1)),
                            r=keys2k("G2", wc0, wc0 + 2048) +
                              keys2k("G2", 12288 + ec * 1024, 12288 + (ec + 1) * 1024),
                            w=[kk])
            for half, (gg, kk) in enumerate(gsets):
                P.op("scalar", lambda e, gg=gg, half=half: e.activation(
                    t.SGM[:, half * 512:(half + 1) * 512], gg[:, :], AF.Sigmoid),
                    r=[kk], w=t.SGM_K)
            hcol = hc * 1024
            hidt = G1 if hcol + 1024 <= 32768 else G2
            hbase = hcol if hcol + 1024 <= 32768 else hcol - 32768
            hname = "G1" if hcol + 1024 <= 32768 else "G2"
            for half, (gg, kk) in enumerate(gsets2):
                P.op("vector", lambda e, hidt=hidt, hbase=hbase, half=half, gg=gg: e.tensor_tensor(
                    hidt[:, hbase + half * 512: hbase + (half + 1) * 512],
                    t.SGM[:, half * 512:(half + 1) * 512], gg[:, :], op=ALU.mult),
                    r=t.SGM_K + [kk], w=keys2k(hname, hbase + half * 512, hbase + (half + 1) * 512))
        for ec in range(FC):
            P.op("sync", lambda e, ec=ec: e.dma_start(
                out=G3[:, 0:FH].rearrange("p (hcc d) -> p hcc d", d=128),
                in_=q3t_d[:, ec * 128:(ec + 1) * 128].rearrange("(hcc h) d -> h hcc d", h=128)),
                r=q3t_all, w=keys2k("G3", 0, FH), dma=True)
            pa = ec % 2
            for half in range(2):
                gg = t.ps[pa * 2 + half]
                kk = f"PS.{pa * 2 + half}"
                for hc in range(HC):
                    hcol = hc * 1024
                    hidt = G1 if hcol + 1024 <= 32768 else G2
                    hbase = hcol if hcol + 1024 <= 32768 else hcol - 32768
                    hname = "G1" if hcol + 1024 <= 32768 else "G2"
                    P.op("tensor", lambda e, gg=gg, hc=hc, hidt=hidt, hbase=hbase, half=half:
                         e.matmul(gg[:, :], G3[:, hc * 128:(hc + 1) * 128],
                                  hidt[:, hbase + half * 512: hbase + half * 512 + 512],
                                  start=(hc == 0), stop=(hc == HC - 1)),
                         r=keys2k("G3", hc * 128, (hc + 1) * 128) +
                           keys2k(hname, hbase, hbase + 1024), w=[kk])
                P.op("scalar", lambda e, gg=gg, half=half: e.activation(
                    t.GFT[:, half * 512:(half + 1) * 512], gg[:, :], AF.Silu),
                    r=[kk], w=t.GFT_K)
            P.op("gpsimd", lambda e, ec=ec: e.dma_start(
                out=gfm_d[ec * 128:(ec + 1) * 128, :], in_=t.GFT[:, :]),
                r=t.GFT_K, w=[f"gfm.{ec}"], dma=True)
        gfm_all = [f"gfm.{ec}" for ec in range(FC)]
        for rt in range(8):
            ncx = rt // 4
            l = l0 + rt % 4
            pp = rt % 2
            xt = t.XT[pp]
            src_ap, src_keys = x1src(l, ncx)
            P.op("sync", lambda e, xt=xt, src_ap=src_ap: e.dma_start(out=xt[:, :], in_=src_ap),
                 r=src_keys, w=[f"xt{pp}"], dma=True)
            P.op("scalar", lambda e, rt=rt: e.dma_start(
                out=t.XNT[:, :], in_=gfm_d[:, rt * 128:(rt + 1) * 128], transpose=True),
                r=gfm_all, w=["XNT"], dma=True)
            P.op("sync", lambda e, ncx=ncx, l=l: e.dma_start(
                out=G3[:, 8192:8192 + 2048], in_=s1_d[ncx, l * 128:(l + 1) * 128, :]),
                r=[f"s1.{ncx}.{l}.{cg}" for cg in range(4)],
                w=keys2k("G3", 8192, 8192 + 2048), dma=True)
            for cg in range(4):
                P.op("vector", lambda e, cg=cg: e.tensor_tensor(
                    t.SCR[:, 0:512],
                    G3[:, 8192 + cg * 512:8192 + (cg + 1) * 512],
                    t.XNT[:, cg * 512:(cg + 1) * 512], op=ALU.mult),
                    r=keys2k("G3", 8192, 10240) + ["XNT"], w=["SCR"])
                P.op("vector", lambda e, xt=xt, cg=cg: e.tensor_tensor(
                    xt[:, cg * 512:(cg + 1) * 512], xt[:, cg * 512:(cg + 1) * 512],
                    t.SCR[:, 0:512], op=ALU.add),
                    r=[f"xt{pp}", "SCR"], w=[f"xt{pp}"])
            P.op("sync", lambda e, l=l, ncx=ncx, xt=xt: e.dma_start(
                out=out_h[l, ncx * 128:(ncx + 1) * 128, :], in_=xt[:, :]),
                r=[f"xt{pp}"], w=[f"out.{l}.{ncx}"], dma=True)


def a_scratch(nc):
    return dict(
        xn_d=nc.dram_tensor("xn_d", [T, C], BF16).ap(),
        bw_d=nc.dram_tensor("bw_d", [C, C], BF16).ap(),
        g_d=nc.dram_tensor("g_d", [T, C], BF16).ap(),
        v_d=nc.dram_tensor("v_d", [H, T, DH], BF16).ap(),
        y_d=nc.dram_tensor("y_d", [H, DH, T], BF16).ap(),
    )


def b_scratch(nc):
    return dict(
        wqkv_bf=nc.dram_tensor("wqkv_bf", [3 * C, C], BF16).ap(),
        wout_bf=nc.dram_tensor("wout_bf", [C, C], BF16).ap(),
        q1_d=nc.dram_tensor("q1_d", [FH, C], BF16).ap(),
        q2_d=nc.dram_tensor("q2_d", [FH, C], BF16).ap(),
        q3_d=nc.dram_tensor("q3_d", [C, FH], BF16).ap(),
        q1t_d=nc.dram_tensor("q1t_d", [C, FH], BF16).ap(),
        q2t_d=nc.dram_tensor("q2t_d", [C, FH], BF16).ap(),
        q3t_d=nc.dram_tensor("q3t_d", [FH, C], BF16).ap(),
        xn1_d=nc.dram_tensor("xn1_d", [2, 2048, C], BF16).ap(),
        y1_d=nc.dram_tensor("y1_d", [2, 2048, C], BF16).ap(),
        s1_d=nc.dram_tensor("s1_d", [2, 2048, C], BF16).ap(),
        gfm_d=nc.dram_tensor("gfm_d", [C, 1024], BF16).ap(),
    )


def a_inputs(nc):
    return dict(
        x=nc.dram_tensor("xa", [2, T, C], F32, kind="ExternalInput").ap(),
        wq=nc.dram_tensor("wq", [H, C, DH], F32, kind="ExternalInput").ap(),
        wk=nc.dram_tensor("wk", [H, C, DH], F32, kind="ExternalInput").ap(),
        wv=nc.dram_tensor("wv", [H, C, DH], F32, kind="ExternalInput").ap(),
        bq=nc.dram_tensor("bq", [H, DH], F32, kind="ExternalInput").ap(),
        bk=nc.dram_tensor("bk", [H, DH], F32, kind="ExternalInput").ap(),
        bv=nc.dram_tensor("bv", [H, DH], F32, kind="ExternalInput").ap(),
        wo=nc.dram_tensor("wo", [C, C], F32, kind="ExternalInput").ap(),
        bo=nc.dram_tensor("bo", [C], F32, kind="ExternalInput").ap(),
        wf=nc.dram_tensor("wf", [C, C], F32, kind="ExternalInput").ap(),
        bf=nc.dram_tensor("bf", [C], F32, kind="ExternalInput").ap(),
    )


def b_inputs(nc):
    return dict(
        wqkv=nc.dram_tensor("wqkv", [3 * C, C], F32, kind="ExternalInput").ap(),
        wout=nc.dram_tensor("wout", [C, C], F32, kind="ExternalInput").ap(),
        w1=nc.dram_tensor("w1", [FH, C], F32, kind="ExternalInput").ap(),
        w2=nc.dram_tensor("w2", [FH, C], F32, kind="ExternalInput").ap(),
        w3=nc.dram_tensor("w3", [C, FH], F32, kind="ExternalInput").ap(),
    )


def build_a_program():
    nc = bass.Bass("TRN2", target_bir_lowering=False)
    es = contextlib.ExitStack()
    t = alloc_all(nc, es)
    io = a_inputs(nc)
    io.update(a_scratch(nc))
    x1_h = nc.dram_tensor("x1out", [2, T, C], F32, kind="ExternalOutput").ap()
    io["x1dst"] = lambda b, i: (x1_h[b, i * 128:(i + 1) * 128, :], f"x1.{b}.{i}")
    P = Plan(nc)
    prologue_consts(P, t)
    build_stage_a(nc, P, t, io)
    with es:
        P.build(es)
    return nc


def build_b_program():
    nc = bass.Bass("TRN2", target_bir_lowering=False)
    es = contextlib.ExitStack()
    t = alloc_all(nc, es)
    io = b_inputs(nc)
    io.update(b_scratch(nc))
    x1_h = nc.dram_tensor("x1c", [B, NB, C], F32, kind="ExternalInput").ap()
    io["x1src"] = lambda l, ncx: (x1_h[l, ncx * 128:(ncx + 1) * 128, :], [])
    io["out"] = nc.dram_tensor("outc", [B, NB, C], F32, kind="ExternalOutput").ap()
    P = Plan(nc)
    prologue_consts(P, t)
    build_stage_b(nc, P, t, io)
    with es:
        P.build(es)
    return nc


def build_fused_program():
    nc = bass.Bass("TRN2", target_bir_lowering=False)
    es = contextlib.ExitStack()
    t = alloc_all(nc, es)
    io = a_inputs(nc)
    io.update(b_inputs(nc))
    io.update(a_scratch(nc))
    io.update(b_scratch(nc))
    x1s = [nc.dram_tensor(f"x1s{b}", [T, C], F32).ap() for b in range(2)]
    xall = [nc.dram_tensor(f"xall{b}", [NCORES * NB, C], F32).ap() for b in range(2)]
    io["x1dst"] = lambda b, i: (x1s[b][i * 128:(i + 1) * 128, :], f"x1.{b}.{i}")
    io["x1src"] = lambda l, ncx: (
        xall[l % 2][(l // 2) * NB + ncx * 128: (l // 2) * NB + (ncx + 1) * 128, :],
        [f"xall.{l % 2}"])
    io["out"] = nc.dram_tensor("outc", [B, NB, C], F32, kind="ExternalOutput").ap()

    P = Plan(nc)
    prologue_consts(P, t)
    build_stage_a(nc, P, t, io)
    for b in range(2):
        def cc(e, b=b):
            with allow_ici_all_to_all():
                return e.collective_compute(
                    "AllToAll", ALU.bypass,
                    replica_groups=[list(range(NCORES))],
                    ins=[x1s[b].opt()],
                    outs=[xall[b].opt()],
                )
        P.op("gpsimd", cc, r=[f"x1.{b}.{i}" for i in range(16)], w=[f"xall.{b}"], cc=True)
    P.barrier()
    build_stage_b(nc, P, t, io)
    with es:
        P.build(es)
    return nc


def kernel(L0_attn_x, L0_attn_Wq, L0_attn_bq, L0_attn_Wk, L0_attn_bk, L0_attn_Wv,
           L0_attn_bv, L0_attn_Wo, L0_attn_bo, L0_ffn_weight, L0_ffn_bias,
           L1_attn_weight_qkv, L1_attn_weight_out, L1_ffn_w1, L1_ffn_w2, L1_ffn_w3):
    f = lambda a: np.ascontiguousarray(np.asarray(a, np.float32))
    x = f(L0_attn_x)
    wA = {"wq": f(L0_attn_Wq), "wk": f(L0_attn_Wk), "wv": f(L0_attn_Wv),
          "bq": f(L0_attn_bq), "bk": f(L0_attn_bk), "bv": f(L0_attn_bv),
          "wo": f(L0_attn_Wo), "bo": f(L0_attn_bo),
          "wf": f(L0_ffn_weight), "bf": f(L0_ffn_bias)}
    wB = {"wqkv": f(L1_attn_weight_qkv), "wout": f(L1_attn_weight_out),
          "w1": f(L1_ffn_w1), "w2": f(L1_ffn_w2), "w3": f(L1_ffn_w3)}
    if FUSED:
        ncF = build_fused_program()
        in_maps = [dict(wA, **wB, xa=np.ascontiguousarray(x[2 * c:2 * c + 2]))
                   for c in range(NCORES)]
        res = run_bass_kernel_spmd(ncF, in_maps, core_ids=list(range(NCORES)))
        return np.concatenate([res.results[c]["outc"] for c in range(NCORES)], axis=1)

    ncA = build_a_program()
    in_maps = [dict(wA, xa=np.ascontiguousarray(x[2 * c:2 * c + 2])) for c in range(NCORES)]
    resA = run_bass_kernel_spmd(ncA, in_maps, core_ids=list(range(NCORES)))
    x1 = np.concatenate([resA.results[c]["x1out"] for c in range(NCORES)], axis=0)
    ncB = build_b_program()
    in_maps_b = [dict(wB, x1c=np.ascontiguousarray(x1[:, c * NB:(c + 1) * NB, :]))
                 for c in range(NCORES)]
    resB = run_bass_kernel_spmd(ncB, in_maps_b, core_ids=list(range(NCORES)))
    return np.concatenate([resB.results[c]["outc"] for c in range(NCORES)], axis=1)
